# version 01
# baseline (speedup 1.0000x reference)
import math
import numpy as np
import concourse.bass as bass
import concourse.mybir as mybir
from concourse.bass_utils import run_bass_kernel_spmd
from concourse.ap import AP as mkAP

F32 = mybir.dt.float32
AF = mybir.ActivationFunctionType
ALU = mybir.AluOpType
AX = mybir.AxisListType

T = 2048
D = 1024
H = 16
HD = 64
E = 1228
LRK = 128
TB = 512
NBLK = T // TB
CH = D // 128
DECAY_C = 0.7408182206817179

ROLES = ("sync", "tensor", "scalar", "vector", "gpsimd")


class EM:

    def __init__(self, nc, role, eng, sems):
        self.nc = nc
        self.role = role
        self.eng = eng
        self.S = sems
        self.c = {k: 0 for k in sems}
        self.bi = {}

    def bump(self, sem, n=1):
        self.c[sem] += n
        return self.c[sem]

    def inc(self, instr, sem, n=1):
        if instr is not None:
            instr.then_inc(self.S[sem], n)
        return self.bump(sem, n)

    def wait(self, role, sem, val):
        if self.role == role:
            self.eng.wait_ge(self.S[sem], val)

    def wait_cur(self, role, sem):
        self.wait(role, sem, self.c[sem])

    def ldn(self, base, i):
        return base + ("", "b", "c", "d")[i % 4]

    def batch(self, base):
        self.bi.setdefault(base, -1)
        self.bi[base] += 1
        return self.ldn(base, self.bi[base])

    def flush(self, role, bases):
        for b in bases:
            for suf in ("", "b", "c", "d"):
                nm = b + suf
                if nm in self.S and self.c[nm] > 0:
                    self.wait(role, nm, self.c[nm])

    def sy(self):
        return self.eng if self.role == "sync" else _Null
    def te(self):
        return self.eng if self.role == "tensor" else _Null
    def sc(self):
        return self.eng if self.role == "scalar" else _Null
    def ve(self):
        return self.eng if self.role == "vector" else _Null
    def gp(self):
        return self.eng if self.role == "gpsimd" else _Null

    def barrier(self, tag):
        self.flush("sync", [f"ld{i}" for i in range(10)])
        self.flush("gpsimd", [f"st{i}" for i in range(4)])
        self.eng.sem_inc(self.S["ph"], 1)
        val = self.bump("ph", 5)
        self.eng.wait_ge(self.S["ph"], val)


class _NullEngine:
    def __getattr__(self, name):
        def f(*a, **k):
            return None
        return f


_Null = _NullEngine()


def host_constants():
    C = {}
    C["ident"] = np.eye(128, dtype=np.float32)
    r = np.arange(128)
    C["trilS"] = (r[None, :] < r[:, None]).astype(np.float32)
    C["triuS"] = (r[:, None] < r[None, :]).astype(np.float32)
    C["triuI"] = (r[:, None] <= r[None, :]).astype(np.float32)
    C["ntriuI"] = -C["triuI"]
    bd = np.zeros((128, 128), dtype=np.float32)
    bd[:64, :64] = 1.0
    bd[64:, 64:] = 1.0
    C["bdiag64"] = bd
    P = np.zeros((128, 128), dtype=np.float32)
    for blk in range(2):
        b = 64 * blk
        for i in range(32):
            P[b + 32 + i, b + i] = -1.0
            P[b + i, b + 32 + i] = 1.0
    C["rotperm"] = P
    half = np.arange(0, HD, 2, dtype=np.float32)
    inv_freq = np.float32(1.0) / np.float32(10000.0) ** (half / np.float32(HD))
    t = np.arange(T, dtype=np.float32)
    freqs = t[:, None] * inv_freq[None, :]
    freqs = np.concatenate([freqs, freqs], -1)
    base_scale = (half + np.float32(0.4) * HD) / np.float32(1.4 * HD)
    power = (t - T // 2) / np.float32(128)
    scale = base_scale[None, :] ** power[:, None]
    scale = np.concatenate([scale, scale], -1)
    cos = np.cos(freqs); sin = np.sin(freqs)
    inv = np.float32(1.0) / scale
    def fm(a):
        a = np.ascontiguousarray(a.T.astype(np.float32))
        return np.concatenate([a, a], 0)
    C["csq"] = fm(cos * scale)
    C["ssq"] = fm(sin * scale)
    C["csk"] = fm(cos * inv)
    C["ssk"] = fm(sin * inv)
    NUM_BUCKETS, MAX_DIST = 32, 128
    nb = NUM_BUCKETS // 2
    d = np.arange(-2047, 2048)
    rel_buckets = (d > 0).astype(np.int64) * nb
    ad = np.abs(d)
    max_exact = nb // 2
    is_small = ad < max_exact
    rp_large = max_exact + (
        np.log(np.maximum(ad, 1).astype(np.float32) / max_exact)
        / math.log(MAX_DIST / max_exact) * (nb - max_exact)
    ).astype(np.int32)
    rp_large = np.minimum(rp_large, nb - 1)
    bucket = rel_buckets + np.where(is_small, ad, rp_large)
    oh = np.zeros((4095, 32), dtype=np.float32)
    oh[np.arange(4095), bucket] = 1.0
    ohp = np.zeros((4096, 32), dtype=np.float32)
    ohp[:4095] = oh
    C["t5oneh"] = np.ascontiguousarray(ohp.T)
    masks = []
    for off in (-256, -128, 0):
        s_idx = off + np.arange(128)[:, None]
        t_idx = np.arange(128)[None, :]
        dd = t_idx - s_idx
        masks.append(((dd >= 0) & (dd <= 256)).astype(np.float32))
    C["lmask"] = np.concatenate(masks, axis=1)
    C["zeros64"] = np.zeros((128, 128), dtype=np.float32)
    rp = np.zeros((16, 8 * 128), dtype=np.float32)
    for cc in range(8):
        for mm_ in range(128):
            rp[2 * cc + mm_ // 64, cc * 128 + mm_] = 1.0
    C["repl16"] = rp
    r2 = np.arange(128)
    C["trilI"] = (r2[None, :] <= r2[:, None]).astype(np.float32)
    return C


class Builder:
    def __init__(self, nc, dbg_taps=()):
        self.nc = nc
        self.dbg_taps = set(dbg_taps)
        self.inputs = {}
        self.scratch = {}
        self.outputs = {}
        self.sb = {}
        self.psum = []

    def inp(self, name, shape):
        h = self.nc.dram_tensor(name, list(shape), F32, kind="ExternalInput")
        self.inputs[name] = h
        return h

    def scr(self, name, shape):
        kind = "ExternalOutput" if name in self.dbg_taps else "Internal"
        h = self.nc.dram_tensor(name, list(shape), F32, kind=kind)
        self.scratch[name] = h
        if kind == "ExternalOutput":
            self.outputs[name] = h
        return h

    def out(self, name, shape):
        h = self.nc.dram_tensor(name, list(shape), F32, kind="ExternalOutput")
        self.outputs[name] = h
        return h

    def tap(self, name, src_hbm):
        return name in self.dbg_taps


def declare_io(B, with_weights=True):
    nc = B.nc
    B.inp("xin", (D, T))
    B.out("xout", (D, T))
    if not with_weights:
        return
    for p in ("r", "k", "v", "o"):
        B.inp(f"L0_{p}W1", (D, E))
        B.inp(f"L0_{p}b1", (E,))
        B.inp(f"L0_{p}W2", (E, D))
        B.inp(f"L0_{p}b2", (D,))
    for p in ("mu_r", "mu_k", "mu_v", "mu_g", "mu_a", "mu_d",
              "db", "ab", "rem_mult", "iclr_mix", "bonus_mult",
              "ln_g", "ln_b"):
        B.inp(f"L0_{p}", (D,))
    for p in ("dA", "aA", "gA"):
        B.inp(f"L0_{p}", (D, LRK))
    for p in ("dB", "aB", "gB"):
        B.inp(f"L0_{p}", (LRK, D))
    B.inp("F0_Win", (D, 64))
    B.inp("F0_bin", (64,))
    B.inp("F0_W1", (64, 16))
    B.inp("F0_b1", (64, 16))
    B.inp("F0_W2", (64, 16))
    B.inp("F0_b2", (64,))
    B.inp("F0_Wout", (64, D))
    B.inp("F0_bout", (D,))
    for p in ("Wq", "Wk", "Wv", "Wo"):
        B.inp(f"A1_{p}", (D, D))
    B.inp("A1_table", (32, H))
    B.inp("FF_Wqkv", (6, D, 3 * D))
    B.inp("FF_qs", (6, HD))
    B.inp("FF_ks", (6, HD))
    B.inp("FF_Wo", (6, D, D))
    B.inp("FF_lng", (6, D))
    B.inp("FF_lnb", (6, D))
    B.inp("FF_W1", (6, D, 5460))
    B.inp("FF_b1", (6, 5460))
    B.inp("FF_W2", (6, 2730, D))
    B.inp("FF_b2", (6, D))
    B.inp("c_ident", (128, 128))
    B.inp("c_trilS", (128, 128))
    B.inp("c_triuS", (128, 128))
    B.inp("c_triuI", (128, 128))
    B.inp("c_ntriuI", (128, 128))
    B.inp("c_bdiag64", (128, 128))
    B.inp("c_rotperm", (128, 128))
    B.inp("c_csq", (128, T))
    B.inp("c_ssq", (128, T))
    B.inp("c_csk", (128, T))
    B.inp("c_ssk", (128, T))
    B.inp("c_t5oneh", (32, 4096))
    B.inp("c_lmask", (128, 384))
    B.inp("c_repl16", (16, 8 * 128))
    B.inp("c_trilI", (128, 128))


def alloc_sbuf(B):
    nc = B.nc
    B.AR1 = nc.alloc_sbuf_tensor("AR1", [128, 13000], F32)
    B.AR2 = nc.alloc_sbuf_tensor("AR2", [128, 13000], F32)
    B.AR3 = nc.alloc_sbuf_tensor("AR3", [128, 16400], F32)
    B.CON = nc.alloc_sbuf_tensor("CON", [128, 10800], F32)
    B.PS = [nc.alloc_psum_tensor(f"PS{i}", [128, 512], F32) for i in range(8)]


CON_OFF = {
    "ident": 0, "trilS": 128, "triuS": 256, "triuI": 384, "ntriuI": 512,
    "bdiag64": 640, "rotperm": 768, "lmask": 896,
    "csq": 1280, "ssq": 3328, "csk": 5376, "ssk": 7424,
    "sca": 9472,
    "ones": 9488,
    "repl16": 9616,
    "trilI": 10640,
}

SCA = {"eps_ln": 0, "eps_attln": 1, "negB": 2, "eps12": 3, "one": 4}
SCA_VAL = {"eps_ln": 1e-5, "eps_attln": 1e-6, "negB": -8.0, "eps12": 1e-12,
           "one": 1.0}


def sca(B, name):
    off = CON_OFF["sca"] + SCA[name]
    return B.CON[:, off:off + 1]


def con(B, name, rows=slice(0, 128), width=None):
    off = CON_OFF[name]
    if width is None:
        width = {"csq": T, "ssq": T, "csk": T, "ssk": T, "lmask": 384}.get(name, 128)
    return B.CON[rows, off:off + width]


def load_constants(em, B):
    names = ["ident", "trilS", "triuS", "triuI", "ntriuI", "bdiag64", "rotperm",
             "trilI"]
    for n in names:
        i = em.sy().dma_start(out=con(B, n), in_=B.inputs["c_" + n].ap())
        em.inc(i, "ld0", 16)
    i = em.sy().dma_start(out=con(B, "lmask"), in_=B.inputs["c_lmask"].ap())
    em.inc(i, "ld0", 16)
    i = em.sy().dma_start(out=B.CON[0:16, CON_OFF["repl16"]:CON_OFF["repl16"] + 1024],
                          in_=B.inputs["c_repl16"].ap())
    em.inc(i, "ld0", 16)
    for n in ("csq", "ssq", "csk", "ssk"):
        i = em.sy().dma_start(out=con(B, n), in_=B.inputs["c_" + n].ap())
        em.inc(i, "ld0", 16)
    for nm, val in SCA_VAL.items():
        i = em.ve().memset(sca(B, nm), float(val))
        em.inc(i, "vec")
    i = em.ve().memset(B.CON[:, CON_OFF["ones"]:CON_OFF["ones"] + 128], 1.0)
    em.inc(i, "vec")
    em.ve().drain(fusable=False)
    em.wait("sync", "ld0", em.c["ld0"])
    em.barrier("consts")


def phase_layernorm(em, B, src, dst, eps_ap=None, ldsem="ld1", tag="",
                    shift=False, lng=None, lnb=None):
    nc = B.nc
    onesT = B.CON[:, CON_OFF["ones"]:CON_OFF["ones"] + 128]
    aff_sb = B.AR3[:, 16000:16016]
    if lng is not None:
        gh, go = lng
        bh, bo = lnb
        ls0 = em.batch(ldsem)
        for c in range(CH):
            i = em.sy().dma_start(out=aff_sb[:, c:c + 1],
                                  in_=mkAP(gh, go + c * 128, [[1, 128], [1, 1]]))
            em.inc(i, ls0, 16)
            i = em.sy().dma_start(out=aff_sb[:, 8 + c: 9 + c],
                                  in_=mkAP(bh, bo + c * 128, [[1, 128], [1, 1]]))
            em.inc(i, ls0, 16)
        em.wait("vector", ls0, em.c[ls0])
    x_t = lambda c: B.AR3[:, c * TB: (c + 1) * TB]
    x2_t = B.AR3[:, 4096: 4096 + TB]
    xn_t = lambda c: B.AR3[:, 4608 + c * TB: 4608 + (c + 1) * TB]
    mu_t = B.AR3[:, 8704: 8704 + TB]
    iv_t = B.AR3[:, 9216: 9216 + TB]

    for b in range(NBLK):
        t0 = b * TB
        ls = em.batch(ldsem)
        for c in range(CH):
            if shift and c >= 4:
                if b == 0:
                    i = em.ve().memset(x_t(c)[:, 0:1], 0.0)
                    em.inc(i, "vec")
                    em.ve().drain(fusable=False)
                    i = em.sy().dma_start(out=x_t(c)[:, 1:TB],
                                          in_=src[c * 128:(c + 1) * 128, 0:TB - 1])
                else:
                    i = em.sy().dma_start(out=x_t(c),
                                          in_=src[c * 128:(c + 1) * 128, t0 - 1:t0 + TB - 1])
            else:
                i = em.sy().dma_start(out=x_t(c), in_=src[c * 128:(c + 1) * 128, t0:t0 + TB])
            em.inc(i, ls, 16)
        ldv = em.c[ls]
        em.wait("tensor", ls, ldv)
        em.wait("vector", ls, ldv)
        em.wait("scalar", ls, ldv)
        em.wait("tensor", "vec", em.c["vec"])
        em.wait("scalar", "vec", em.c["vec"])
        for c in range(CH):
            i = em.sc().activation(x2_t, x_t(c), AF.Square)
            sq = em.inc(i, "act")
            em.wait("tensor", "act", sq)
            i = em.te().matmul(B.PS[0][:, 0:TB], onesT, x_t(c),
                               start=(c == 0), stop=(c == CH - 1))
            em.inc(i, "mm")
            i = em.te().matmul(B.PS[1][:, 0:TB], onesT, x2_t,
                               start=(c == 0), stop=(c == CH - 1))
            mmv = em.inc(i, "mm")
            em.wait("scalar", "mm", mmv)
        mmv = em.c["mm"]
        em.wait("vector", "mm", mmv)
        i = em.ve().tensor_scalar(mu_t, B.PS[0][:, 0:TB], 1.0 / D, None, ALU.mult)
        em.inc(i, "vec")
        em.ve().drain(fusable=False)
        i = em.ve().tensor_tensor(iv_t, mu_t, mu_t, ALU.mult)
        em.inc(i, "vec")
        em.ve().drain(fusable=False)
        i = em.ve().scalar_tensor_tensor(iv_t, B.PS[1][:, 0:TB], 1.0 / D, iv_t,
                                         ALU.mult, ALU.subtract)
        vv = em.inc(i, "vec")
        em.wait("scalar", "vec", vv)
        i = em.sc().activation(iv_t, iv_t, AF.Sqrt, bias=eps_ap)
        sv = em.inc(i, "act")
        em.wait("vector", "act", sv)
        i = em.ve().reciprocal(iv_t, iv_t)
        em.inc(i, "vec")
        em.ve().drain(fusable=False)
        for c in range(CH):
            i = em.ve().tensor_sub(xn_t(c), x_t(c), mu_t)
            em.inc(i, "vec")
            em.ve().drain(fusable=False)
            i = em.ve().tensor_mul(xn_t(c), xn_t(c), iv_t)
            vv = em.inc(i, "vec")
            em.ve().drain(fusable=False)
            if lng is not None:
                i = em.ve().tensor_scalar(xn_t(c), xn_t(c), aff_sb[:, c:c + 1],
                                          aff_sb[:, 8 + c: 9 + c], ALU.mult, ALU.add)
                vv = em.inc(i, "vec")
                em.ve().drain(fusable=False)
            em.wait("gpsimd", "vec", vv)
            i = em.gp().dma_start(out=dst[c * 128:(c + 1) * 128, t0:t0 + TB],
                                  in_=xn_t(c))
            sv = em.inc(i, "st0", 16)
            em.wait("vector", "st0", sv)
        em.wait("sync", "vec", em.c["vec"])
        em.wait("sync", "mm", em.c["mm"])
    em.wait("gpsimd", "st0", em.c["st0"])
    em.barrier("ln" + tag)


class Banks:

    def __init__(self, em, idxs):
        self.em = em
        self.pend = {i: None for i in idxs}

    def use(self, i):
        if self.pend[i] is not None:
            sem, val = self.pend[i]
            self.em.wait("tensor", sem, val)
        return i

    def done(self, i, sem, val):
        self.pend[i] = (sem, val)



def col(h, c, rows=128):
    n = h.shape[0]
    return h.reshape([n, 1])[c * rows: c * rows + rows, :]


def colw(h, start, rows):
    n = h.shape[0]
    return h.reshape([n, 1])[start: start + rows, :]

def ts(i, n=128):
    return slice(i * n, (i + 1) * n)


def phase_l0_mlp(em, B, src, dst, wpfx, mu_name, ldsem, tag, eouter=E, resid=None):
    nc = B.nc
    W1 = B.inputs[wpfx + "W1"]
    b1 = B.inputs[wpfx + "b1"]
    W2 = B.inputs[wpfx + "W2"]
    b2 = B.inputs[wpfx + "b2"]
    mu = B.inputs[mu_name] if mu_name else None
    MC = (eouter + 127) // 128
    mws = [min(128, eouter - mc * 128) for mc in range(MC)]

    w1t = lambda c: B.AR1[:, c * eouter: (c + 1) * eouter]
    w2t = lambda mc: B.AR2[:, mc * D: (mc + 1) * D]
    mu_sb = B.AR2[:, 12800:12808]
    b2_sb = B.AR2[:, 12808:12816]
    mb_sb = B.AR2[:, 12816:12816 + MC]
    x_t = lambda c: B.AR3[:, c * TB: (c + 1) * TB]
    h_t = lambda mc: B.AR3[:, 4608 + mc * TB: 4608 + (mc + 1) * TB]
    o_t = B.AR3[:, 4608 + MC * TB: 4608 + MC * TB + TB]
    res_t = lambda c: B.AR3[:, 10240 + c * TB: 10240 + (c + 1) * TB]

    ls0 = em.batch(ldsem)
    for c in range(CH):
        i = em.sy().dma_start(out=w1t(c), in_=W1[ts(c), :])
        em.inc(i, ls0, 16)
    for mc in range(MC):
        i = em.sy().dma_start(out=w2t(mc)[0:mws[mc], :], in_=W2[mc * 128: mc * 128 + mws[mc], :])
        em.inc(i, ls0, 16)
    for c in range(CH):
        if mu is not None:
            i = em.sy().dma_start(out=mu_sb[:, c:c + 1], in_=col(mu, c))
            em.inc(i, ls0, 16)
        i = em.sy().dma_start(out=b2_sb[:, c:c + 1], in_=col(b2, c))
        em.inc(i, ls0, 16)
    for mc in range(MC):
        i = em.sy().dma_start(out=mb_sb[0:mws[mc], mc:mc + 1],
                              in_=colw(b1, mc * 128, mws[mc]))
        em.inc(i, ls0, 16)
    ldw = em.c[ls0]
    em.wait("tensor", ls0, ldw)
    em.wait("vector", ls0, ldw)
    banks = Banks(em, range(8))

    for mc in range(MC if mu is not None else 0):
        bk = banks.use(mc % 4)
        for c in range(CH):
            i = em.te().matmul(B.PS[bk][0:mws[mc], 0:1], w1t(c)[:, mc * 128: mc * 128 + mws[mc]],
                               mu_sb[:, c:c + 1], start=(c == 0), stop=(c == CH - 1))
            mmv = em.inc(i, "mm")
        em.wait("vector", "mm", mmv)
        i = em.ve().tensor_add(mb_sb[0:mws[mc], mc:mc + 1], mb_sb[0:mws[mc], mc:mc + 1],
                               B.PS[bk][0:mws[mc], 0:1])
        vv = em.inc(i, "vec")
        em.ve().drain(fusable=False)
        banks.done(bk, "vec", vv)
    em.wait("scalar", "vec", em.c["vec"])

    for b in range(NBLK):
        t0 = b * TB
        ls = em.batch(ldsem)
        for c in range(CH):
            i = em.sy().dma_start(out=x_t(c), in_=src[ts(c), t0:t0 + TB])
            em.inc(i, ls, 16)
        if resid is not None:
            for c in range(CH):
                i = em.sy().dma_start(out=res_t(c), in_=resid[ts(c), t0:t0 + TB])
                em.inc(i, ls, 16)
        ldv = em.c[ls]
        em.wait("tensor", ls, ldv)
        if resid is not None:
            em.wait("vector", ls, ldv)
        for mc in range(MC):
            bk = banks.use(mc % 4)
            for c in range(CH):
                i = em.te().matmul(B.PS[bk][0:mws[mc], 0:TB], w1t(c)[:, mc * 128: mc * 128 + mws[mc]],
                                   x_t(c), start=(c == 0), stop=(c == CH - 1))
                mmv = em.inc(i, "mm")
            em.wait("scalar", "mm", mmv)
            i = em.sc().activation(h_t(mc)[0:mws[mc], :], B.PS[bk][0:mws[mc], 0:TB],
                                   AF.Gelu, bias=mb_sb[0:mws[mc], mc:mc + 1])
            av = em.inc(i, "act")
            banks.done(bk, "act", av)
        em.wait("tensor", "act", em.c["act"])
        for oc in range(CH):
            bk = banks.use(4 + oc % 4)
            for mc in range(MC):
                i = em.te().matmul(B.PS[bk][:, 0:TB], w2t(mc)[0:mws[mc], ts(oc)],
                                   h_t(mc)[0:mws[mc], :], start=(mc == 0), stop=(mc == MC - 1))
                mmv = em.inc(i, "mm")
            em.wait("vector", "mm", mmv)
            if resid is not None:
                i = em.ve().scalar_tensor_tensor(o_t, B.PS[bk][:, 0:TB],
                                                 b2_sb[:, oc:oc + 1], res_t(oc),
                                                 ALU.add, ALU.add)
            else:
                i = em.ve().tensor_scalar(o_t, B.PS[bk][:, 0:TB], b2_sb[:, oc:oc + 1],
                                          None, ALU.add)
            vv = em.inc(i, "vec")
            em.ve().drain(fusable=False)
            banks.done(bk, "vec", vv)
            em.wait("gpsimd", "vec", vv)
            i = em.gp().dma_start(out=dst[ts(oc), t0:t0 + TB], in_=o_t)
            sv = em.inc(i, "st0", 16)
            em.wait("vector", "st0", sv)
        em.wait("sync", "mm", em.c["mm"])
        em.wait("sync", "vec", em.c["vec"])
    em.barrier("mlp" + tag)


def phase_l0_gates(em, B, src, ldsem="ld3"):
    nc = B.nc
    outs = {"g": B.scratch["GATE"].ap(), "a": B.scratch["ICLR"].ap(),
            "d": B.scratch["SIGT"].ap()}
    A_in = {"g": "L0_gA", "a": "L0_aA", "d": "L0_dA"}
    Bw_in = {"g": "L0_gB", "a": "L0_aB", "d": "L0_dB"}
    mu_in = {"g": "L0_mu_g", "a": "L0_mu_a", "d": "L0_mu_d"}
    b_in = {"g": None, "a": "L0_ab", "d": "L0_db"}

    aT = lambda c: B.AR1[:, c * 128: (c + 1) * 128]
    bT = B.AR1[:, 1024: 1024 + D]
    mu_sb = B.AR2[:, 12800:12808]
    ob_sb = B.AR2[:, 12808:12816]
    mb_sb = B.AR2[:, 12816:12817]
    x_t = lambda c: B.AR3[:, c * TB: (c + 1) * TB]
    inner_t = B.AR3[:, 4608: 4608 + TB]
    o_t = B.AR3[:, 5120: 5120 + TB]
    banks = Banks(em, range(8))

    for pi, p in enumerate(("g", "a", "d")):
        Aw = B.inputs[A_in[p]]
        Bw = B.inputs[Bw_in[p]]
        mu = B.inputs[mu_in[p]]
        ls0 = em.batch(ldsem)
        for c in range(CH):
            i = em.sy().dma_start(out=aT(c), in_=Aw[ts(c), :])
            em.inc(i, ls0, 16)
            i = em.sy().dma_start(out=mu_sb[:, c:c + 1], in_=col(mu, c))
            em.inc(i, ls0, 16)
        i = em.sy().dma_start(out=bT, in_=Bw.ap())
        em.inc(i, ls0, 16)
        if b_in[p]:
            ob = B.inputs[b_in[p]]
            for c in range(CH):
                i = em.sy().dma_start(out=ob_sb[:, c:c + 1], in_=col(ob, c))
                em.inc(i, ls0, 16)
        ldw = em.c[ls0]
        em.wait("tensor", ls0, ldw)
        bk = banks.use(6)
        for c in range(CH):
            i = em.te().matmul(B.PS[bk][:, 0:1], aT(c), mu_sb[:, c:c + 1],
                               start=(c == 0), stop=(c == CH - 1))
            mmv = em.inc(i, "mm")
        em.wait("vector", "mm", mmv)
        i = em.ve().tensor_copy(mb_sb, B.PS[bk][:, 0:1])
        vv = em.inc(i, "vec")
        em.ve().drain(fusable=False)
        banks.done(bk, "vec", vv)
        em.wait("scalar", "vec", vv)
        for b in range(NBLK):
            t0 = b * TB
            ls = em.batch(ldsem)
            for c in range(CH):
                i = em.sy().dma_start(out=x_t(c), in_=src[ts(c), t0:t0 + TB])
                em.inc(i, ls, 16)
            ldv = em.c[ls]
            em.wait("tensor", ls, ldv)
            bk = banks.use(0)
            for c in range(CH):
                i = em.te().matmul(B.PS[bk][:, 0:TB], aT(c), x_t(c),
                                   start=(c == 0), stop=(c == CH - 1))
                mmv = em.inc(i, "mm")
            em.wait("scalar", "mm", mmv)
            i = em.sc().activation(inner_t, B.PS[bk][:, 0:TB], AF.Identity, bias=mb_sb)
            av = em.inc(i, "act")
            banks.done(bk, "act", av)
            em.wait("tensor", "act", av)
            for oc in range(CH):
                bk = banks.use(1 + oc % 4)
                i = em.te().matmul(B.PS[bk][:, 0:TB], bT[:, ts(oc)], inner_t,
                                   start=True, stop=True)
                mmv = em.inc(i, "mm")
                em.wait("scalar", "mm", mmv)
                bias = ob_sb[:, oc:oc + 1] if b_in[p] else 0.0
                if p == "d":
                    i = em.sc().activation(o_t, B.PS[bk][:, 0:TB], AF.Tanh, bias=bias)
                    av = em.inc(i, "act")
                    em.sc().drain(fusable=False)
                    i = em.sc().activation(o_t, o_t, AF.Sigmoid)
                    av = em.inc(i, "act")
                else:
                    i = em.sc().activation(o_t, B.PS[bk][:, 0:TB], AF.Sigmoid, bias=bias)
                    av = em.inc(i, "act")
                banks.done(bk, "act", av)
                em.wait("gpsimd", "act", av)
                i = em.gp().dma_start(out=outs[p][ts(oc), t0:t0 + TB], in_=o_t)
                sv = em.inc(i, "st0", 16)
                em.wait("scalar", "st0", sv)
            em.wait("sync", "mm", em.c["mm"])
            em.wait("sync", "act", em.c["act"])
    em.wait("gpsimd", "st0", em.c["st0"])
    em.barrier("gates")


def phase_l0_prep(em, B, ldsem="ld4"):
    nc = B.nc
    Ksrc = B.scratch["KK"].ap()
    Isrc = B.scratch["ICLR"].ap()
    KN = B.scratch["KN"].ap()
    BV = B.scratch["BVEC"].ap()
    RK = B.scratch["RK"].ap()
    rem = B.inputs["L0_rem_mult"]
    mix = B.inputs["L0_iclr_mix"]

    rem_sb = B.AR2[:, 12800:12808]
    mix_sb = B.AR2[:, 12808:12816]
    om_sb = B.AR2[:, 12816:12824]
    k_t = lambda c: B.AR3[:, c * TB: (c + 1) * TB]
    i_t = lambda c: B.AR3[:, 4096 + c * TB: 4096 + (c + 1) * TB]
    rm_t = B.AR3[:, 8192: 8192 + TB]
    s_t = B.AR3[:, 8704: 8704 + TB]
    o1_t = B.AR3[:, 9216: 9216 + TB]
    o2_t = B.AR3[:, 9728: 9728 + TB]
    o3_t = B.AR3[:, 10240: 10240 + TB]
    bdiag = con(B, "bdiag64")

    ls0 = em.batch(ldsem)
    for c in range(CH):
        i = em.sy().dma_start(out=rem_sb[:, c:c + 1], in_=col(rem, c))
        em.inc(i, ls0, 16)
        i = em.sy().dma_start(out=mix_sb[:, c:c + 1], in_=col(mix, c))
        em.inc(i, ls0, 16)
    ldw = em.c[ls0]
    em.wait("vector", ls0, ldw)
    i = em.ve().tensor_scalar(om_sb, mix_sb, -1.0, 1.0, ALU.mult, ALU.add)
    vv = em.inc(i, "vec")
    em.ve().drain(fusable=False)
    banks = Banks(em, range(8))

    for b in range(NBLK):
        t0 = b * TB
        ls = em.batch(ldsem)
        for c in range(CH):
            i = em.sy().dma_start(out=k_t(c), in_=Ksrc[ts(c), t0:t0 + TB])
            em.inc(i, ls, 16)
            i = em.sy().dma_start(out=i_t(c), in_=Isrc[ts(c), t0:t0 + TB])
            em.inc(i, ls, 16)
        ldv = em.c[ls]
        em.wait("vector", ls, ldv)
        em.wait("scalar", ls, ldv)
        for c in range(CH):
            i = em.ve().tensor_scalar(rm_t, k_t(c), rem_sb[:, c:c + 1], None, ALU.mult)
            vv = em.inc(i, "vec")
            em.ve().drain(fusable=False)
            em.wait("scalar", "vec", vv)
            i = em.sc().activation(s_t, rm_t, AF.Square)
            av = em.inc(i, "act")
            em.wait("tensor", "act", av)
            bk = banks.use(c % 4)
            i = em.te().matmul(B.PS[bk][:, 0:TB], bdiag, s_t, start=True, stop=True)
            mmv = em.inc(i, "mm")
            em.wait("scalar", "mm", mmv)
            i = em.sc().activation(s_t, B.PS[bk][:, 0:TB], AF.Sqrt)
            av = em.inc(i, "act")
            banks.done(bk, "act", av)
            em.wait("vector", "act", av)
            i = em.ve().tensor_scalar_max(s_t, s_t, sca(B, "eps12"))
            em.inc(i, "vec")
            em.ve().drain(fusable=False)
            i = em.ve().reciprocal(s_t, s_t)
            em.inc(i, "vec")
            em.ve().drain(fusable=False)
            i = em.ve().tensor_mul(o1_t, rm_t, s_t)
            vv = em.inc(i, "vec")
            em.ve().drain(fusable=False)
            em.wait("gpsimd", "vec", vv)
            i = em.gp().dma_start(out=KN[ts(c), t0:t0 + TB], in_=o1_t)
            sv = em.inc(i, "st0", 16)
            i = em.ve().tensor_mul(o2_t, i_t(c), o1_t)
            vv = em.inc(i, "vec")
            em.ve().drain(fusable=False)
            em.wait("gpsimd", "vec", vv)
            i = em.gp().dma_start(out=BV[ts(c), t0:t0 + TB], in_=o2_t)
            sv = em.inc(i, "st0", 16)
            i = em.ve().tensor_mul(s_t, k_t(c), i_t(c))
            em.inc(i, "vec")
            em.ve().drain(fusable=False)
            i = em.ve().tensor_scalar(o3_t, k_t(c), om_sb[:, c:c + 1], None, ALU.mult)
            em.inc(i, "vec")
            em.ve().drain(fusable=False)
            i = em.ve().scalar_tensor_tensor(o3_t, s_t, mix_sb[:, c:c + 1], o3_t,
                                             ALU.mult, ALU.add)
            vv = em.inc(i, "vec")
            em.ve().drain(fusable=False)
            em.wait("gpsimd", "vec", vv)
            i = em.gp().dma_start(out=RK[ts(c), t0:t0 + TB], in_=o3_t)
            sv = em.inc(i, "st0", 16)
            em.wait("vector", "st0", sv)
        em.wait("sync", "vec", em.c["vec"])
    em.wait("gpsimd", "st0", em.c["st0"])
    em.barrier("prep")


def phase_deltanet(em, B, ldsem="ld5"):
    CC = 128
    NCH = T // CC
    KNs = B.scratch["KN"].ap()
    BVs = B.scratch["BVEC"].ap()
    RKs = B.scratch["RK"].ap()
    VVs = B.scratch["VV"].ap()
    RRs = B.scratch["RR"].ap()
    SGs = B.scratch["SIGT"].ap()
    DO = B.scratch["DOUT"].ap()
    ident = con(B, "ident")
    id64 = B.CON[0:64, CON_OFF["ident"]: CON_OFF["ident"] + 64]
    trilS = con(B, "trilS")
    triuS = con(B, "triuS")
    triuI = con(B, "triuI")
    ntriuI = con(B, "ntriuI")

    S_f = lambda h: B.AR2[0:64, h * 64: h * 64 + 64]
    ST_f = lambda h: B.AR2[0:64, 1024 + h * 64: 1024 + h * 64 + 64]
    zeros_t = B.AR2[0:64, 2048:2176]

    def W(par, idx, rows=64, width=CC):
        off = par * 8500 + idx * 128
        return B.AR3[0:rows, off:off + width]

    def W128(par, idx, width=CC):
        off = par * 8500 + idx * 128
        return B.AR3[0:128, off:off + width]

    IN_G, IN_KN, IN_BV, IN_RK, IN_V, IN_R = 0, 1, 2, 3, 4, 5
    T_G, T_GP, T_E1, T_E2, T_P = 6, 7, 8, 9, 10
    T_B, T_K, T_V = 11, 12, 13
    T_LM, T_LMT, T_WVB = 14, 15, 16
    T_M2T, T_M3T = 17, 18
    T_PW = 19
    T_PWT = 25
    T_RKT, T_VT, T_KTN, T_U0, T_U1 = 31, 32, 33, 34, 35
    T_UF, T_OSB, T_OUTF = 36, 37, 38

    i = em.ve().memset(B.AR2[0:64, 0:2176], 0.0)
    em.inc(i, "vec")
    em.ve().drain(fusable=False)
    em.barrier("dn0")

    banksP = [Banks(em, (0, 1, 2, 3)), Banks(em, (4, 5, 6, 7))]
    prev = [None, None]
    svec = [0] * H
    sact = [0] * H

    units = [(c, h) for c in range(NCH) for h in range(H)]
    for u, (c, h) in enumerate(units):
        par = u % 2
        banks = banksP[par]
        bk_i = [par * 4, par * 4 + 1, par * 4 + 2, par * 4 + 3]
        t0 = c * CC
        hr = h * 64
        if prev[par] is not None:
            for sem, val in prev[par]:
                em.wait("sync", sem, val)
                em.wait("vector", sem, val)
        ls = em.batch(ldsem)
        for idx, src in ((IN_G, SGs), (IN_KN, KNs), (IN_BV, BVs),
                         (IN_RK, RKs), (IN_V, VVs), (IN_R, RRs)):
            i = em.sy().dma_start(out=W(par, idx), in_=src[hr:hr + 64, t0:t0 + CC])
            em.inc(i, ls, 16)
        ldv = em.c[ls]
        em.wait("vector", ls, ldv)
        i = em.ve().tensor_tensor_scan(W(par, T_G), W(par, IN_G), zeros_t, 0.0,
                                       ALU.add, ALU.add)
        em.inc(i, "vec")
        em.ve().drain(fusable=False)
        i = em.ve().tensor_sub(W(par, T_GP), W(par, T_G), W(par, IN_G))
        vv = em.inc(i, "vec")
        em.wait("scalar", "vec", vv)
        i = em.sc().activation(W(par, T_E1), W(par, T_GP), AF.Exp, scale=-DECAY_C)
        em.inc(i, "act")
        i = em.sc().activation(W(par, T_E2), W(par, T_G), AF.Exp, scale=DECAY_C)
        em.inc(i, "act")
        i = em.sc().activation(W(par, T_P), W(par, T_G), AF.Exp, scale=-DECAY_C)
        av = em.inc(i, "act")
        em.wait("vector", "act", av)
        i = em.ve().tensor_mul(W(par, T_B), W(par, IN_BV), W(par, T_E1))
        em.inc(i, "vec")
        em.ve().drain(fusable=False)
        i = em.ve().tensor_mul(W(par, T_K), W(par, IN_KN), W(par, T_E2))
        em.inc(i, "vec")
        em.ve().drain(fusable=False)
        i = em.ve().tensor_mul(W(par, T_V), W(par, IN_V), W(par, T_E2))
        vv_bkv = em.inc(i, "vec")
        em.wait("tensor", "vec", vv_bkv)
        em.wait("tensor", "vec", svec[h])
        em.wait("tensor", "act", sact[h])

        bk = banks.use(bk_i[0])
        i = em.te().matmul(B.PS[bk][:, 0:CC], W(par, T_B), W(par, T_K),
                           start=True, stop=True)
        mv = em.inc(i, "mm")
        em.wait("vector", "mm", mv)
        i = em.ve().tensor_mul(W128(par, T_LM), B.PS[bk][:, 0:CC], trilS)
        vv = em.inc(i, "vec")
        em.ve().drain(fusable=False)
        banks.done(bk, "vec", vv)
        bk = banks.use(bk_i[1])
        i = em.te().matmul(B.PS[bk][:, 0:CC], W(par, T_K), W(par, T_B),
                           start=True, stop=True)
        mv = em.inc(i, "mm")
        em.wait("vector", "mm", mv)
        i = em.ve().tensor_mul(W128(par, T_LMT), B.PS[bk][:, 0:CC], triuS)
        vv = em.inc(i, "vec")
        em.ve().drain(fusable=False)
        banks.done(bk, "vec", vv)
        bk = banks.use(bk_i[2])
        i = em.te().matmul(B.PS[bk][:, 0:CC], W(par, T_V), W(par, T_B),
                           start=True, stop=True)
        mv = em.inc(i, "mm")
        em.wait("vector", "mm", mv)
        i = em.ve().tensor_mul(W128(par, T_WVB), B.PS[bk][:, 0:CC], triuS)
        vv_wvb = em.inc(i, "vec")
        em.ve().drain(fusable=False)
        banks.done(bk, "vec", vv_wvb)

        bk = banks.use(bk_i[3])
        i = em.te().transpose(B.PS[bk][:, 0:64], W(par, IN_RK), id64)
        mv = em.inc(i, "mm")
        em.wait("scalar", "mm", mv)
        i = em.sc().activation(W128(par, T_RKT, 64), B.PS[bk][:, 0:64], AF.Copy)
        av = em.inc(i, "act")
        banks.done(bk, "act", av)
        bk = banks.use(bk_i[0])
        i = em.te().transpose(B.PS[bk][:, 0:64], W(par, T_V), id64)
        mv = em.inc(i, "mm")
        em.wait("scalar", "mm", mv)
        i = em.sc().activation(W128(par, T_VT, 64), B.PS[bk][:, 0:64], AF.Copy)
        av = em.inc(i, "act")
        banks.done(bk, "act", av)
        bk = banks.use(bk_i[1])
        i = em.te().transpose(B.PS[bk][:, 0:64], W(par, T_K), id64)
        mv = em.inc(i, "mm")
        em.wait("scalar", "mm", mv)
        i = em.sc().mul(W128(par, T_KTN, 64), B.PS[bk][:, 0:64], -1.0)
        av_ktn = em.inc(i, "act")
        banks.done(bk, "act", av_ktn)

        bk_u = banks.use(bk_i[2])
        i = em.te().matmul(B.PS[bk_u][:, 0:64], W(par, T_B), S_f(h),
                           start=True, stop=False)
        em.inc(i, "mm")
        em.wait("tensor", "act", em.c["act"])
        i = em.te().matmul(B.PS[bk_u][:, 0:64], W128(par, T_WVB),
                           W128(par, T_RKT, 64), start=False, stop=True)
        mv = em.inc(i, "mm")
        em.wait("scalar", "mm", mv)
        i = em.sc().activation(W128(par, T_U0, 64), B.PS[bk_u][:, 0:64], AF.Copy)
        av = em.inc(i, "act")
        banks.done(bk_u, "act", av)

        curT, cur = T_LMT, T_LM
        for j in range(6):
            bk = banks.use(bk_i[3 - (j % 2)])
            i = em.te().matmul(B.PS[bk][:, 0:CC], W128(par, curT), W128(par, cur),
                               start=True, stop=True)
            mv = em.inc(i, "mm")
            em.wait("scalar", "mm", mv)
            i = em.sc().activation(W128(par, T_PW + j), B.PS[bk][:, 0:CC], AF.Copy)
            av = em.inc(i, "act")
            banks.done(bk, "act", av)
            bk = banks.use(bk_i[2 + (j % 2)])
            i = em.te().matmul(B.PS[bk][:, 0:CC], W128(par, cur), W128(par, curT),
                               start=True, stop=True)
            mv = em.inc(i, "mm")
            em.wait("scalar", "mm", mv)
            i = em.sc().activation(W128(par, T_PWT + j), B.PS[bk][:, 0:CC], AF.Copy)
            av = em.inc(i, "act")
            banks.done(bk, "act", av)
            em.wait("tensor", "act", av)
            cur, curT = T_PW + j, T_PWT + j

        ucur, unext = T_U0, T_U1
        for j in range(5, -2, -1):
            bk = banks.use(bk_i[(6 - j) % 2])
            lt = T_PWT + j if j >= 0 else T_LMT
            i = em.te().matmul(B.PS[bk][:, 0:64], W128(par, lt), W128(par, ucur, 64),
                               start=True, stop=True)
            mv = em.inc(i, "mm")
            em.wait("vector", "mm", mv)
            op = ALU.add if j >= 0 else ALU.subtract
            i = em.ve().tensor_tensor(W128(par, unext, 64), W128(par, ucur, 64),
                                      B.PS[bk][:, 0:64], op)
            vv = em.inc(i, "vec")
            em.ve().drain(fusable=False)
            banks.done(bk, "vec", vv)
            em.wait("tensor", "vec", vv)
            ucur, unext = unext, ucur
        ufin = ucur

        bk = banks.use(bk_i[0])
        i = em.te().transpose(B.PS[bk][0:64, 0:CC], W128(par, ufin, 64), ident)
        mv = em.inc(i, "mm")
        em.wait("scalar", "mm", mv)
        i = em.sc().activation(W(par, T_UF), B.PS[bk][0:64, 0:CC], AF.Copy)
        av_uf = em.inc(i, "act")
        banks.done(bk, "act", av_uf)

        bk = banks.use(bk_i[1])
        i = em.te().matmul(B.PS[bk][:, 0:CC], W(par, IN_RK), W(par, IN_R),
                           start=True, stop=True)
        mv = em.inc(i, "mm")
        em.wait("vector", "mm", mv)
        i = em.ve().tensor_mul(W128(par, T_M2T), B.PS[bk][:, 0:CC], triuI)
        vv = em.inc(i, "vec")
        em.ve().drain(fusable=False)
        banks.done(bk, "vec", vv)
        em.wait("tensor", "act", av_uf)
        bk = banks.use(bk_i[2])
        i = em.te().matmul(B.PS[bk][:, 0:CC], W(par, T_UF), W(par, IN_R),
                           start=True, stop=True)
        mv = em.inc(i, "mm")
        em.wait("vector", "mm", mv)
        i = em.ve().tensor_mul(W128(par, T_M3T), B.PS[bk][:, 0:CC], triuI)
        vv_m3 = em.inc(i, "vec")
        em.ve().drain(fusable=False)
        banks.done(bk, "vec", vv_m3)
        em.wait("tensor", "vec", vv_m3)

        bk_o = banks.use(bk_i[3])
        i = em.te().matmul(B.PS[bk_o][:, 0:64], W128(par, T_M2T),
                           W128(par, T_VT, 64), start=True, stop=False)
        em.inc(i, "mm")
        i = em.te().matmul(B.PS[bk_o][:, 0:64], W128(par, T_M3T),
                           W128(par, T_KTN, 64), start=False, stop=False)
        em.inc(i, "mm")
        i = em.te().matmul(B.PS[bk_o][:, 0:64], W(par, IN_R), ST_f(h),
                           start=False, stop=True)
        mv = em.inc(i, "mm")
        em.wait("scalar", "mm", mv)
        i = em.sc().activation(W128(par, T_OSB, 64), B.PS[bk_o][:, 0:64], AF.Copy)
        av = em.inc(i, "act")
        banks.done(bk_o, "act", av)
        em.wait("tensor", "act", av)
        bk = banks.use(bk_i[0])
        i = em.te().transpose(B.PS[bk][0:64, 0:CC], W128(par, T_OSB, 64), ident)
        mv = em.inc(i, "mm")
        em.wait("vector", "mm", mv)
        stn = em.ldn("st0", u)
        if prev[par] is not None:
            em.wait("vector", prev[par][3][0], prev[par][3][1])
        i = em.ve().tensor_mul(W(par, T_OUTF), B.PS[bk][0:64, 0:CC], W(par, T_P))
        vv = em.inc(i, "vec")
        em.ve().drain(fusable=False)
        banks.done(bk, "vec", vv)
        em.wait("gpsimd", "vec", vv)
        i = em.gp().dma_start(out=DO[hr:hr + 64, t0:t0 + CC], in_=W(par, T_OUTF))
        stv = em.inc(i, stn, 16)

        bk_s = banks.use(bk_i[1])
        i = em.te().matmul(B.PS[bk_s][0:64, 0:64], W128(par, T_VT, 64),
                           W128(par, T_RKT, 64), start=True, stop=False)
        em.inc(i, "mm")
        i = em.te().matmul(B.PS[bk_s][0:64, 0:64], W128(par, T_KTN, 64),
                           W128(par, ufin, 64), start=False, stop=True)
        mv = em.inc(i, "mm")
        em.wait("vector", "mm", mv)
        i = em.ve().tensor_add(W(par, T_GP, 64, 64), B.PS[bk_s][0:64, 0:64], S_f(h))
        em.inc(i, "vec")
        em.ve().drain(fusable=False)
        i = em.ve().tensor_scalar(S_f(h), W(par, T_GP, 64, 64),
                                  W(par, T_P)[:, 127:128], None, ALU.mult)
        vv = em.inc(i, "vec")
        em.ve().drain(fusable=False)
        banks.done(bk_s, "vec", vv)
        svec[h] = vv
        em.wait("tensor", "vec", vv)
        bk = banks.use(bk_i[2])
        i = em.te().transpose(B.PS[bk][0:64, 0:64], S_f(h), id64)
        mv = em.inc(i, "mm")
        em.wait("scalar", "mm", mv)
        i = em.sc().activation(ST_f(h), B.PS[bk][0:64, 0:64], AF.Copy)
        av = em.inc(i, "act")
        banks.done(bk, "act", av)
        sact[h] = av

        prev[par] = [("mm", em.c["mm"]), ("act", em.c["act"]),
                     ("vec", em.c["vec"]), (stn, stv)]
    em.barrier("deltanet")


def phase_l0_combine(em, B, ldsem="ld6"):
    DOs = B.scratch["DOUT"].ap()
    RRs = B.scratch["RR"].ap()
    RKs = B.scratch["RK"].ap()
    VVs = B.scratch["VV"].ap()
    GTs = B.scratch["GATE"].ap()
    OIN = B.scratch["OIN"].ap()
    bm = B.inputs["L0_bonus_mult"]
    lng = B.inputs["L0_ln_g"]
    lnb = B.inputs["L0_ln_b"]
    bdiag = con(B, "bdiag64")
    onesT = B.CON[:, CON_OFF["ones"]:CON_OFF["ones"] + 128]

    bm_sb = B.AR2[:, 12800:12808]
    lng_sb = B.AR2[:, 12808:12816]
    lnb_sb = B.AR2[:, 12816:12824]
    d_t = lambda c: B.AR3[:, c * TB: (c + 1) * TB]
    r_t = B.AR3[:, 4096: 4096 + TB]
    k_t = B.AR3[:, 4608: 4608 + TB]
    v_t = B.AR3[:, 5120: 5120 + TB]
    t_t = B.AR3[:, 5632: 5632 + TB]
    x2_t = B.AR3[:, 6144: 6144 + TB]
    mu_t = B.AR3[:, 6656: 6656 + TB]
    iv_t = B.AR3[:, 7168: 7168 + TB]
    g_t = B.AR3[:, 7680: 7680 + TB]
    o_t = B.AR3[:, 8192: 8192 + TB]

    ls0 = em.batch(ldsem)
    for c in range(CH):
        i = em.sy().dma_start(out=bm_sb[:, c:c + 1], in_=col(bm, c))
        em.inc(i, ls0, 16)
        i = em.sy().dma_start(out=lng_sb[:, c:c + 1], in_=col(lng, c))
        em.inc(i, ls0, 16)
        i = em.sy().dma_start(out=lnb_sb[:, c:c + 1], in_=col(lnb, c))
        em.inc(i, ls0, 16)
    ldw = em.c[ls0]
    em.wait("vector", ls0, ldw)
    i = em.ve().tensor_scalar(bm_sb, bm_sb, 0.2, None, ALU.mult)
    em.inc(i, "vec")
    em.ve().drain(fusable=False)
    banks = Banks(em, range(8))

    for b in range(NBLK):
        t0 = b * TB
        lsd = em.batch(ldsem)
        for c in range(CH):
            i = em.sy().dma_start(out=d_t(c), in_=DOs[ts(c), t0:t0 + TB])
            em.inc(i, lsd, 16)
        em.wait("vector", lsd, em.c[lsd])
        for c in range(CH):
            ls = em.batch(ldsem)
            i = em.sy().dma_start(out=r_t, in_=RRs[ts(c), t0:t0 + TB])
            em.inc(i, ls, 16)
            i = em.sy().dma_start(out=k_t, in_=RKs[ts(c), t0:t0 + TB])
            em.inc(i, ls, 16)
            i = em.sy().dma_start(out=v_t, in_=VVs[ts(c), t0:t0 + TB])
            em.inc(i, ls, 16)
            ldv = em.c[ls]
            em.wait("vector", ls, ldv)
            i = em.ve().tensor_mul(t_t, r_t, k_t)
            vv = em.inc(i, "vec")
            em.ve().drain(fusable=False)
            em.wait("tensor", "vec", vv)
            bk = banks.use(c % 2)
            i = em.te().matmul(B.PS[bk][:, 0:TB], bdiag, t_t, start=True, stop=True)
            mv = em.inc(i, "mm")
            em.wait("vector", "mm", mv)
            i = em.ve().tensor_mul(t_t, v_t, B.PS[bk][:, 0:TB])
            vv = em.inc(i, "vec")
            em.ve().drain(fusable=False)
            banks.done(bk, "vec", vv)
            i = em.ve().scalar_tensor_tensor(d_t(c), t_t, bm_sb[:, c:c + 1], d_t(c),
                                             ALU.mult, ALU.add)
            vv = em.inc(i, "vec")
            em.ve().drain(fusable=False)
            em.wait("sync", "vec", vv)
        em.wait("scalar", "vec", em.c["vec"])
        for c in range(CH):
            i = em.sc().activation(x2_t, d_t(c), AF.Square)
            sq = em.inc(i, "act")
            em.wait("tensor", "act", sq)
            i = em.te().matmul(B.PS[2][:, 0:TB], onesT, d_t(c),
                               start=(c == 0), stop=(c == CH - 1))
            em.inc(i, "mm")
            i = em.te().matmul(B.PS[3][:, 0:TB], onesT, x2_t,
                               start=(c == 0), stop=(c == CH - 1))
            mmv = em.inc(i, "mm")
            em.wait("scalar", "mm", mmv)
        mmv = em.c["mm"]
        em.wait("vector", "mm", mmv)
        i = em.ve().tensor_scalar(mu_t, B.PS[2][:, 0:TB], 1.0 / D, None, ALU.mult)
        em.inc(i, "vec")
        em.ve().drain(fusable=False)
        i = em.ve().tensor_tensor(iv_t, mu_t, mu_t, ALU.mult)
        em.inc(i, "vec")
        em.ve().drain(fusable=False)
        i = em.ve().scalar_tensor_tensor(iv_t, B.PS[3][:, 0:TB], 1.0 / D, iv_t,
                                         ALU.mult, ALU.subtract)
        vv = em.inc(i, "vec")
        em.wait("scalar", "vec", vv)
        i = em.sc().activation(iv_t, iv_t, AF.Sqrt, bias=sca(B, "eps_attln"))
        sv = em.inc(i, "act")
        em.wait("vector", "act", sv)
        i = em.ve().reciprocal(iv_t, iv_t)
        em.inc(i, "vec")
        em.ve().drain(fusable=False)
        for c in range(CH):
            ls = em.batch(ldsem)
            i = em.sy().dma_start(out=g_t, in_=GTs[ts(c), t0:t0 + TB])
            ldv = em.inc(i, ls, 16)
            i = em.ve().tensor_sub(o_t, d_t(c), mu_t)
            em.inc(i, "vec")
            em.ve().drain(fusable=False)
            i = em.ve().tensor_mul(o_t, o_t, iv_t)
            em.inc(i, "vec")
            em.ve().drain(fusable=False)
            i = em.ve().tensor_scalar(o_t, o_t, lng_sb[:, c:c + 1],
                                      lnb_sb[:, c:c + 1], ALU.mult, ALU.add)
            em.inc(i, "vec")
            em.ve().drain(fusable=False)
            em.wait("vector", ls, ldv)
            i = em.ve().tensor_mul(o_t, o_t, g_t)
            vv = em.inc(i, "vec")
            em.ve().drain(fusable=False)
            em.wait("gpsimd", "vec", vv)
            i = em.gp().dma_start(out=OIN[ts(c), t0:t0 + TB], in_=o_t)
            sv = em.inc(i, "st0", 16)
            em.wait("vector", "st0", sv)
            em.wait("sync", "vec", vv)
    em.wait("gpsimd", "st0", em.c["st0"])
    em.barrier("combine")


def phase_l0ffn(em, B, ldsem="ld7"):
    XN2 = B.scratch["XN2"].ap()
    XR1 = B.scratch["XR1"].ap()
    XR2 = B.scratch["XR2"].ap()
    Win = B.inputs["F0_Win"]
    binv = B.inputs["F0_bin"]
    W1 = B.inputs["F0_W1"]
    b1 = B.inputs["F0_b1"]
    W2 = B.inputs["F0_W2"]
    b2 = B.inputs["F0_b2"]
    Wout = B.inputs["F0_Wout"]
    bout = B.inputs["F0_bout"]

    win_t = lambda c: B.AR1[:, c * 64: (c + 1) * 64]
    wout_t = B.AR1[0:64, 512: 512 + D]
    w1_sb = B.AR2[0:64, 12800:12816]
    b1_sb = B.AR2[0:64, 12816:12832]
    w2_sb = B.AR2[0:64, 12832:12848]
    bin_sb = B.AR2[0:64, 12848:12849]
    b2_sb = B.AR2[0:64, 12849:12850]
    bout_sb = B.AR2[:, 12850:12858]
    x_t = lambda c: B.AR3[:, c * TB: (c + 1) * TB]
    res_t = lambda c: B.AR3[:, 4096 + c * TB: 4096 + (c + 1) * TB]
    s_sb = B.AR3[0:64, 8192: 8192 + TB]
    t_sb = B.AR3[0:64, 8704: 8704 + TB]
    ga_sb = B.AR3[0:64, 9216: 9216 + TB]
    gb_sb = B.AR3[0:64, 9728: 9728 + TB]
    o_t = B.AR3[:, 10240: 10240 + TB]

    ls0 = em.batch(ldsem)
    for c in range(CH):
        i = em.sy().dma_start(out=win_t(c), in_=Win[ts(c), :])
        em.inc(i, ls0, 16)
        i = em.sy().dma_start(out=bout_sb[:, c:c + 1], in_=col(bout, c))
        em.inc(i, ls0, 16)
    i = em.sy().dma_start(out=wout_t, in_=Wout.ap())
    em.inc(i, ls0, 16)
    i = em.sy().dma_start(out=w1_sb, in_=W1.ap())
    em.inc(i, ls0, 16)
    i = em.sy().dma_start(out=b1_sb, in_=b1.ap())
    em.inc(i, ls0, 16)
    i = em.sy().dma_start(out=w2_sb, in_=W2.ap())
    em.inc(i, ls0, 16)
    i = em.sy().dma_start(out=bin_sb, in_=colw(binv, 0, 64))
    em.inc(i, ls0, 16)
    i = em.sy().dma_start(out=b2_sb, in_=colw(b2, 0, 64))
    em.inc(i, ls0, 16)
    ldw = em.c[ls0]
    em.wait("tensor", ls0, ldw)
    em.wait("scalar", ls0, ldw)
    em.wait("vector", ls0, ldw)
    banks = Banks(em, range(8))

    for b in range(NBLK):
        t0 = b * TB
        ls = em.batch(ldsem)
        for c in range(CH):
            i = em.sy().dma_start(out=x_t(c), in_=XN2[ts(c), t0:t0 + TB])
            em.inc(i, ls, 16)
            i = em.sy().dma_start(out=res_t(c), in_=XR1[ts(c), t0:t0 + TB])
            em.inc(i, ls, 16)
        ldv = em.c[ls]
        em.wait("tensor", ls, ldv)
        em.wait("vector", ls, ldv)
        bk = banks.use(0)
        for c in range(CH):
            i = em.te().matmul(B.PS[bk][0:64, 0:TB], win_t(c), x_t(c),
                               start=(c == 0), stop=(c == CH - 1))
            mv = em.inc(i, "mm")
        em.wait("scalar", "mm", mv)
        i = em.sc().activation(s_sb, B.PS[bk][0:64, 0:TB], AF.Identity, bias=bin_sb)
        av = em.inc(i, "act")
        em.sc().drain(fusable=False)
        banks.done(bk, "act", av)
        em.wait("vector", "act", av)
        cur, nxt = ga_sb, gb_sb
        for hh in range(16):
            i = em.sc().activation(t_sb, s_sb, AF.Relu,
                                   bias=b1_sb[:, hh:hh + 1], scale=w1_sb[:, hh:hh + 1])
            av = em.inc(i, "act")
            em.wait("vector", "act", av)
            if hh == 0:
                i = em.ve().tensor_scalar(cur, t_sb, w2_sb[:, hh:hh + 1], None, ALU.mult)
            else:
                i = em.ve().scalar_tensor_tensor(nxt, t_sb, w2_sb[:, hh:hh + 1],
                                                 cur, ALU.mult, ALU.add)
                cur, nxt = nxt, cur
            vv = em.inc(i, "vec")
            em.ve().drain(fusable=False)
            em.wait("scalar", "vec", vv)
        i = em.ve().tensor_scalar(cur, cur, b2_sb, None, ALU.add)
        vv = em.inc(i, "vec")
        em.ve().drain(fusable=False)
        em.wait("tensor", "vec", vv)
        for oc in range(CH):
            bk = banks.use(1 + oc % 4)
            i = em.te().matmul(B.PS[bk][:, 0:TB], wout_t[:, ts(oc)], cur,
                               start=True, stop=True)
            mv = em.inc(i, "mm")
            em.wait("vector", "mm", mv)
            i = em.ve().scalar_tensor_tensor(o_t, B.PS[bk][:, 0:TB],
                                             bout_sb[:, oc:oc + 1], res_t(oc),
                                             ALU.add, ALU.add)
            vv = em.inc(i, "vec")
            em.ve().drain(fusable=False)
            banks.done(bk, "vec", vv)
            em.wait("gpsimd", "vec", vv)
            i = em.gp().dma_start(out=XR2[ts(oc), t0:t0 + TB], in_=o_t)
            sv = em.inc(i, "st0", 16)
            em.wait("vector", "st0", sv)
        em.wait("sync", "mm", em.c["mm"])
        em.wait("sync", "vec", em.c["vec"])
    em.wait("gpsimd", "st0", em.c["st0"])
    em.barrier("l0ffn")


def phase_proj(em, B, src, dst, Wname, ldsem, tag, resid=None, bias=None, li=None):
    Wt = B.inputs[Wname]
    w_t = lambda c: B.AR1[:, c * D: (c + 1) * D]
    x_t = lambda c: B.AR3[:, c * TB: (c + 1) * TB]
    res_t = lambda c: B.AR3[:, 4096 + c * TB: 4096 + (c + 1) * TB]
    o_t = B.AR3[:, 8192: 8192 + TB]
    ls0 = em.batch(ldsem)
    for c in range(CH):
        wsrc = Wt[ts(c), :] if li is None else Wt[li, ts(c), :]
        i = em.sy().dma_start(out=w_t(c), in_=wsrc)
        em.inc(i, ls0, 16)
    em.wait("tensor", ls0, em.c[ls0])
    banks = Banks(em, range(8))
    for b in range(NBLK):
        t0 = b * TB
        ls = em.batch(ldsem)
        for c in range(CH):
            i = em.sy().dma_start(out=x_t(c), in_=src[ts(c), t0:t0 + TB])
            em.inc(i, ls, 16)
            if resid is not None:
                i = em.sy().dma_start(out=res_t(c), in_=resid[ts(c), t0:t0 + TB])
                em.inc(i, ls, 16)
        ldv = em.c[ls]
        em.wait("tensor", ls, ldv)
        em.wait("vector", ls, ldv)
        for oc in range(CH):
            bk = banks.use(oc % 8)
            for c in range(CH):
                i = em.te().matmul(B.PS[bk][:, 0:TB], w_t(c)[:, ts(oc)], x_t(c),
                                   start=(c == 0), stop=(c == CH - 1))
                mmv = em.inc(i, "mm")
            em.wait("vector", "mm", mmv)
            if resid is not None:
                i = em.ve().tensor_add(o_t, B.PS[bk][:, 0:TB], res_t(oc))
            else:
                i = em.ve().tensor_copy(o_t, B.PS[bk][:, 0:TB])
            vv = em.inc(i, "vec")
            em.ve().drain(fusable=False)
            banks.done(bk, "vec", vv)
            em.wait("gpsimd", "vec", vv)
            i = em.gp().dma_start(out=dst[ts(oc), t0:t0 + TB], in_=o_t)
            sv = em.inc(i, "st0", 16)
            em.wait("vector", "st0", sv)
        em.wait("sync", "mm", em.c["mm"])
        em.wait("sync", "vec", em.c["vec"])
    em.wait("gpsimd", "st0", em.c["st0"])
    em.barrier("proj" + tag)


def phase_l1attn_pre(em, B, ldsem="ld8"):
    XN3 = B.scratch["XN3"].ap()
    QF = B.scratch["QF"].ap()
    KF = B.scratch["KF"].ap()
    VTM = B.scratch["VTM"].ap()
    oneh = B.inputs["c_t5oneh"]
    table = B.inputs["A1_table"]

    oneh_sb = B.AR1[0:32, 0:4096]
    tab_sb = B.AR1[0:32, 4096:4112]
    w5_sb = B.AR1[0:16, 4112: 4112 + 4096]
    ls0 = em.batch(ldsem)
    i = em.sy().dma_start(out=oneh_sb, in_=oneh.ap())
    em.inc(i, ls0, 16)
    i = em.sy().dma_start(out=tab_sb, in_=table.ap())
    em.inc(i, ls0, 16)
    ldv = em.c[ls0]
    em.wait("tensor", ls0, ldv)
    banks = Banks(em, range(8))
    for j in range(8):
        bk = banks.use(j % 4)
        i = em.te().matmul(B.PS[bk][0:16, 0:TB], tab_sb, oneh_sb[:, j * TB:(j + 1) * TB],
                           start=True, stop=True)
        mv = em.inc(i, "mm")
        em.wait("scalar", "mm", mv)
        i = em.sc().activation(w5_sb[:, j * TB:(j + 1) * TB], B.PS[bk][0:16, 0:TB],
                               AF.Copy)
        av = em.inc(i, "act")
        banks.done(bk, "act", av)
    avw = em.c["act"]
    em.wait("gpsimd", "act", avw)
    i = em.gp().dma_start(out=B.scratch["W5"].ap(), in_=B.AR1[0:16, 4112: 4112 + 4096])
    wv5 = em.inc(i, "st1", 16)
    em.wait("gpsimd", "st1", wv5)
    for h in range(H):
        wsh = B.scratch[f"WSH{h}"].reshape([128, 4226])
        i = em.gp().dma_start(out=wsh[:, 0:4095],
                              in_=B.scratch["W5"][h:h + 1, 0:4095].broadcast_to([128, 4095]))
        em.inc(i, "st1", 16)

    wq_t = lambda c: B.AR2[:, c * D: (c + 1) * D]
    x_t = lambda c: B.AR3[:, c * TB: (c + 1) * TB]
    o_t = B.AR3[:, 4096: 4096 + TB]
    for b in range(NBLK):
        t0 = b * TB
        lsx = em.batch(ldsem)
        for c in range(CH):
            i = em.sy().dma_start(out=x_t(c), in_=XN3[ts(c), t0:t0 + TB])
            em.inc(i, lsx, 16)
        ldv = em.c[lsx]
        em.wait("tensor", lsx, ldv)
        for wi, (wn, dst) in enumerate((("A1_Wq", QF), ("A1_Wk", KF))):
            Wh = B.inputs[wn]
            em.wait("sync", "mm", em.c["mm"])
            ls = em.batch(ldsem)
            for c in range(CH):
                i = em.sy().dma_start(out=wq_t(c) if wi == 0 else wq_t(c),
                                      in_=Wh[ts(c), :])
                em.inc(i, ls, 16)
            ldv = em.c[ls]
            em.wait("tensor", ls, ldv)
            for oc in range(CH):
                bk = banks.use(oc % 4)
                for c in range(CH):
                    i = em.te().matmul(B.PS[bk][:, 0:TB], wq_t(c)[:, ts(oc)], x_t(c),
                                       start=(c == 0), stop=(c == CH - 1))
                    mmv = em.inc(i, "mm")
                em.wait("scalar", "mm", mmv)
                i = em.sc().activation(o_t, B.PS[bk][:, 0:TB], AF.Copy)
                av = em.inc(i, "act")
                banks.done(bk, "act", av)
                em.wait("gpsimd", "act", av)
                i = em.gp().dma_start(out=dst[ts(oc), t0:t0 + TB], in_=o_t)
                sv = em.inc(i, "st1", 16)
                em.wait("scalar", "st1", sv)
        Wv = B.inputs["A1_Wv"]
        em.wait("sync", "mm", em.c["mm"])
        ls = em.batch(ldsem)
        for c in range(CH):
            i = em.sy().dma_start(out=wq_t(c), in_=Wv[ts(c), :])
            em.inc(i, ls, 16)
        ldv = em.c[ls]
        em.wait("tensor", ls, ldv)
        for tt in range(TB // 128):
            for vb in range(2):
                bk = banks.use(4 + (tt * 2 + vb) % 4)
                for c in range(CH):
                    i = em.te().matmul(B.PS[bk][:, 0:TB], x_t(c)[:, tt * 128:(tt + 1) * 128],
                                       wq_t(c)[:, vb * TB:(vb + 1) * TB],
                                       start=(c == 0), stop=(c == CH - 1))
                    mmv = em.inc(i, "mm")
                em.wait("scalar", "mm", mmv)
                i = em.sc().activation(o_t, B.PS[bk][:, 0:TB], AF.Copy)
                av = em.inc(i, "act")
                banks.done(bk, "act", av)
                em.wait("gpsimd", "act", av)
                i = em.gp().dma_start(
                    out=VTM[t0 + tt * 128: t0 + (tt + 1) * 128, vb * TB:(vb + 1) * TB],
                    in_=o_t)
                sv = em.inc(i, "st1", 16)
                em.wait("scalar", "st1", sv)
        em.wait("sync", "mm", em.c["mm"])
    em.wait("gpsimd", "st1", em.c["st1"])
    em.barrier("l1pre")


def phase_l1attn_core(em, B, ldsem="ld9"):
    QF = B.scratch["QF"].ap()
    KF = B.scratch["KF"].ap()
    VTM = B.scratch["VTM"].ap()
    OATT = B.scratch["OATT"].ap()
    ident = con(B, "ident")
    NT = T // 128

    kf_sb = B.AR1[0:64, 0:T]
    vtm_sb = lambda kt: B.AR1[:, 2048 + kt * 64: 2048 + (kt + 1) * 64]
    q_sb = B.AR2[0:64, 0:128]
    bias_sb = lambda j: B.AR2[:, 128 + j * TB: 128 + (j + 1) * TB]
    sc_sb = lambda j: B.AR2[:, 2176 + j * TB: 2176 + (j + 1) * TB]
    att_sb = lambda kt: B.AR2[:, 4224 + kt * 128: 4224 + (kt + 1) * 128]
    red_sb = B.AR2[:, 6272: 6272 + 8]
    of_sb = B.AR3[0:64, 12000: 12000 + 128]

    banks = Banks(em, range(8))
    prevu = None
    for h in range(H):
        hr = h * 64
        wsh = B.scratch[f"WSH{h}"]
        if prevu is not None:
            for sem, val in prevu:
                em.wait("sync", sem, val)
        lsh = em.batch(ldsem)
        i = em.sy().dma_start(out=kf_sb, in_=KF[hr:hr + 64, :])
        em.inc(i, lsh, 16)
        for kt in range(NT):
            i = em.sy().dma_start(out=vtm_sb(kt), in_=VTM[kt * 128:(kt + 1) * 128, hr:hr + 64])
            em.inc(i, lsh, 16)
        ldh = em.c[lsh]
        for qt in range(NT):
            q0 = qt * 128
            ls = em.batch(ldsem)
            i = em.sy().dma_start(out=q_sb, in_=QF[hr:hr + 64, q0:q0 + 128])
            em.inc(i, ls, 16)
            for j in range(4):
                base = j * TB - q0 + 2047
                i = em.sy().dma_start(out=bias_sb(j),
                                      in_=mkAP(B.scratch[f"WSH{h}"], base,
                                               [[4225, 128], [1, TB]]))
                em.inc(i, ls, 16)
            ldv = em.c[ls]
            em.wait("tensor", ls, ldv)
            em.wait("tensor", lsh, ldh)
            em.wait("vector", ls, ldv)
            for j in range(4):
                bk = banks.use(j)
                i = em.te().matmul(B.PS[bk][:, 0:TB], q_sb, kf_sb[:, j * TB:(j + 1) * TB],
                                   start=True, stop=True)
                mmv = em.inc(i, "mm")
                em.wait("vector", "mm", mmv)
                i = em.ve().tensor_add(sc_sb(j), B.PS[bk][:, 0:TB], bias_sb(j))
                vv = em.inc(i, "vec")
                em.ve().drain(fusable=False)
                banks.done(bk, "vec", vv)
            i = em.ve().tensor_reduce(red_sb[:, 0:1], B.AR2[:, 2176: 2176 + T],
                                      AX.X, ALU.max, negate=True)
            vv = em.inc(i, "vec")
            em.ve().drain(fusable=False)
            em.wait("scalar", "vec", vv)
            for j in range(4):
                i = em.sc().activation(sc_sb(j), sc_sb(j), AF.Exp,
                                       bias=red_sb[:, 0:1],
                                       accum_out=red_sb[:, 2 + j: 3 + j])
                av = em.inc(i, "act")
                em.sc().drain(fusable=False)
            em.wait("vector", "act", em.c["act"])
            i = em.ve().tensor_reduce(red_sb[:, 6:7], red_sb[:, 2:6], AX.X, ALU.add)
            vv = em.inc(i, "vec")
            em.ve().drain(fusable=False)
            i = em.ve().reciprocal(red_sb[:, 6:7], red_sb[:, 6:7])
            vv = em.inc(i, "vec")
            em.ve().drain(fusable=False)
            em.wait("scalar", "vec", vv)
            for j in range(4):
                i = em.sc().mul(sc_sb(j), sc_sb(j), red_sb[:, 6:7])
                av = em.inc(i, "act")
                em.sc().drain(fusable=False)
            avn = em.c["act"]
            em.wait("tensor", "act", avn)
            for kt in range(NT):
                bk = banks.use(4 + kt % 2)
                i = em.te().transpose(B.PS[bk][:, 0:128],
                                      B.AR2[:, 2176 + kt * 128: 2176 + (kt + 1) * 128],
                                      ident)
                mv = em.inc(i, "mm")
                if kt % 2 == 0:
                    em.wait("scalar", "mm", mv)
                    i = em.sc().activation(att_sb(kt), B.PS[bk][:, 0:128], AF.Copy)
                    av = em.inc(i, "act")
                    banks.done(bk, "act", av)
                else:
                    em.wait("vector", "mm", mv)
                    i = em.ve().tensor_copy(att_sb(kt), B.PS[bk][:, 0:128])
                    av = em.inc(i, "vec")
                    em.ve().drain(fusable=False)
                    banks.done(bk, "vec", av)
            em.wait("tensor", "act", em.c["act"])
            em.wait("tensor", "vec", em.c["vec"])
            em.wait("tensor", lsh, ldh)
            bk_o = banks.use(6 + qt % 2)
            for kt in range(NT):
                i = em.te().matmul(B.PS[bk_o][0:64, 0:128], vtm_sb(kt), att_sb(kt),
                                   start=(kt == 0), stop=(kt == NT - 1))
                mv = em.inc(i, "mm")
            em.wait("vector", "mm", mv)
            i = em.ve().tensor_copy(of_sb, B.PS[bk_o][0:64, 0:128])
            vv = em.inc(i, "vec")
            em.ve().drain(fusable=False)
            banks.done(bk_o, "vec", vv)
            em.wait("gpsimd", "vec", vv)
            i = em.gp().dma_start(out=OATT[hr:hr + 64, q0:q0 + 128], in_=of_sb)
            sv = em.inc(i, "st0", 16)
            em.wait("vector", "st0", sv)
            em.wait("sync", "mm", em.c["mm"])
            em.wait("sync", "act", em.c["act"])
            em.wait("sync", "vec", em.c["vec"])
        prevu = [("mm", em.c["mm"]), ("act", em.c["act"]), ("vec", em.c["vec"])]
    em.wait("gpsimd", "st0", em.c["st0"])
    em.barrier("l1core")


def phase_ffqkv(em, B, li, src, ldsem="ld2"):
    Wqkv = B.inputs["FF_Wqkv"]
    QF2 = B.scratch["QF2"].ap()
    KF2 = B.scratch["KF2"].ap()
    V65 = B.scratch["V65"].ap()
    w_t = lambda c: B.AR1[:, c * D: (c + 1) * D]
    x_t = lambda c: B.AR3[:, c * TB: (c + 1) * TB]
    xs_t = lambda c: B.AR3[:, (4 + c) * TB: (5 + c) * TB]
    o_t = B.AR3[:, 8192: 8192 + TB]
    banks = Banks(em, range(8))
    for b in range(NBLK):
        t0 = b * TB
        lsx = em.batch(ldsem)
        for c in range(CH):
            if c >= 4:
                if b == 0:
                    i = em.ve().memset(xs_t(c - 4)[:, 0:1], 0.0)
                    em.inc(i, "vec")
                    em.ve().drain(fusable=False)
                    i = em.sy().dma_start(out=xs_t(c - 4)[:, 1:TB],
                                          in_=src[ts(c), 0:TB - 1])
                else:
                    i = em.sy().dma_start(out=xs_t(c - 4),
                                          in_=src[ts(c), t0 - 1:t0 + TB - 1])
            else:
                i = em.sy().dma_start(out=x_t(c), in_=src[ts(c), t0:t0 + TB])
            em.inc(i, lsx, 16)
        ldv = em.c[lsx]
        em.wait("tensor", lsx, ldv)
        em.wait("tensor", "vec", em.c["vec"])
        xin = lambda c: x_t(c) if c < 4 else xs_t(c - 4)
        for wi, dst in ((0, QF2), (1, KF2)):
            em.wait("sync", "mm", em.c["mm"])
            ls = em.batch(ldsem)
            for c in range(CH):
                i = em.sy().dma_start(out=w_t(c), in_=Wqkv[li, ts(c), wi * D:(wi + 1) * D])
                em.inc(i, ls, 16)
            ldv = em.c[ls]
            em.wait("tensor", ls, ldv)
            for oc in range(CH):
                bk = banks.use(oc % 4)
                for c in range(CH):
                    i = em.te().matmul(B.PS[bk][:, 0:TB], w_t(c)[:, ts(oc)], xin(c),
                                       start=(c == 0), stop=(c == CH - 1))
                    mmv = em.inc(i, "mm")
                em.wait("scalar", "mm", mmv)
                i = em.sc().activation(o_t, B.PS[bk][:, 0:TB], AF.Copy)
                av = em.inc(i, "act")
                banks.done(bk, "act", av)
                em.wait("gpsimd", "act", av)
                i = em.gp().dma_start(out=dst[ts(oc), t0:t0 + TB], in_=o_t)
                sv = em.inc(i, "st1", 16)
                em.wait("scalar", "st1", sv)
        em.wait("sync", "mm", em.c["mm"])
        ls = em.batch(ldsem)
        for c in range(CH):
            i = em.sy().dma_start(out=w_t(c), in_=Wqkv[li, ts(c), 2 * D: 3 * D])
            em.inc(i, ls, 16)
        ldv = em.c[ls]
        em.wait("tensor", ls, ldv)
        for tt in range(TB // 128):
            for vb in range(2):
                bk = banks.use(4 + (tt * 2 + vb) % 4)
                for c in range(CH):
                    i = em.te().matmul(B.PS[bk][:, 0:TB],
                                       xin(c)[:, tt * 128:(tt + 1) * 128],
                                       w_t(c)[:, vb * TB:(vb + 1) * TB],
                                       start=(c == 0), stop=(c == CH - 1))
                    mmv = em.inc(i, "mm")
                em.wait("scalar", "mm", mmv)
                i = em.sc().activation(
                    mkAP(B.AR3, 8192, [[16400, 128], [65, 8], [1, 64]]),
                    B.PS[bk][:, 0:TB], AF.Copy)
                av = em.inc(i, "act")
                banks.done(bk, "act", av)
                em.wait("gpsimd", "act", av)
                i = em.gp().dma_start(
                    out=mkAP(B.scratch["V65"],
                             (t0 + tt * 128) * (16 * 65) + vb * 8 * 65,
                             [[16 * 65, 128], [1, 8 * 65]]),
                    in_=B.AR3[:, 8192: 8192 + 8 * 65])
                sv = em.inc(i, "st1", 16)
                em.wait("scalar", "st1", sv)
        em.wait("sync", "mm", em.c["mm"])
    em.wait("gpsimd", "st1", em.c["st1"])
    em.barrier(f"ffqkv{li}")


def phase_rotnorm(em, B, li, ldsem="ld3"):
    rotp = con(B, "rotperm")
    bdiag = con(B, "bdiag64")
    sc_sb = B.AR2[:, 12800:12802]
    x_sb = B.AR3[:, 0:TB]
    sq_sb = B.AR3[:, 512:512 + TB]
    n_sb = B.AR3[:, 1024:1024 + TB]
    qn_sb = B.AR3[:, 1536:1536 + TB]
    qa_sb = B.AR3[:, 2048:2048 + TB]
    qo_sb = B.AR3[:, 2560:2560 + TB]
    ls0 = em.batch(ldsem)
    for half in range(2):
        for wi, name in ((0, "FF_qs"), (1, "FF_ks")):
            i = em.sy().dma_start(out=sc_sb[half * 64:(half + 1) * 64, wi:wi + 1],
                                  in_=B.inputs[name][li:li + 1, :].transpose([1, 0]))
            em.inc(i, ls0, 16)
    ldw = em.c[ls0]
    em.wait("vector", ls0, ldw)
    banks = Banks(em, range(8))
    for wi, (srcn, dstn, cs_n, ss_n) in enumerate(
            (("QF2", "QR", "csq", "ssq"), ("KF2", "KR", "csk", "ssk"))):
        srch = B.scratch[srcn].ap()
        dsth = B.scratch[dstn].ap()
        for c in range(CH):
            for b in range(NBLK):
                t0 = b * TB
                ls = em.batch(ldsem)
                i = em.sy().dma_start(out=x_sb, in_=srch[ts(c), t0:t0 + TB])
                ldv = em.inc(i, ls, 16)
                em.wait("scalar", ls, ldv)
                i = em.sc().activation(sq_sb, x_sb, AF.Square)
                av = em.inc(i, "act")
                em.wait("tensor", "act", av)
                bk = banks.use(0 if wi == 0 else 2)
                i = em.te().matmul(B.PS[bk][:, 0:TB], bdiag, sq_sb, start=True, stop=True)
                mv = em.inc(i, "mm")
                em.wait("scalar", "mm", mv)
                i = em.sc().activation(n_sb, B.PS[bk][:, 0:TB], AF.Sqrt)
                av = em.inc(i, "act")
                banks.done(bk, "act", av)
                em.wait("vector", "act", av)
                i = em.ve().tensor_scalar_max(n_sb, n_sb, sca(B, "eps12"))
                em.inc(i, "vec")
                em.ve().drain(fusable=False)
                i = em.ve().reciprocal(n_sb, n_sb)
                em.inc(i, "vec")
                em.ve().drain(fusable=False)
                em.wait("vector", ls, ldv)
                i = em.ve().tensor_mul(qn_sb, x_sb, n_sb)
                em.inc(i, "vec")
                em.ve().drain(fusable=False)
                i = em.ve().tensor_scalar(qn_sb, qn_sb, sc_sb[:, wi:wi + 1], None,
                                          ALU.mult)
                vv = em.inc(i, "vec")
                em.ve().drain(fusable=False)
                em.wait("tensor", "vec", vv)
                bk = banks.use(1 if wi == 0 else 3)
                i = em.te().matmul(B.PS[bk][:, 0:TB], rotp, qn_sb, start=True, stop=True)
                mv = em.inc(i, "mm")
                i = em.ve().tensor_mul(qa_sb, qn_sb, con(B, cs_n)[:, t0:t0 + TB])
                em.inc(i, "vec")
                em.ve().drain(fusable=False)
                em.wait("vector", "mm", mv)
                i = em.ve().tensor_mul(qo_sb, B.PS[bk][:, 0:TB],
                                       con(B, ss_n)[:, t0:t0 + TB])
                vv = em.inc(i, "vec")
                em.ve().drain(fusable=False)
                banks.done(bk, "vec", vv)
                i = em.ve().tensor_add(qo_sb, qo_sb, qa_sb)
                vv = em.inc(i, "vec")
                em.ve().drain(fusable=False)
                em.wait("gpsimd", "vec", vv)
                i = em.gp().dma_start(out=dsth[ts(c), t0:t0 + TB], in_=qo_sb)
                sv = em.inc(i, "st1", 16)
                em.wait("vector", "st1", sv)
                em.wait("sync", "vec", vv)
                em.wait("scalar", "vec", vv)
    em.wait("gpsimd", "st1", em.c["st1"])
    em.barrier(f"rotnorm{li}")


def phase_local_attn(em, B, li, ldsem="ld4"):
    QR = B.scratch["QR"].ap()
    KR = B.scratch["KR"].ap()
    OA2 = B.scratch["OA2"].ap()
    NT = T // 128
    trilI = con(B, "trilI")
    triuI = con(B, "triuI")
    onesT = B.CON[:, CON_OFF["ones"]:CON_OFF["ones"] + 128]

    q_sb = B.AR2[0:64, 0:128]
    k_sb = B.AR2[0:64, 128:512]
    v_sb = lambda sb: B.AR2[:, 512 + sb * 64: 512 + (sb + 1) * 64]
    att_sb = lambda sb: B.AR2[:, 768 + sb * 128: 768 + (sb + 1) * 128]
    den_sb = B.AR2[:, 1152 + 0: 1152 + 2048]
    oa_sb = B.AR2[:, 3200: 3200 + 2048]

    banks = Banks(em, range(8))
    prevq = None
    for qt in range(NT):
        q0 = qt * 128
        nsb = min(qt, 2) + 1
        sb0 = 2 - (nsb - 1)
        if prevq is not None:
            for sem, val in prevq:
                em.wait("sync", sem, val)
        lasth = 0
        for h in range(H):
            hr = h * 64
            em.wait("sync", "mm", lasth)
            ls = em.batch(ldsem)
            i = em.sy().dma_start(out=q_sb, in_=QR[hr:hr + 64, q0:q0 + 128])
            em.inc(i, ls, 16)
            for j in range(nsb):
                s0 = q0 - 128 * (nsb - 1 - j)
                i = em.sy().dma_start(out=k_sb[:, j * 128:(j + 1) * 128],
                                      in_=KR[hr:hr + 64, s0:s0 + 128])
                em.inc(i, ls, 16)
                i = em.sy().dma_start(
                    out=v_sb(j),
                    in_=mkAP(B.scratch["V65"], s0 * (16 * 65) + h * 65,
                             [[16 * 65, 128], [1, 64]]))
                em.inc(i, ls, 16)
            ldv = em.c[ls]
            em.wait("tensor", ls, ldv)
            bk_s = banks.use(6 + h % 2)
            for j in range(nsb):
                i = em.te().matmul(B.PS[bk_s][:, j * 128:(j + 1) * 128],
                                   k_sb[:, j * 128:(j + 1) * 128], q_sb,
                                   start=True, stop=True)
                mv = em.inc(i, "mm")
            em.wait("scalar", "mm", mv)
            for j in range(nsb):
                i = em.sc().activation(att_sb(j), B.PS[bk_s][:, j * 128:(j + 1) * 128],
                                       AF.Exp, bias=sca(B, "negB"), scale=8.0)
                if i is not None:
                    i.annotate(f"exp q{qt} h{h} j{j}")
                av = em.inc(i, "act")
                em.sc().drain(fusable=False)
            banks.done(bk_s, "act", av)
            em.wait("vector", "act", av)
            for j in range(nsb):
                sb_abs = sb0 + j
                if sb_abs == 0:
                    i = em.ve().tensor_mul(att_sb(j), att_sb(j), trilI)
                    em.inc(i, "vec")
                    em.ve().drain(fusable=False)
                elif sb_abs == 2:
                    i = em.ve().tensor_mul(att_sb(j), att_sb(j), triuI)
                    em.inc(i, "vec")
                    em.ve().drain(fusable=False)
            vv = em.c["vec"]
            em.wait("tensor", "vec", vv)
            bk_a = banks.use(h // 4)
            for j in range(nsb):
                i = em.te().matmul(B.PS[bk_a][0:64, (h % 4) * 128:(h % 4 + 1) * 128],
                                   v_sb(j), att_sb(j),
                                   start=(j == 0), stop=(j == nsb - 1))
                mv = em.inc(i, "mm")
            bk_d = banks.use(4 + h % 2)
            for j in range(nsb):
                i = em.te().matmul(B.PS[bk_d][:, 0:128], onesT, att_sb(j),
                                   start=(j == 0), stop=(j == nsb - 1))
                if i is not None:
                    i.annotate(f"den q{qt} h{h} j{j}")
                mv = em.inc(i, "mm")
            em.wait("vector", "mm", mv)
            i = em.ve().reciprocal(den_sb[:, h * 128:(h + 1) * 128], B.PS[bk_d][:, 0:128])
            dv = em.inc(i, "vec")
            em.ve().drain(fusable=False)
            banks.done(bk_d, "vec", dv)
            if "DBGA" in B.scratch and qt == 6 and h == 0:
                i = em.ve().tensor_copy(B.AR2[0:64, 5376:5504], B.PS[0][0:64, 0:128])
                dvv = em.inc(i, "vec")
                em.ve().drain(fusable=False)
                em.wait("gpsimd", "vec", dvv)
                i = em.gp().dma_start(out=B.scratch["DBGA3"].ap()[0:64, :],
                                      in_=B.AR2[0:64, 5376:5504])
                em.inc(i, "st2", 16)
                em.wait("gpsimd", "vec", dv)
                em.wait("gpsimd", "act", em.c["act"])
                em.wait("gpsimd", ldsem, em.c[ldsem])
                for j in range(nsb):
                    i = em.gp().dma_start(
                        out=B.scratch["DBGA"].ap()[:, j * 128:(j + 1) * 128],
                        in_=att_sb(j))
                    em.inc(i, "st2", 16)
                i = em.gp().dma_start(out=B.scratch["DBGA"].ap()[:, 384:512],
                                      in_=den_sb[:, 0:128])
                em.inc(i, "st2", 16)
                for j in range(nsb):
                    i = em.gp().dma_start(
                        out=B.scratch["DBGA"].ap()[:, 512 + j * 64: 512 + (j + 1) * 64],
                        in_=v_sb(j))
                    em.inc(i, "st2", 16)
                em.wait("vector", "st2", em.c["st2"])
                em.wait("sync", "st2", em.c["st2"])
            lasth = em.c["mm"]
        em.wait("vector", "mm", em.c["mm"])
        for h in range(H):
            if "DBGA" in B.scratch and qt == 6 and h == 0:
                i = em.ve().tensor_copy(B.AR2[0:64, 5248:5376],
                                        B.PS[0][0:64, 0:128])
                dvv = em.inc(i, "vec")
                em.ve().drain(fusable=False)
                em.wait("gpsimd", "vec", dvv)
                i = em.gp().dma_start(out=B.scratch["DBGA2"].ap()[0:64, :],
                                      in_=B.AR2[0:64, 5248:5376])
                em.inc(i, "st2", 16)
            i = em.ve().tensor_mul(oa_sb[0:64, h * 128:(h + 1) * 128],
                                   B.PS[h // 4][0:64, (h % 4) * 128:(h % 4 + 1) * 128],
                                   den_sb[0:64, h * 128:(h + 1) * 128])
            vv = em.inc(i, "vec")
            em.ve().drain(fusable=False)
            em.wait("gpsimd", "vec", vv)
            i = em.gp().dma_start(out=OA2[h * 64:(h + 1) * 64, q0:q0 + 128],
                                  in_=oa_sb[0:64, h * 128:(h + 1) * 128])
            sv = em.inc(i, "st1", 16)
        for bb in range(4):
            banks.done(bb, "vec", em.c["vec"])
        em.wait("vector", "st1", em.c["st1"])
        prevq = [("mm", em.c["mm"]), ("act", em.c["act"]), ("vec", em.c["vec"])]
    em.wait("gpsimd", "st1", em.c["st1"])
    em.barrier(f"latt{li}")


def phase_geglu(em, B, li, src, resid, dst, ldsem="ld6"):
    TBF = 256
    W1h = B.inputs["FF_W1"]
    b1h = B.inputs["FF_b1"]
    W2h = B.inputs["FF_W2"]
    b2h = B.inputs["FF_b2"]
    NJ = (2730 + 127) // 128
    jws = [min(128, 2730 - j * 128) for j in range(NJ)]

    w1a_t = lambda par, c: B.AR1[:, par * 2048 + c * 128: par * 2048 + (c + 1) * 128]
    w1g_t = lambda par, c: B.AR1[:, 4096 + par * 2048 + c * 128: 4096 + par * 2048 + (c + 1) * 128]
    w2_t = lambda par: B.AR1[:, 8192 + par * D: 8192 + (par + 1) * D]
    b1a_sb = B.AR2[:, 12800 + 0: 12800 + 22]
    b1g_sb = B.AR2[:, 12822: 12844]
    b2_sb = B.AR2[:, 12844: 12852]
    x_t = lambda c: B.AR3[:, c * TBF: (c + 1) * TBF]
    res_t = lambda c: B.AR3[:, 2048 + c * TBF: 2048 + (c + 1) * TBF]
    g_sb = B.AR3[:, 4096: 4096 + TBF]
    y_t = lambda j: B.AR3[:, 4608 + j * TBF: 4608 + (j + 1) * TBF]
    o_t = B.AR3[:, 10240: 10240 + TBF]

    ls0 = em.batch(ldsem)
    for j in range(NJ):
        i = em.sy().dma_start(out=b1a_sb[0:jws[j], j:j + 1],
                              in_=mkAP(b1h, li * 5460 + j * 128, [[1, jws[j]], [1, 1]]))
        em.inc(i, ls0, 16)
        i = em.sy().dma_start(out=b1g_sb[0:jws[j], j:j + 1],
                              in_=mkAP(b1h, li * 5460 + 2730 + j * 128, [[1, jws[j]], [1, 1]]))
        em.inc(i, ls0, 16)
    for c in range(CH):
        i = em.sy().dma_start(out=b2_sb[:, c:c + 1],
                              in_=mkAP(b2h, li * D + c * 128, [[1, 128], [1, 1]]))
        em.inc(i, ls0, 16)
    ldb = em.c[ls0]
    em.wait("scalar", ls0, ldb)
    em.wait("vector", ls0, ldb)
    banks = Banks(em, range(8))
    lastw = [0, 0]
    for b in range(T // TBF):
        t0 = b * TBF
        lsx = em.batch(ldsem)
        for c in range(CH):
            i = em.sy().dma_start(out=x_t(c), in_=src[ts(c), t0:t0 + TBF])
            em.inc(i, lsx, 16)
            i = em.sy().dma_start(out=res_t(c), in_=resid[ts(c), t0:t0 + TBF])
            em.inc(i, lsx, 16)
        ldv = em.c[lsx]
        em.wait("tensor", lsx, ldv)
        em.wait("vector", lsx, ldv)
        for j in range(NJ):
            par = j % 2
            wj = jws[j]
            em.wait("sync", "mm", lastw[par])
            ls = em.batch(ldsem)
            for c in range(CH):
                i = em.sy().dma_start(out=w1a_t(par, c)[:, 0:wj],
                                      in_=W1h[li, ts(c), j * 128: j * 128 + wj])
                em.inc(i, ls, 16)
                i = em.sy().dma_start(out=w1g_t(par, c)[:, 0:wj],
                                      in_=W1h[li, ts(c), 2730 + j * 128: 2730 + j * 128 + wj])
                em.inc(i, ls, 16)
            i = em.sy().dma_start(out=w2_t(par)[0:wj, :],
                                  in_=W2h[li, j * 128: j * 128 + wj, :])
            em.inc(i, ls, 16)
            ldv = em.c[ls]
            em.wait("tensor", ls, ldv)
            bk_a = banks.use(4)
            for c in range(CH):
                i = em.te().matmul(B.PS[bk_a][0:wj, 0:TBF], w1a_t(par, c)[:, 0:wj],
                                   x_t(c), start=(c == 0), stop=(c == CH - 1))
                mmv = em.inc(i, "mm")
            bk_g = banks.use(5)
            for c in range(CH):
                i = em.te().matmul(B.PS[bk_g][0:wj, 0:TBF], w1g_t(par, c)[:, 0:wj],
                                   x_t(c), start=(c == 0), stop=(c == CH - 1))
                mmv = em.inc(i, "mm")
            em.wait("scalar", "mm", mmv)
            i = em.sc().activation(g_sb[0:wj, :], B.PS[bk_g][0:wj, 0:TBF], AF.Gelu,
                                   bias=b1g_sb[0:wj, j:j + 1])
            av = em.inc(i, "act")
            banks.done(bk_g, "act", av)
            em.wait("vector", "act", av)
            i = em.ve().scalar_tensor_tensor(y_t(j)[0:wj, :], B.PS[bk_a][0:wj, 0:TBF],
                                             b1a_sb[0:wj, j:j + 1], g_sb[0:wj, :],
                                             ALU.add, ALU.mult)
            vv = em.inc(i, "vec")
            em.ve().drain(fusable=False)
            banks.done(bk_a, "vec", vv)
            em.wait("tensor", "vec", vv)
            for oc in range(CH):
                bk_o = oc // 2
                if j == 0 and oc % 2 == 0:
                    banks.use(bk_o)
                i = em.te().matmul(
                    B.PS[bk_o][:, (oc % 2) * TBF: (oc % 2 + 1) * TBF],
                    w2_t(par)[0:wj, ts(oc)], y_t(j)[0:wj, :],
                    start=(j == 0 and oc % 2 == 0),
                    stop=(j == NJ - 1 and oc % 2 == 1))
                mmv = em.inc(i, "mm")
            em.wait("scalar", "vec", vv)
            lastw[par] = em.c["mm"]
        em.wait("vector", "mm", em.c["mm"])
        for oc in range(CH):
            bk_o = oc // 2
            i = em.ve().scalar_tensor_tensor(
                o_t, B.PS[bk_o][:, (oc % 2) * TBF: (oc % 2 + 1) * TBF],
                b2_sb[:, oc:oc + 1], res_t(oc), ALU.add, ALU.add)
            vv = em.inc(i, "vec")
            em.ve().drain(fusable=False)
            if oc % 2 == 1:
                banks.done(bk_o, "vec", vv)
            em.wait("gpsimd", "vec", vv)
            i = em.gp().dma_start(out=dst[ts(oc), t0:t0 + TBF], in_=o_t)
            sv = em.inc(i, "st1", 16)
            em.wait("vector", "st1", sv)
        em.wait("sync", "mm", em.c["mm"])
        em.wait("sync", "vec", em.c["vec"])
    em.wait("gpsimd", "st1", em.c["st1"])
    em.barrier(f"geglu{li}")


def phase_add(em, B, asrc, bsrc, dst, ldsem="ld1"):
    a_t = B.AR3[:, 0:T]
    b_t = B.AR3[:, 2048:2048 + T]
    o_t = B.AR3[:, 4096:4096 + T]
    for c in range(CH):
        ls = em.batch(ldsem)
        i = em.sy().dma_start(out=a_t, in_=asrc[ts(c), :])
        em.inc(i, ls, 16)
        i = em.sy().dma_start(out=b_t, in_=bsrc[ts(c), :])
        em.inc(i, ls, 16)
        ldv = em.c[ls]
        em.wait("vector", ls, ldv)
        i = em.ve().tensor_add(o_t, a_t, b_t)
        vv = em.inc(i, "vec")
        em.ve().drain(fusable=False)
        em.wait("gpsimd", "vec", vv)
        i = em.gp().dma_start(out=dst[ts(c), :], in_=o_t)
        sv = em.inc(i, "st0", 16)
        em.wait("vector", "st0", sv)
        em.wait("sync", "vec", vv)
    em.barrier("addout")




def phase_add(em, B, asrc, bsrc, dst, ldsem="ld1"):
    a_t = B.AR3[:, 0:T]
    b_t = B.AR3[:, 2048:2048 + T]
    o_t = B.AR3[:, 4096:4096 + T]
    for c in range(CH):
        ls = em.batch(ldsem)
        i = em.sy().dma_start(out=a_t, in_=asrc[ts(c), :])
        em.inc(i, ls, 16)
        i = em.sy().dma_start(out=b_t, in_=bsrc[ts(c), :])
        em.inc(i, ls, 16)
        ldv = em.c[ls]
        em.wait("vector", ls, ldv)
        i = em.ve().tensor_add(o_t, a_t, b_t)
        vv = em.inc(i, "vec")
        em.ve().drain(fusable=False)
        em.wait("gpsimd", "vec", vv)
        i = em.gp().dma_start(out=dst[ts(c), :], in_=o_t)
        sv = em.inc(i, "st0", 16)
        em.wait("vector", "st0", sv)
        em.wait("sync", "vec", vv)
    em.barrier("addout")


def emit_all(em, B, phases):
    load_constants(em, B)
    for ph in phases:
        ph(em, B)


def declare_scratch(B):
    for n in ("XN1", "RR", "KK", "VV", "GATE", "ICLR", "SIGT",
              "KN", "BVEC", "RK", "DOUT", "OIN", "XR1", "XN2", "XR2",
              "XN3", "XR3", "XN4", "QF", "KF", "OATT"):
        B.scr(n, (D, T))
    B.scr("VTM", (T, D))
    B.scr("W5", (16, 4096))
    for n in ("QF2", "KF2", "QR", "KR", "OA2", "XNF", "XRA", "XRB"):
        B.scr(n, (D, T))
    B.scr("V65", (T, 16 * 65))
    B.scr("DBGA", (128, 768))
    B.scr("DBGA2", (128, 128))
    B.scr("DBGA3", (128, 128))
    for h in range(H):
        B.scr(f"WSH{h}", (128 * 4226,))


def default_phases(B, upto=None):
    ph = [
        ("ln1", lambda em, B: phase_layernorm(
            em, B, B.inputs["xin"].ap(), B.scratch["XN1"].ap(),
            sca(B, "eps_ln"), "ld1", "1")),
        ("mlp_r", lambda em, B: phase_l0_mlp(
            em, B, B.scratch["XN1"].ap(), B.scratch["RR"].ap(),
            "L0_r", "L0_mu_r", "ld2", "r")),
        ("mlp_k", lambda em, B: phase_l0_mlp(
            em, B, B.scratch["XN1"].ap(), B.scratch["KK"].ap(),
            "L0_k", "L0_mu_k", "ld2", "k")),
        ("mlp_v", lambda em, B: phase_l0_mlp(
            em, B, B.scratch["XN1"].ap(), B.scratch["VV"].ap(),
            "L0_v", "L0_mu_v", "ld2", "v")),
        ("gates", lambda em, B: phase_l0_gates(
            em, B, B.scratch["XN1"].ap(), "ld3")),
        ("prep", lambda em, B: phase_l0_prep(em, B, "ld4")),
        ("deltanet", lambda em, B: phase_deltanet(em, B, "ld5")),
        ("combine", lambda em, B: phase_l0_combine(em, B, "ld6")),
        ("mlp_o", lambda em, B: phase_l0_mlp(
            em, B, B.scratch["OIN"].ap(), B.scratch["XR1"].ap(),
            "L0_o", None, "ld2", "o", E, B.inputs["xin"].ap())),
        ("ln2", lambda em, B: phase_layernorm(
            em, B, B.scratch["XR1"].ap(), B.scratch["XN2"].ap(),
            sca(B, "eps_ln"), "ld1", "2")),
        ("l0ffn", lambda em, B: phase_l0ffn(em, B, "ld7")),
        ("ln3", lambda em, B: phase_layernorm(
            em, B, B.scratch["XR2"].ap(), B.scratch["XN3"].ap(),
            sca(B, "eps_ln"), "ld1", "3")),
        ("l1pre", lambda em, B: phase_l1attn_pre(em, B, "ld8")),
        ("l1core", lambda em, B: phase_l1attn_core(em, B, "ld9")),
        ("l1wo", lambda em, B: phase_proj(
            em, B, B.scratch["OATT"].ap(), B.scratch["XR3"].ap(),
            "A1_Wo", "ld2", "wo", B.scratch["XR2"].ap())),
    ]
    ph += [("ln4", lambda em, B: phase_layernorm(
        em, B, B.scratch["XR3"].ap(), B.scratch["XN4"].ap(),
        sca(B, "eps_ln"), "ld1", "4"))]
    pool = ["XN4", "XRA", "XRB"]
    for li in range(6):
        cur = pool[(2 * li) % 3]
        mid = pool[(2 * li + 1) % 3]
        nxt = pool[(2 * li + 2) % 3]
        last = (li == 5)

        def mk(li, cur, mid, nxt, last):
            def f_qkv(em, B):
                phase_ffqkv(em, B, li, B.scratch[cur].ap(), "ld2")

            def f_rot(em, B):
                phase_rotnorm(em, B, li, "ld3")

            def f_latt(em, B):
                phase_local_attn(em, B, li, "ld4")

            def f_wo(em, B):
                phase_proj(em, B, B.scratch["OA2"].ap(), B.scratch[mid].ap(),
                           "FF_Wo", "ld2", f"ffwo{li}", B.scratch[cur].ap(), li=li)

            def f_ln(em, B):
                phase_layernorm(em, B, B.scratch[mid].ap(), B.scratch["XNF"].ap(),
                                sca(B, "eps_ln"), "ld1", f"f{li}", shift=True,
                                lng=(B.inputs["FF_lng"], li * D),
                                lnb=(B.inputs["FF_lnb"], li * D))

            def f_geglu(em, B):
                phase_geglu(em, B, li, B.scratch["XNF"].ap(),
                            B.scratch[mid].ap(), B.scratch[nxt].ap(), "ld6")
            return f_qkv, f_rot, f_latt, f_wo, f_ln, f_geglu

        fs = mk(li, cur, mid, nxt, last)
        ph += [(f"ffqkv{li}", fs[0]), (f"rot{li}", fs[1]), (f"latt{li}", fs[2]),
               (f"ffwo{li}", fs[3]), (f"ffln{li}", fs[4]), (f"geglu{li}", fs[5])]
    ph += [("addout", lambda em, B: phase_add(
        em, B, B.scratch["XN4"].ap(), B.scratch["XR3"].ap(),
        B.outputs["xout"].ap(), "ld1"))]
    if upto is not None:
        names = [p[0] for p in ph]
        ph = ph[:names.index(upto) + 1]
    return [p[1] for p in ph]


def build_nc(dbg_taps=(), upto=None, final=None):
    nc = bass.Bass("TRN2", target_bir_lowering=False)
    B = Builder(nc, dbg_taps)
    declare_io(B)
    alloc_sbuf(B)
    declare_scratch(B)
    phase_list = default_phases(B, upto)
    if upto is not None:
        fin = final or "XN1"
        phase_list = phase_list + [lambda em, B: phase_copyout(em, B, fin)]

    sems = {}
    semnames = (["ph", "mm", "act", "vec", "gpc"]
                + [f"ld{i}{s}" for i in range(10) for s in ("", "b", "c", "d")]
                + [f"st{i}{s}" for i in range(4) for s in ("", "b", "c", "d")])
    with _multi(nc, semnames, sems):
        with nc.Block() as blk:
            def run(role, eng):
                em = EM(nc, role, eng, sems)
                emit_all(em, B, phase_list)

            @blk.sync
            def _(eng):
                run("sync", eng)

            @blk.tensor
            def _(eng):
                run("tensor", eng)

            @blk.scalar
            def _(eng):
                run("scalar", eng)

            @blk.vector
            def _(eng):
                run("vector", eng)

            @blk.gpsimd
            def _(eng):
                run("gpsimd", eng)
    return nc, B


def phase_copyout(em, B, name="XN1"):
    src = B.scratch[name].ap()
    dst = B.outputs["xout"].ap()
    for c in range(CH):
        t = B.AR3[:, 0:T]
        ls = em.batch("ld2")
        i = em.sy().dma_start(out=t, in_=src[c * 128:(c + 1) * 128, :])
        lv = em.inc(i, ls, 16)
        em.wait("gpsimd", ls, lv)
        i = em.gp().dma_start(out=dst[c * 128:(c + 1) * 128, :], in_=t)
        sv = em.inc(i, "st1", 16)
        em.wait("sync", "st1", sv)
    em.wait("gpsimd", "st1", em.c["st1"])
    em.barrier("out")


from contextlib import ExitStack, contextmanager


@contextmanager
def _multi(nc, names, out):
    with ExitStack() as st:
        for n in names:
            out[n] = st.enter_context(nc.semaphore(n))
        yield


def make_in_map(core_inputs, weights):
    m = dict(weights)
    m.update(core_inputs)
    return m


def host_weights(I):
    w = {}
    for p in ("r", "k", "v", "o"):
        w[f"L0_{p}W1"] = I[f"L0_attn_{p}W1"]
        w[f"L0_{p}b1"] = I[f"L0_attn_{p}b1"]
        w[f"L0_{p}W2"] = I[f"L0_attn_{p}W2"]
        w[f"L0_{p}b2"] = I[f"L0_attn_{p}b2"]
    for p in ("mu_r", "mu_k", "mu_v", "mu_g", "mu_a", "mu_d", "db", "ab",
              "rem_mult", "iclr_mix", "bonus_mult", "ln_g", "ln_b"):
        w[f"L0_{p}"] = I[f"L0_attn_{p}"]
    for p in ("dA", "aA", "gA", "dB", "aB", "gB"):
        w[f"L0_{p}"] = I[f"L0_attn_{p}"]
    w["F0_Win"] = I["L0_ffn_Win"]
    w["F0_bin"] = I["L0_ffn_bin_"]
    for p in ("W1", "b1", "W2", "b2", "Wout", "bout"):
        w[f"F0_{p}"] = I[f"L0_ffn_{p}"]
    for p in ("Wq", "Wk", "Wv", "Wo"):
        w[f"A1_{p}"] = I[f"L1_attn_{p}"]
    w["A1_table"] = I["L1_attn_rel_bias_table"]
    w["FF_Wqkv"] = I["L1_ffn_Wqkv"]
    w["FF_qs"] = I["L1_ffn_q_scale"]
    w["FF_ks"] = I["L1_ffn_k_scale"]
    w["FF_Wo"] = I["L1_ffn_Wo"]
    w["FF_lng"] = I["L1_ffn_ln_g"]
    w["FF_lnb"] = I["L1_ffn_ln_b"]
    w["FF_W1"] = I["L1_ffn_W1"]
    w["FF_b1"] = I["L1_ffn_b1"]
    w["FF_W2"] = I["L1_ffn_W2"]
    w["FF_b2"] = I["L1_ffn_b2"]
    C = host_constants()
    for k, v in C.items():
        if "c_" + k in ("c_zeros64",):
            continue
        w["c_" + k] = v
    w.pop("c_zeros64", None)
    return w


def kernel(**inputs):
    X = np.asarray(inputs["L0_attn_queries"], dtype=np.float32)
    Bn = X.shape[0]
    nc, B = build_nc()
    w = host_weights(inputs)
    w = {k: np.ascontiguousarray(np.asarray(v, dtype=np.float32)) for k, v in w.items()}
    in_maps = []
    for b in range(Bn):
        m = dict(w)
        m["xin"] = np.ascontiguousarray(X[b].T)
        in_maps.append(m)
    res = run_bass_kernel_spmd(nc, in_maps, core_ids=list(range(8)))
    out = np.stack([np.ascontiguousarray(r["xout"].T) for r in res.results])
    return out.astype(np.float32)


if __name__ == "__main__":
    pass


# version 02
# speedup vs baseline: 1.1658x; 1.1658x over previous
import math
import numpy as np
import concourse.bass as bass
import concourse.mybir as mybir
from concourse.bass_utils import run_bass_kernel_spmd
from concourse.ap import AP as mkAP

F32 = mybir.dt.float32
AF = mybir.ActivationFunctionType
ALU = mybir.AluOpType
AX = mybir.AxisListType

T = 2048
D = 1024
H = 16
HD = 64
E = 1228
LRK = 128
TB = 512
NBLK = T // TB
CH = D // 128
DECAY_C = 0.7408182206817179

ROLES = ("sync", "tensor", "scalar", "vector", "gpsimd")


class EM:

    def __init__(self, nc, role, eng, sems):
        self.nc = nc
        self.role = role
        self.eng = eng
        self.S = sems
        self.c = {k: 0 for k in sems}
        self.bi = {}
        self.lastw = {}

    def bump(self, sem, n=1):
        self.c[sem] += n
        return self.c[sem]

    def inc(self, instr, sem, n=1):
        if instr is not None:
            instr.then_inc(self.S[sem], n)
        return self.bump(sem, n)

    def wait(self, role, sem, val):
        if self.role == role:
            last = self.lastw.get(sem, -1)
            if val > last:
                self.eng.wait_ge(self.S[sem], val)
                self.lastw[sem] = val

    def wait_cur(self, role, sem):
        self.wait(role, sem, self.c[sem])

    def ldn(self, base, i):
        return base + ("", "b", "c", "d")[i % 4]

    def batch(self, base):
        self.bi.setdefault(base, -1)
        self.bi[base] += 1
        return self.ldn(base, self.bi[base])

    def flush(self, role, bases):
        for b in bases:
            for suf in ("", "b", "c", "d"):
                nm = b + suf
                if nm in self.S and self.c[nm] > 0:
                    self.wait(role, nm, self.c[nm])

    def sy(self):
        return self.eng if self.role == "sync" else _Null
    def te(self):
        return self.eng if self.role == "tensor" else _Null
    def sc(self):
        return self.eng if self.role == "scalar" else _Null
    def ve(self):
        return self.eng if self.role == "vector" else _Null
    def gp(self):
        return self.eng if self.role == "gpsimd" else _Null

    def barrier(self, tag):
        self.flush("sync", [f"ld{i}" for i in range(10)])
        self.flush("gpsimd", [f"st{i}" for i in range(4)])
        self.eng.sem_inc(self.S["ph"], 1)
        val = self.bump("ph", 5)
        self.eng.wait_ge(self.S["ph"], val)


class _NullEngine:
    def __getattr__(self, name):
        def f(*a, **k):
            return None
        return f


_Null = _NullEngine()


def host_constants():
    C = {}
    C["ident"] = np.eye(128, dtype=np.float32)
    r = np.arange(128)
    C["trilS"] = (r[None, :] < r[:, None]).astype(np.float32)
    C["triuS"] = (r[:, None] < r[None, :]).astype(np.float32)
    C["triuI"] = (r[:, None] <= r[None, :]).astype(np.float32)
    C["ntriuI"] = -C["triuI"]
    bd = np.zeros((128, 128), dtype=np.float32)
    bd[:64, :64] = 1.0
    bd[64:, 64:] = 1.0
    C["bdiag64"] = bd
    P = np.zeros((128, 128), dtype=np.float32)
    for blk in range(2):
        b = 64 * blk
        for i in range(32):
            P[b + 32 + i, b + i] = -1.0
            P[b + i, b + 32 + i] = 1.0
    C["rotperm"] = P
    half = np.arange(0, HD, 2, dtype=np.float32)
    inv_freq = np.float32(1.0) / np.float32(10000.0) ** (half / np.float32(HD))
    t = np.arange(T, dtype=np.float32)
    freqs = t[:, None] * inv_freq[None, :]
    freqs = np.concatenate([freqs, freqs], -1)
    base_scale = (half + np.float32(0.4) * HD) / np.float32(1.4 * HD)
    power = (t - T // 2) / np.float32(128)
    scale = base_scale[None, :] ** power[:, None]
    scale = np.concatenate([scale, scale], -1)
    cos = np.cos(freqs); sin = np.sin(freqs)
    inv = np.float32(1.0) / scale
    def fm(a):
        a = np.ascontiguousarray(a.T.astype(np.float32))
        return np.concatenate([a, a], 0)
    C["csq"] = fm(cos * scale)
    C["ssq"] = fm(sin * scale)
    C["csk"] = fm(cos * inv)
    C["ssk"] = fm(sin * inv)
    NUM_BUCKETS, MAX_DIST = 32, 128
    nb = NUM_BUCKETS // 2
    d = np.arange(-2047, 2048)
    rel_buckets = (d > 0).astype(np.int64) * nb
    ad = np.abs(d)
    max_exact = nb // 2
    is_small = ad < max_exact
    rp_large = max_exact + (
        np.log(np.maximum(ad, 1).astype(np.float32) / max_exact)
        / math.log(MAX_DIST / max_exact) * (nb - max_exact)
    ).astype(np.int32)
    rp_large = np.minimum(rp_large, nb - 1)
    bucket = rel_buckets + np.where(is_small, ad, rp_large)
    oh = np.zeros((4095, 32), dtype=np.float32)
    oh[np.arange(4095), bucket] = 1.0
    ohp = np.zeros((4096, 32), dtype=np.float32)
    ohp[:4095] = oh
    C["t5oneh"] = np.ascontiguousarray(ohp.T)
    masks = []
    for off in (-256, -128, 0):
        s_idx = off + np.arange(128)[:, None]
        t_idx = np.arange(128)[None, :]
        dd = t_idx - s_idx
        masks.append(((dd >= 0) & (dd <= 256)).astype(np.float32))
    C["lmask"] = np.concatenate(masks, axis=1)
    C["zeros64"] = np.zeros((128, 128), dtype=np.float32)
    rp = np.zeros((16, 8 * 128), dtype=np.float32)
    for cc in range(8):
        for mm_ in range(128):
            rp[2 * cc + mm_ // 64, cc * 128 + mm_] = 1.0
    C["repl16"] = rp
    r2 = np.arange(128)
    C["trilI"] = (r2[None, :] <= r2[:, None]).astype(np.float32)
    return C


class Builder:
    def __init__(self, nc, dbg_taps=()):
        self.nc = nc
        self.dbg_taps = set(dbg_taps)
        self.inputs = {}
        self.scratch = {}
        self.outputs = {}
        self.sb = {}
        self.psum = []

    def inp(self, name, shape):
        h = self.nc.dram_tensor(name, list(shape), F32, kind="ExternalInput")
        self.inputs[name] = h
        return h

    def scr(self, name, shape):
        kind = "ExternalOutput" if name in self.dbg_taps else "Internal"
        h = self.nc.dram_tensor(name, list(shape), F32, kind=kind)
        self.scratch[name] = h
        if kind == "ExternalOutput":
            self.outputs[name] = h
        return h

    def out(self, name, shape):
        h = self.nc.dram_tensor(name, list(shape), F32, kind="ExternalOutput")
        self.outputs[name] = h
        return h

    def tap(self, name, src_hbm):
        return name in self.dbg_taps


def declare_io(B, with_weights=True):
    nc = B.nc
    B.inp("xin", (D, T))
    B.out("xout", (D, T))
    if not with_weights:
        return
    for p in ("r", "k", "v", "o"):
        B.inp(f"L0_{p}W1", (D, E))
        B.inp(f"L0_{p}b1", (E,))
        B.inp(f"L0_{p}W2", (E, D))
        B.inp(f"L0_{p}b2", (D,))
    for p in ("mu_r", "mu_k", "mu_v", "mu_g", "mu_a", "mu_d",
              "db", "ab", "rem_mult", "iclr_mix", "bonus_mult",
              "ln_g", "ln_b"):
        B.inp(f"L0_{p}", (D,))
    for p in ("dA", "aA", "gA"):
        B.inp(f"L0_{p}", (D, LRK))
    for p in ("dB", "aB", "gB"):
        B.inp(f"L0_{p}", (LRK, D))
    B.inp("F0_Win", (D, 64))
    B.inp("F0_bin", (64,))
    B.inp("F0_W1", (64, 16))
    B.inp("F0_b1", (64, 16))
    B.inp("F0_W2", (64, 16))
    B.inp("F0_b2", (64,))
    B.inp("F0_Wout", (64, D))
    B.inp("F0_bout", (D,))
    for p in ("Wq", "Wk", "Wv", "Wo"):
        B.inp(f"A1_{p}", (D, D))
    B.inp("A1_table", (32, H))
    B.inp("FF_Wqkv", (6, D, 3 * D))
    B.inp("FF_qs", (6, HD))
    B.inp("FF_ks", (6, HD))
    B.inp("FF_Wo", (6, D, D))
    B.inp("FF_lng", (6, D))
    B.inp("FF_lnb", (6, D))
    B.inp("FF_W1T", (6, 22, 128, 3072))
    B.inp("FF_b1T", (6, 128, 44))
    B.inp("FF_b2T", (6, 128, 8))
    B.inp("c_ident", (128, 128))
    B.inp("c_trilS", (128, 128))
    B.inp("c_triuS", (128, 128))
    B.inp("c_triuI", (128, 128))
    B.inp("c_ntriuI", (128, 128))
    B.inp("c_bdiag64", (128, 128))
    B.inp("c_rotperm", (128, 128))
    B.inp("c_csq", (128, T))
    B.inp("c_ssq", (128, T))
    B.inp("c_csk", (128, T))
    B.inp("c_ssk", (128, T))
    B.inp("c_t5oneh", (32, 4096))
    B.inp("c_lmask", (128, 384))
    B.inp("c_repl16", (16, 8 * 128))
    B.inp("c_trilI", (128, 128))


def alloc_sbuf(B):
    nc = B.nc
    B.AR1 = nc.alloc_sbuf_tensor("AR1", [128, 13000], F32)
    B.AR2 = nc.alloc_sbuf_tensor("AR2", [128, 13000], F32)
    B.AR3 = nc.alloc_sbuf_tensor("AR3", [128, 16400], F32)
    B.CON = nc.alloc_sbuf_tensor("CON", [128, 10800], F32)
    B.PS = [nc.alloc_psum_tensor(f"PS{i}", [128, 512], F32) for i in range(8)]


CON_OFF = {
    "ident": 0, "trilS": 128, "triuS": 256, "triuI": 384, "ntriuI": 512,
    "bdiag64": 640, "rotperm": 768, "lmask": 896,
    "csq": 1280, "ssq": 3328, "csk": 5376, "ssk": 7424,
    "sca": 9472,
    "ones": 9488,
    "repl16": 9616,
    "trilI": 10640,
}

SCA = {"eps_ln": 0, "eps_attln": 1, "negB": 2, "eps12": 3, "one": 4}
SCA_VAL = {"eps_ln": 1e-5, "eps_attln": 1e-6, "negB": -8.0, "eps12": 1e-12,
           "one": 1.0}


def sca(B, name):
    off = CON_OFF["sca"] + SCA[name]
    return B.CON[:, off:off + 1]


def con(B, name, rows=slice(0, 128), width=None):
    off = CON_OFF[name]
    if width is None:
        width = {"csq": T, "ssq": T, "csk": T, "ssk": T, "lmask": 384}.get(name, 128)
    return B.CON[rows, off:off + width]


def load_constants(em, B):
    names = ["ident", "trilS", "triuS", "triuI", "ntriuI", "bdiag64", "rotperm",
             "trilI"]
    for n in names:
        i = em.sy().dma_start(out=con(B, n), in_=B.inputs["c_" + n].ap())
        em.inc(i, "ld0", 16)
    i = em.sy().dma_start(out=con(B, "lmask"), in_=B.inputs["c_lmask"].ap())
    em.inc(i, "ld0", 16)
    i = em.sy().dma_start(out=B.CON[0:16, CON_OFF["repl16"]:CON_OFF["repl16"] + 1024],
                          in_=B.inputs["c_repl16"].ap())
    em.inc(i, "ld0", 16)
    for n in ("csq", "ssq", "csk", "ssk"):
        i = em.sy().dma_start(out=con(B, n), in_=B.inputs["c_" + n].ap())
        em.inc(i, "ld0", 16)
    for nm, val in SCA_VAL.items():
        i = em.ve().memset(sca(B, nm), float(val))
        em.inc(i, "vec")
    i = em.ve().memset(B.CON[:, CON_OFF["ones"]:CON_OFF["ones"] + 128], 1.0)
    em.inc(i, "vec")
    em.ve().drain(fusable=False)
    em.wait("sync", "ld0", em.c["ld0"])
    em.barrier("consts")


def phase_layernorm(em, B, src, dst, eps_ap=None, ldsem="ld1", tag="",
                    shift=False, lng=None, lnb=None):
    nc = B.nc
    onesT = B.CON[:, CON_OFF["ones"]:CON_OFF["ones"] + 128]
    aff_sb = B.AR3[:, 16000:16016]
    if lng is not None:
        gh, go = lng
        bh, bo = lnb
        ls0 = em.batch(ldsem)
        for c in range(CH):
            i = em.sy().dma_start(out=aff_sb[:, c:c + 1],
                                  in_=mkAP(gh, go + c * 128, [[1, 128], [1, 1]]))
            em.inc(i, ls0, 16)
            i = em.sy().dma_start(out=aff_sb[:, 8 + c: 9 + c],
                                  in_=mkAP(bh, bo + c * 128, [[1, 128], [1, 1]]))
            em.inc(i, ls0, 16)
        em.wait("vector", ls0, em.c[ls0])
    x_t = lambda c: B.AR3[:, c * TB: (c + 1) * TB]
    x2_t = B.AR3[:, 4096: 4096 + TB]
    xn_t = lambda c: B.AR3[:, 4608 + c * TB: 4608 + (c + 1) * TB]
    mu_t = B.AR3[:, 8704: 8704 + TB]
    iv_t = B.AR3[:, 9216: 9216 + TB]

    for b in range(NBLK):
        t0 = b * TB
        ls = em.batch(ldsem)
        for c in range(CH):
            if shift and c >= 4:
                if b == 0:
                    i = em.ve().memset(x_t(c)[:, 0:1], 0.0)
                    em.inc(i, "vec")
                    em.ve().drain(fusable=False)
                    i = em.sy().dma_start(out=x_t(c)[:, 1:TB],
                                          in_=src[c * 128:(c + 1) * 128, 0:TB - 1])
                else:
                    i = em.sy().dma_start(out=x_t(c),
                                          in_=src[c * 128:(c + 1) * 128, t0 - 1:t0 + TB - 1])
            else:
                i = em.sy().dma_start(out=x_t(c), in_=src[c * 128:(c + 1) * 128, t0:t0 + TB])
            em.inc(i, ls, 16)
        ldv = em.c[ls]
        em.wait("tensor", ls, ldv)
        em.wait("vector", ls, ldv)
        em.wait("scalar", ls, ldv)
        em.wait("tensor", "vec", em.c["vec"])
        em.wait("scalar", "vec", em.c["vec"])
        for c in range(CH):
            i = em.sc().activation(x2_t, x_t(c), AF.Square)
            sq = em.inc(i, "act")
            em.wait("tensor", "act", sq)
            i = em.te().matmul(B.PS[0][:, 0:TB], onesT, x_t(c),
                               start=(c == 0), stop=(c == CH - 1))
            em.inc(i, "mm")
            i = em.te().matmul(B.PS[1][:, 0:TB], onesT, x2_t,
                               start=(c == 0), stop=(c == CH - 1))
            mmv = em.inc(i, "mm")
            em.wait("scalar", "mm", mmv)
        mmv = em.c["mm"]
        em.wait("vector", "mm", mmv)
        i = em.ve().tensor_scalar(mu_t, B.PS[0][:, 0:TB], 1.0 / D, None, ALU.mult)
        em.inc(i, "vec")
        em.ve().drain(fusable=False)
        i = em.ve().tensor_tensor(iv_t, mu_t, mu_t, ALU.mult)
        em.inc(i, "vec")
        em.ve().drain(fusable=False)
        i = em.ve().scalar_tensor_tensor(iv_t, B.PS[1][:, 0:TB], 1.0 / D, iv_t,
                                         ALU.mult, ALU.subtract)
        vv = em.inc(i, "vec")
        em.wait("scalar", "vec", vv)
        i = em.sc().activation(iv_t, iv_t, AF.Sqrt, bias=eps_ap)
        sv = em.inc(i, "act")
        em.wait("vector", "act", sv)
        i = em.ve().reciprocal(iv_t, iv_t)
        em.inc(i, "vec")
        em.ve().drain(fusable=False)
        for c in range(CH):
            i = em.ve().tensor_sub(xn_t(c), x_t(c), mu_t)
            em.inc(i, "vec")
            em.ve().drain(fusable=False)
            i = em.ve().tensor_mul(xn_t(c), xn_t(c), iv_t)
            vv = em.inc(i, "vec")
            em.ve().drain(fusable=False)
            if lng is not None:
                i = em.ve().tensor_scalar(xn_t(c), xn_t(c), aff_sb[:, c:c + 1],
                                          aff_sb[:, 8 + c: 9 + c], ALU.mult, ALU.add)
                vv = em.inc(i, "vec")
                em.ve().drain(fusable=False)
            em.wait("gpsimd", "vec", vv)
            i = em.gp().dma_start(out=dst[c * 128:(c + 1) * 128, t0:t0 + TB],
                                  in_=xn_t(c))
            sv = em.inc(i, "st0", 16)
            em.wait("vector", "st0", sv)
        em.wait("sync", "vec", em.c["vec"])
        em.wait("sync", "mm", em.c["mm"])
    em.wait("gpsimd", "st0", em.c["st0"])
    em.barrier("ln" + tag)


class Banks:

    def __init__(self, em, idxs):
        self.em = em
        self.pend = {i: None for i in idxs}

    def use(self, i):
        if self.pend[i] is not None:
            sem, val = self.pend[i]
            self.em.wait("tensor", sem, val)
        return i

    def done(self, i, sem, val):
        self.pend[i] = (sem, val)



def col(h, c, rows=128):
    n = h.shape[0]
    return h.reshape([n, 1])[c * rows: c * rows + rows, :]


def colw(h, start, rows):
    n = h.shape[0]
    return h.reshape([n, 1])[start: start + rows, :]

def ts(i, n=128):
    return slice(i * n, (i + 1) * n)


def phase_l0_mlp(em, B, src, dst, wpfx, mu_name, ldsem, tag, eouter=E, resid=None):
    nc = B.nc
    W1 = B.inputs[wpfx + "W1"]
    b1 = B.inputs[wpfx + "b1"]
    W2 = B.inputs[wpfx + "W2"]
    b2 = B.inputs[wpfx + "b2"]
    mu = B.inputs[mu_name] if mu_name else None
    MC = (eouter + 127) // 128
    mws = [min(128, eouter - mc * 128) for mc in range(MC)]

    w1t = lambda c: B.AR1[:, c * eouter: (c + 1) * eouter]
    w2t = lambda mc: B.AR2[:, mc * D: (mc + 1) * D]
    mu_sb = B.AR2[:, 12800:12808]
    b2_sb = B.AR2[:, 12808:12816]
    mb_sb = B.AR2[:, 12816:12816 + MC]
    x_t = lambda c: B.AR3[:, c * TB: (c + 1) * TB]
    h_t = lambda mc: B.AR3[:, 4608 + mc * TB: 4608 + (mc + 1) * TB]
    o_t = B.AR3[:, 4608 + MC * TB: 4608 + MC * TB + TB]
    res_t = lambda c: B.AR3[:, 10240 + c * TB: 10240 + (c + 1) * TB]

    ls0 = em.batch(ldsem)
    for c in range(CH):
        i = em.sy().dma_start(out=w1t(c), in_=W1[ts(c), :])
        em.inc(i, ls0, 16)
    for mc in range(MC):
        i = em.sy().dma_start(out=w2t(mc)[0:mws[mc], :], in_=W2[mc * 128: mc * 128 + mws[mc], :])
        em.inc(i, ls0, 16)
    for c in range(CH):
        if mu is not None:
            i = em.sy().dma_start(out=mu_sb[:, c:c + 1], in_=col(mu, c))
            em.inc(i, ls0, 16)
        i = em.sy().dma_start(out=b2_sb[:, c:c + 1], in_=col(b2, c))
        em.inc(i, ls0, 16)
    for mc in range(MC):
        i = em.sy().dma_start(out=mb_sb[0:mws[mc], mc:mc + 1],
                              in_=colw(b1, mc * 128, mws[mc]))
        em.inc(i, ls0, 16)
    ldw = em.c[ls0]
    em.wait("tensor", ls0, ldw)
    em.wait("vector", ls0, ldw)
    banks = Banks(em, range(8))

    for mc in range(MC if mu is not None else 0):
        bk = banks.use(mc % 4)
        for c in range(CH):
            i = em.te().matmul(B.PS[bk][0:mws[mc], 0:1], w1t(c)[:, mc * 128: mc * 128 + mws[mc]],
                               mu_sb[:, c:c + 1], start=(c == 0), stop=(c == CH - 1))
            mmv = em.inc(i, "mm")
        em.wait("vector", "mm", mmv)
        i = em.ve().tensor_add(mb_sb[0:mws[mc], mc:mc + 1], mb_sb[0:mws[mc], mc:mc + 1],
                               B.PS[bk][0:mws[mc], 0:1])
        vv = em.inc(i, "vec")
        em.ve().drain(fusable=False)
        banks.done(bk, "vec", vv)
    em.wait("scalar", "vec", em.c["vec"])

    for b in range(NBLK):
        t0 = b * TB
        ls = em.batch(ldsem)
        for c in range(CH):
            i = em.sy().dma_start(out=x_t(c), in_=src[ts(c), t0:t0 + TB])
            em.inc(i, ls, 16)
        if resid is not None:
            for c in range(CH):
                i = em.sy().dma_start(out=res_t(c), in_=resid[ts(c), t0:t0 + TB])
                em.inc(i, ls, 16)
        ldv = em.c[ls]
        em.wait("tensor", ls, ldv)
        if resid is not None:
            em.wait("vector", ls, ldv)
        for mc in range(MC):
            bk = banks.use(mc % 4)
            for c in range(CH):
                i = em.te().matmul(B.PS[bk][0:mws[mc], 0:TB], w1t(c)[:, mc * 128: mc * 128 + mws[mc]],
                                   x_t(c), start=(c == 0), stop=(c == CH - 1))
                mmv = em.inc(i, "mm")
            em.wait("scalar", "mm", mmv)
            i = em.sc().activation(h_t(mc)[0:mws[mc], :], B.PS[bk][0:mws[mc], 0:TB],
                                   AF.Gelu, bias=mb_sb[0:mws[mc], mc:mc + 1])
            av = em.inc(i, "act")
            banks.done(bk, "act", av)
        em.wait("tensor", "act", em.c["act"])
        for oc in range(CH):
            bk = banks.use(4 + oc % 4)
            for mc in range(MC):
                i = em.te().matmul(B.PS[bk][:, 0:TB], w2t(mc)[0:mws[mc], ts(oc)],
                                   h_t(mc)[0:mws[mc], :], start=(mc == 0), stop=(mc == MC - 1))
                mmv = em.inc(i, "mm")
            em.wait("vector", "mm", mmv)
            if resid is not None:
                i = em.ve().scalar_tensor_tensor(o_t, B.PS[bk][:, 0:TB],
                                                 b2_sb[:, oc:oc + 1], res_t(oc),
                                                 ALU.add, ALU.add)
            else:
                i = em.ve().tensor_scalar(o_t, B.PS[bk][:, 0:TB], b2_sb[:, oc:oc + 1],
                                          None, ALU.add)
            vv = em.inc(i, "vec")
            em.ve().drain(fusable=False)
            banks.done(bk, "vec", vv)
            em.wait("gpsimd", "vec", vv)
            i = em.gp().dma_start(out=dst[ts(oc), t0:t0 + TB], in_=o_t)
            sv = em.inc(i, "st0", 16)
            em.wait("vector", "st0", sv)
        em.wait("sync", "mm", em.c["mm"])
        em.wait("sync", "vec", em.c["vec"])
    em.barrier("mlp" + tag)


def phase_l0_gates(em, B, src, ldsem="ld3"):
    nc = B.nc
    outs = {"g": B.scratch["GATE"].ap(), "a": B.scratch["ICLR"].ap(),
            "d": B.scratch["SIGT"].ap()}
    A_in = {"g": "L0_gA", "a": "L0_aA", "d": "L0_dA"}
    Bw_in = {"g": "L0_gB", "a": "L0_aB", "d": "L0_dB"}
    mu_in = {"g": "L0_mu_g", "a": "L0_mu_a", "d": "L0_mu_d"}
    b_in = {"g": None, "a": "L0_ab", "d": "L0_db"}

    aT = lambda c: B.AR1[:, c * 128: (c + 1) * 128]
    bT = B.AR1[:, 1024: 1024 + D]
    mu_sb = B.AR2[:, 12800:12808]
    ob_sb = B.AR2[:, 12808:12816]
    mb_sb = B.AR2[:, 12816:12817]
    x_t = lambda c: B.AR3[:, c * TB: (c + 1) * TB]
    inner_t = B.AR3[:, 4608: 4608 + TB]
    o_t = B.AR3[:, 5120: 5120 + TB]
    banks = Banks(em, range(8))

    for pi, p in enumerate(("g", "a", "d")):
        Aw = B.inputs[A_in[p]]
        Bw = B.inputs[Bw_in[p]]
        mu = B.inputs[mu_in[p]]
        ls0 = em.batch(ldsem)
        for c in range(CH):
            i = em.sy().dma_start(out=aT(c), in_=Aw[ts(c), :])
            em.inc(i, ls0, 16)
            i = em.sy().dma_start(out=mu_sb[:, c:c + 1], in_=col(mu, c))
            em.inc(i, ls0, 16)
        i = em.sy().dma_start(out=bT, in_=Bw.ap())
        em.inc(i, ls0, 16)
        if b_in[p]:
            ob = B.inputs[b_in[p]]
            for c in range(CH):
                i = em.sy().dma_start(out=ob_sb[:, c:c + 1], in_=col(ob, c))
                em.inc(i, ls0, 16)
        ldw = em.c[ls0]
        em.wait("tensor", ls0, ldw)
        bk = banks.use(6)
        for c in range(CH):
            i = em.te().matmul(B.PS[bk][:, 0:1], aT(c), mu_sb[:, c:c + 1],
                               start=(c == 0), stop=(c == CH - 1))
            mmv = em.inc(i, "mm")
        em.wait("vector", "mm", mmv)
        i = em.ve().tensor_copy(mb_sb, B.PS[bk][:, 0:1])
        vv = em.inc(i, "vec")
        em.ve().drain(fusable=False)
        banks.done(bk, "vec", vv)
        em.wait("scalar", "vec", vv)
        for b in range(NBLK):
            t0 = b * TB
            ls = em.batch(ldsem)
            for c in range(CH):
                i = em.sy().dma_start(out=x_t(c), in_=src[ts(c), t0:t0 + TB])
                em.inc(i, ls, 16)
            ldv = em.c[ls]
            em.wait("tensor", ls, ldv)
            bk = banks.use(0)
            for c in range(CH):
                i = em.te().matmul(B.PS[bk][:, 0:TB], aT(c), x_t(c),
                                   start=(c == 0), stop=(c == CH - 1))
                mmv = em.inc(i, "mm")
            em.wait("scalar", "mm", mmv)
            i = em.sc().activation(inner_t, B.PS[bk][:, 0:TB], AF.Identity, bias=mb_sb)
            av = em.inc(i, "act")
            banks.done(bk, "act", av)
            em.wait("tensor", "act", av)
            for oc in range(CH):
                bk = banks.use(1 + oc % 4)
                i = em.te().matmul(B.PS[bk][:, 0:TB], bT[:, ts(oc)], inner_t,
                                   start=True, stop=True)
                mmv = em.inc(i, "mm")
                em.wait("scalar", "mm", mmv)
                bias = ob_sb[:, oc:oc + 1] if b_in[p] else 0.0
                if p == "d":
                    i = em.sc().activation(o_t, B.PS[bk][:, 0:TB], AF.Tanh, bias=bias)
                    av = em.inc(i, "act")
                    em.sc().drain(fusable=False)
                    i = em.sc().activation(o_t, o_t, AF.Sigmoid)
                    av = em.inc(i, "act")
                else:
                    i = em.sc().activation(o_t, B.PS[bk][:, 0:TB], AF.Sigmoid, bias=bias)
                    av = em.inc(i, "act")
                banks.done(bk, "act", av)
                em.wait("gpsimd", "act", av)
                i = em.gp().dma_start(out=outs[p][ts(oc), t0:t0 + TB], in_=o_t)
                sv = em.inc(i, "st0", 16)
                em.wait("scalar", "st0", sv)
            em.wait("sync", "mm", em.c["mm"])
            em.wait("sync", "act", em.c["act"])
    em.wait("gpsimd", "st0", em.c["st0"])
    em.barrier("gates")


def phase_l0_prep(em, B, ldsem="ld4"):
    nc = B.nc
    Ksrc = B.scratch["KK"].ap()
    Isrc = B.scratch["ICLR"].ap()
    KN = B.scratch["KN"].ap()
    BV = B.scratch["BVEC"].ap()
    RK = B.scratch["RK"].ap()
    rem = B.inputs["L0_rem_mult"]
    mix = B.inputs["L0_iclr_mix"]

    rem_sb = B.AR2[:, 12800:12808]
    mix_sb = B.AR2[:, 12808:12816]
    om_sb = B.AR2[:, 12816:12824]
    k_t = lambda c: B.AR3[:, c * TB: (c + 1) * TB]
    i_t = lambda c: B.AR3[:, 4096 + c * TB: 4096 + (c + 1) * TB]
    rm_t = B.AR3[:, 8192: 8192 + TB]
    s_t = B.AR3[:, 8704: 8704 + TB]
    o1_t = B.AR3[:, 9216: 9216 + TB]
    o2_t = B.AR3[:, 9728: 9728 + TB]
    o3_t = B.AR3[:, 10240: 10240 + TB]
    bdiag = con(B, "bdiag64")

    ls0 = em.batch(ldsem)
    for c in range(CH):
        i = em.sy().dma_start(out=rem_sb[:, c:c + 1], in_=col(rem, c))
        em.inc(i, ls0, 16)
        i = em.sy().dma_start(out=mix_sb[:, c:c + 1], in_=col(mix, c))
        em.inc(i, ls0, 16)
    ldw = em.c[ls0]
    em.wait("vector", ls0, ldw)
    i = em.ve().tensor_scalar(om_sb, mix_sb, -1.0, 1.0, ALU.mult, ALU.add)
    vv = em.inc(i, "vec")
    em.ve().drain(fusable=False)
    banks = Banks(em, range(8))

    for b in range(NBLK):
        t0 = b * TB
        ls = em.batch(ldsem)
        for c in range(CH):
            i = em.sy().dma_start(out=k_t(c), in_=Ksrc[ts(c), t0:t0 + TB])
            em.inc(i, ls, 16)
            i = em.sy().dma_start(out=i_t(c), in_=Isrc[ts(c), t0:t0 + TB])
            em.inc(i, ls, 16)
        ldv = em.c[ls]
        em.wait("vector", ls, ldv)
        em.wait("scalar", ls, ldv)
        for c in range(CH):
            i = em.ve().tensor_scalar(rm_t, k_t(c), rem_sb[:, c:c + 1], None, ALU.mult)
            vv = em.inc(i, "vec")
            em.ve().drain(fusable=False)
            em.wait("scalar", "vec", vv)
            i = em.sc().activation(s_t, rm_t, AF.Square)
            av = em.inc(i, "act")
            em.wait("tensor", "act", av)
            bk = banks.use(c % 4)
            i = em.te().matmul(B.PS[bk][:, 0:TB], bdiag, s_t, start=True, stop=True)
            mmv = em.inc(i, "mm")
            em.wait("scalar", "mm", mmv)
            i = em.sc().activation(s_t, B.PS[bk][:, 0:TB], AF.Sqrt)
            av = em.inc(i, "act")
            banks.done(bk, "act", av)
            em.wait("vector", "act", av)
            i = em.ve().tensor_scalar_max(s_t, s_t, sca(B, "eps12"))
            em.inc(i, "vec")
            em.ve().drain(fusable=False)
            i = em.ve().reciprocal(s_t, s_t)
            em.inc(i, "vec")
            em.ve().drain(fusable=False)
            i = em.ve().tensor_mul(o1_t, rm_t, s_t)
            vv = em.inc(i, "vec")
            em.ve().drain(fusable=False)
            em.wait("gpsimd", "vec", vv)
            i = em.gp().dma_start(out=KN[ts(c), t0:t0 + TB], in_=o1_t)
            sv = em.inc(i, "st0", 16)
            i = em.ve().tensor_mul(o2_t, i_t(c), o1_t)
            vv = em.inc(i, "vec")
            em.ve().drain(fusable=False)
            em.wait("gpsimd", "vec", vv)
            i = em.gp().dma_start(out=BV[ts(c), t0:t0 + TB], in_=o2_t)
            sv = em.inc(i, "st0", 16)
            i = em.ve().tensor_mul(s_t, k_t(c), i_t(c))
            em.inc(i, "vec")
            em.ve().drain(fusable=False)
            i = em.ve().tensor_scalar(o3_t, k_t(c), om_sb[:, c:c + 1], None, ALU.mult)
            em.inc(i, "vec")
            em.ve().drain(fusable=False)
            i = em.ve().scalar_tensor_tensor(o3_t, s_t, mix_sb[:, c:c + 1], o3_t,
                                             ALU.mult, ALU.add)
            vv = em.inc(i, "vec")
            em.ve().drain(fusable=False)
            em.wait("gpsimd", "vec", vv)
            i = em.gp().dma_start(out=RK[ts(c), t0:t0 + TB], in_=o3_t)
            sv = em.inc(i, "st0", 16)
            em.wait("vector", "st0", sv)
        em.wait("sync", "vec", em.c["vec"])
    em.wait("gpsimd", "st0", em.c["st0"])
    em.barrier("prep")


def phase_deltanet(em, B, ldsem="ld5"):
    CC = 128
    NCH = T // CC
    KNs = B.scratch["KN"].ap()
    BVs = B.scratch["BVEC"].ap()
    RKs = B.scratch["RK"].ap()
    VVs = B.scratch["VV"].ap()
    RRs = B.scratch["RR"].ap()
    SGs = B.scratch["SIGT"].ap()
    DO = B.scratch["DOUT"].ap()
    ident = con(B, "ident")
    id64 = B.CON[0:64, CON_OFF["ident"]: CON_OFF["ident"] + 64]
    trilS = con(B, "trilS")
    triuS = con(B, "triuS")
    triuI = con(B, "triuI")
    ntriuI = con(B, "ntriuI")

    S_f = lambda h: B.AR2[0:64, h * 64: h * 64 + 64]
    ST_f = lambda h: B.AR2[0:64, 1024 + h * 64: 1024 + h * 64 + 64]
    zeros_t = B.AR2[0:64, 2048:2176]

    def W(par, idx, rows=64, width=CC):
        off = par * 8500 + idx * 128
        return B.AR3[0:rows, off:off + width]

    def W128(par, idx, width=CC):
        off = par * 8500 + idx * 128
        return B.AR3[0:128, off:off + width]

    IN_G, IN_KN, IN_BV, IN_RK, IN_V, IN_R = 0, 1, 2, 3, 4, 5
    T_G, T_GP, T_E1, T_E2, T_P = 6, 7, 8, 9, 10
    T_B, T_K, T_V = 11, 12, 13
    T_LM, T_LMT, T_WVB = 14, 15, 16
    T_M2T, T_M3T = 17, 18
    T_PW = 19
    T_PWT = 25
    T_RKT, T_VT, T_KTN, T_U0, T_U1 = 31, 32, 33, 34, 35
    T_UF, T_OSB, T_OUTF = 36, 37, 38

    i = em.ve().memset(B.AR2[0:64, 0:2176], 0.0)
    em.inc(i, "vec")
    em.ve().drain(fusable=False)
    em.barrier("dn0")

    banksP = [Banks(em, (0, 1, 2, 3)), Banks(em, (4, 5, 6, 7))]
    prev = [None, None]
    svec = [0] * H
    sact = [0] * H

    units = [(c, h) for c in range(NCH) for h in range(H)]
    for u, (c, h) in enumerate(units):
        par = u % 2
        banks = banksP[par]
        bk_i = [par * 4, par * 4 + 1, par * 4 + 2, par * 4 + 3]
        t0 = c * CC
        hr = h * 64
        if prev[par] is not None:
            for sem, val in prev[par]:
                em.wait("sync", sem, val)
                em.wait("vector", sem, val)
        ls = em.batch(ldsem)
        for idx, src in ((IN_G, SGs), (IN_KN, KNs), (IN_BV, BVs),
                         (IN_RK, RKs), (IN_V, VVs), (IN_R, RRs)):
            i = em.sy().dma_start(out=W(par, idx), in_=src[hr:hr + 64, t0:t0 + CC])
            em.inc(i, ls, 16)
        ldv = em.c[ls]
        em.wait("vector", ls, ldv)
        i = em.ve().tensor_tensor_scan(W(par, T_G), W(par, IN_G), zeros_t, 0.0,
                                       ALU.add, ALU.add)
        em.inc(i, "vec")
        em.ve().drain(fusable=False)
        i = em.ve().tensor_sub(W(par, T_GP), W(par, T_G), W(par, IN_G))
        vv = em.inc(i, "vec")
        em.wait("scalar", "vec", vv)
        i = em.sc().activation(W(par, T_E1), W(par, T_GP), AF.Exp, scale=-DECAY_C)
        em.inc(i, "act")
        i = em.sc().activation(W(par, T_E2), W(par, T_G), AF.Exp, scale=DECAY_C)
        em.inc(i, "act")
        i = em.sc().activation(W(par, T_P), W(par, T_G), AF.Exp, scale=-DECAY_C)
        av = em.inc(i, "act")
        em.wait("vector", "act", av)
        i = em.ve().tensor_mul(W(par, T_B), W(par, IN_BV), W(par, T_E1))
        em.inc(i, "vec")
        em.ve().drain(fusable=False)
        i = em.ve().tensor_mul(W(par, T_K), W(par, IN_KN), W(par, T_E2))
        em.inc(i, "vec")
        em.ve().drain(fusable=False)
        i = em.ve().tensor_mul(W(par, T_V), W(par, IN_V), W(par, T_E2))
        vv_bkv = em.inc(i, "vec")
        em.wait("tensor", "vec", vv_bkv)
        em.wait("tensor", "vec", svec[h])
        em.wait("tensor", "act", sact[h])

        bk = banks.use(bk_i[0])
        i = em.te().matmul(B.PS[bk][:, 0:CC], W(par, T_B), W(par, T_K),
                           start=True, stop=True)
        mv = em.inc(i, "mm")
        em.wait("vector", "mm", mv)
        i = em.ve().tensor_mul(W128(par, T_LM), B.PS[bk][:, 0:CC], trilS)
        vv = em.inc(i, "vec")
        em.ve().drain(fusable=False)
        banks.done(bk, "vec", vv)
        bk = banks.use(bk_i[1])
        i = em.te().matmul(B.PS[bk][:, 0:CC], W(par, T_K), W(par, T_B),
                           start=True, stop=True)
        mv = em.inc(i, "mm")
        em.wait("vector", "mm", mv)
        i = em.ve().tensor_mul(W128(par, T_LMT), B.PS[bk][:, 0:CC], triuS)
        vv = em.inc(i, "vec")
        em.ve().drain(fusable=False)
        banks.done(bk, "vec", vv)
        bk = banks.use(bk_i[2])
        i = em.te().matmul(B.PS[bk][:, 0:CC], W(par, T_V), W(par, T_B),
                           start=True, stop=True)
        mv = em.inc(i, "mm")
        em.wait("vector", "mm", mv)
        i = em.ve().tensor_mul(W128(par, T_WVB), B.PS[bk][:, 0:CC], triuS)
        vv_wvb = em.inc(i, "vec")
        em.ve().drain(fusable=False)
        banks.done(bk, "vec", vv_wvb)

        bk = banks.use(bk_i[3])
        i = em.te().transpose(B.PS[bk][:, 0:64], W(par, IN_RK), id64)
        mv = em.inc(i, "mm")
        em.wait("scalar", "mm", mv)
        i = em.sc().activation(W128(par, T_RKT, 64), B.PS[bk][:, 0:64], AF.Copy)
        av = em.inc(i, "act")
        banks.done(bk, "act", av)
        bk = banks.use(bk_i[0])
        i = em.te().transpose(B.PS[bk][:, 0:64], W(par, T_V), id64)
        mv = em.inc(i, "mm")
        em.wait("scalar", "mm", mv)
        i = em.sc().activation(W128(par, T_VT, 64), B.PS[bk][:, 0:64], AF.Copy)
        av = em.inc(i, "act")
        banks.done(bk, "act", av)
        bk = banks.use(bk_i[1])
        i = em.te().transpose(B.PS[bk][:, 0:64], W(par, T_K), id64)
        mv = em.inc(i, "mm")
        em.wait("scalar", "mm", mv)
        i = em.sc().mul(W128(par, T_KTN, 64), B.PS[bk][:, 0:64], -1.0)
        av_ktn = em.inc(i, "act")
        banks.done(bk, "act", av_ktn)

        bk_u = banks.use(bk_i[2])
        i = em.te().matmul(B.PS[bk_u][:, 0:64], W(par, T_B), S_f(h),
                           start=True, stop=False)
        em.inc(i, "mm")
        em.wait("tensor", "act", em.c["act"])
        i = em.te().matmul(B.PS[bk_u][:, 0:64], W128(par, T_WVB),
                           W128(par, T_RKT, 64), start=False, stop=True)
        mv = em.inc(i, "mm")
        em.wait("scalar", "mm", mv)
        i = em.sc().activation(W128(par, T_U0, 64), B.PS[bk_u][:, 0:64], AF.Copy)
        av = em.inc(i, "act")
        banks.done(bk_u, "act", av)

        curT, cur = T_LMT, T_LM
        for j in range(6):
            bk = banks.use(bk_i[3 - (j % 2)])
            i = em.te().matmul(B.PS[bk][:, 0:CC], W128(par, curT), W128(par, cur),
                               start=True, stop=True)
            mv = em.inc(i, "mm")
            em.wait("scalar", "mm", mv)
            i = em.sc().activation(W128(par, T_PW + j), B.PS[bk][:, 0:CC], AF.Copy)
            av = em.inc(i, "act")
            banks.done(bk, "act", av)
            bk = banks.use(bk_i[2 + (j % 2)])
            i = em.te().matmul(B.PS[bk][:, 0:CC], W128(par, cur), W128(par, curT),
                               start=True, stop=True)
            mv = em.inc(i, "mm")
            em.wait("scalar", "mm", mv)
            i = em.sc().activation(W128(par, T_PWT + j), B.PS[bk][:, 0:CC], AF.Copy)
            av = em.inc(i, "act")
            banks.done(bk, "act", av)
            em.wait("tensor", "act", av)
            cur, curT = T_PW + j, T_PWT + j

        ucur, unext = T_U0, T_U1
        for j in range(5, -2, -1):
            bk = banks.use(bk_i[(6 - j) % 2])
            lt = T_PWT + j if j >= 0 else T_LMT
            i = em.te().matmul(B.PS[bk][:, 0:64], W128(par, lt), W128(par, ucur, 64),
                               start=True, stop=True)
            mv = em.inc(i, "mm")
            em.wait("vector", "mm", mv)
            op = ALU.add if j >= 0 else ALU.subtract
            i = em.ve().tensor_tensor(W128(par, unext, 64), W128(par, ucur, 64),
                                      B.PS[bk][:, 0:64], op)
            vv = em.inc(i, "vec")
            em.ve().drain(fusable=False)
            banks.done(bk, "vec", vv)
            em.wait("tensor", "vec", vv)
            ucur, unext = unext, ucur
        ufin = ucur

        bk = banks.use(bk_i[0])
        i = em.te().transpose(B.PS[bk][0:64, 0:CC], W128(par, ufin, 64), ident)
        mv = em.inc(i, "mm")
        em.wait("scalar", "mm", mv)
        i = em.sc().activation(W(par, T_UF), B.PS[bk][0:64, 0:CC], AF.Copy)
        av_uf = em.inc(i, "act")
        banks.done(bk, "act", av_uf)

        bk = banks.use(bk_i[1])
        i = em.te().matmul(B.PS[bk][:, 0:CC], W(par, IN_RK), W(par, IN_R),
                           start=True, stop=True)
        mv = em.inc(i, "mm")
        em.wait("vector", "mm", mv)
        i = em.ve().tensor_mul(W128(par, T_M2T), B.PS[bk][:, 0:CC], triuI)
        vv = em.inc(i, "vec")
        em.ve().drain(fusable=False)
        banks.done(bk, "vec", vv)
        em.wait("tensor", "act", av_uf)
        bk = banks.use(bk_i[2])
        i = em.te().matmul(B.PS[bk][:, 0:CC], W(par, T_UF), W(par, IN_R),
                           start=True, stop=True)
        mv = em.inc(i, "mm")
        em.wait("vector", "mm", mv)
        i = em.ve().tensor_mul(W128(par, T_M3T), B.PS[bk][:, 0:CC], triuI)
        vv_m3 = em.inc(i, "vec")
        em.ve().drain(fusable=False)
        banks.done(bk, "vec", vv_m3)
        em.wait("tensor", "vec", vv_m3)

        bk_o = banks.use(bk_i[3])
        i = em.te().matmul(B.PS[bk_o][:, 0:64], W128(par, T_M2T),
                           W128(par, T_VT, 64), start=True, stop=False)
        em.inc(i, "mm")
        i = em.te().matmul(B.PS[bk_o][:, 0:64], W128(par, T_M3T),
                           W128(par, T_KTN, 64), start=False, stop=False)
        em.inc(i, "mm")
        i = em.te().matmul(B.PS[bk_o][:, 0:64], W(par, IN_R), ST_f(h),
                           start=False, stop=True)
        mv = em.inc(i, "mm")
        em.wait("scalar", "mm", mv)
        i = em.sc().activation(W128(par, T_OSB, 64), B.PS[bk_o][:, 0:64], AF.Copy)
        av = em.inc(i, "act")
        banks.done(bk_o, "act", av)
        em.wait("tensor", "act", av)
        bk = banks.use(bk_i[0])
        i = em.te().transpose(B.PS[bk][0:64, 0:CC], W128(par, T_OSB, 64), ident)
        mv = em.inc(i, "mm")
        em.wait("vector", "mm", mv)
        stn = em.ldn("st0", u)
        if prev[par] is not None:
            em.wait("vector", prev[par][3][0], prev[par][3][1])
        i = em.ve().tensor_mul(W(par, T_OUTF), B.PS[bk][0:64, 0:CC], W(par, T_P))
        vv = em.inc(i, "vec")
        em.ve().drain(fusable=False)
        banks.done(bk, "vec", vv)
        em.wait("gpsimd", "vec", vv)
        i = em.gp().dma_start(out=DO[hr:hr + 64, t0:t0 + CC], in_=W(par, T_OUTF))
        stv = em.inc(i, stn, 16)

        bk_s = banks.use(bk_i[1])
        i = em.te().matmul(B.PS[bk_s][0:64, 0:64], W128(par, T_VT, 64),
                           W128(par, T_RKT, 64), start=True, stop=False)
        em.inc(i, "mm")
        i = em.te().matmul(B.PS[bk_s][0:64, 0:64], W128(par, T_KTN, 64),
                           W128(par, ufin, 64), start=False, stop=True)
        mv = em.inc(i, "mm")
        em.wait("vector", "mm", mv)
        i = em.ve().tensor_add(W(par, T_GP, 64, 64), B.PS[bk_s][0:64, 0:64], S_f(h))
        em.inc(i, "vec")
        em.ve().drain(fusable=False)
        i = em.ve().tensor_scalar(S_f(h), W(par, T_GP, 64, 64),
                                  W(par, T_P)[:, 127:128], None, ALU.mult)
        vv = em.inc(i, "vec")
        em.ve().drain(fusable=False)
        banks.done(bk_s, "vec", vv)
        svec[h] = vv
        em.wait("tensor", "vec", vv)
        bk = banks.use(bk_i[2])
        i = em.te().transpose(B.PS[bk][0:64, 0:64], S_f(h), id64)
        mv = em.inc(i, "mm")
        em.wait("scalar", "mm", mv)
        i = em.sc().activation(ST_f(h), B.PS[bk][0:64, 0:64], AF.Copy)
        av = em.inc(i, "act")
        banks.done(bk, "act", av)
        sact[h] = av

        prev[par] = [("mm", em.c["mm"]), ("act", em.c["act"]),
                     ("vec", em.c["vec"]), (stn, stv)]
    em.barrier("deltanet")


def phase_l0_combine(em, B, ldsem="ld6"):
    DOs = B.scratch["DOUT"].ap()
    RRs = B.scratch["RR"].ap()
    RKs = B.scratch["RK"].ap()
    VVs = B.scratch["VV"].ap()
    GTs = B.scratch["GATE"].ap()
    OIN = B.scratch["OIN"].ap()
    bm = B.inputs["L0_bonus_mult"]
    lng = B.inputs["L0_ln_g"]
    lnb = B.inputs["L0_ln_b"]
    bdiag = con(B, "bdiag64")
    onesT = B.CON[:, CON_OFF["ones"]:CON_OFF["ones"] + 128]

    bm_sb = B.AR2[:, 12800:12808]
    lng_sb = B.AR2[:, 12808:12816]
    lnb_sb = B.AR2[:, 12816:12824]
    d_t = lambda c: B.AR3[:, c * TB: (c + 1) * TB]
    r_t = B.AR3[:, 4096: 4096 + TB]
    k_t = B.AR3[:, 4608: 4608 + TB]
    v_t = B.AR3[:, 5120: 5120 + TB]
    t_t = B.AR3[:, 5632: 5632 + TB]
    x2_t = B.AR3[:, 6144: 6144 + TB]
    mu_t = B.AR3[:, 6656: 6656 + TB]
    iv_t = B.AR3[:, 7168: 7168 + TB]
    g_t = B.AR3[:, 7680: 7680 + TB]
    o_t = B.AR3[:, 8192: 8192 + TB]

    ls0 = em.batch(ldsem)
    for c in range(CH):
        i = em.sy().dma_start(out=bm_sb[:, c:c + 1], in_=col(bm, c))
        em.inc(i, ls0, 16)
        i = em.sy().dma_start(out=lng_sb[:, c:c + 1], in_=col(lng, c))
        em.inc(i, ls0, 16)
        i = em.sy().dma_start(out=lnb_sb[:, c:c + 1], in_=col(lnb, c))
        em.inc(i, ls0, 16)
    ldw = em.c[ls0]
    em.wait("vector", ls0, ldw)
    i = em.ve().tensor_scalar(bm_sb, bm_sb, 0.2, None, ALU.mult)
    em.inc(i, "vec")
    em.ve().drain(fusable=False)
    banks = Banks(em, range(8))

    for b in range(NBLK):
        t0 = b * TB
        lsd = em.batch(ldsem)
        for c in range(CH):
            i = em.sy().dma_start(out=d_t(c), in_=DOs[ts(c), t0:t0 + TB])
            em.inc(i, lsd, 16)
        em.wait("vector", lsd, em.c[lsd])
        for c in range(CH):
            ls = em.batch(ldsem)
            i = em.sy().dma_start(out=r_t, in_=RRs[ts(c), t0:t0 + TB])
            em.inc(i, ls, 16)
            i = em.sy().dma_start(out=k_t, in_=RKs[ts(c), t0:t0 + TB])
            em.inc(i, ls, 16)
            i = em.sy().dma_start(out=v_t, in_=VVs[ts(c), t0:t0 + TB])
            em.inc(i, ls, 16)
            ldv = em.c[ls]
            em.wait("vector", ls, ldv)
            i = em.ve().tensor_mul(t_t, r_t, k_t)
            vv = em.inc(i, "vec")
            em.ve().drain(fusable=False)
            em.wait("tensor", "vec", vv)
            bk = banks.use(c % 2)
            i = em.te().matmul(B.PS[bk][:, 0:TB], bdiag, t_t, start=True, stop=True)
            mv = em.inc(i, "mm")
            em.wait("vector", "mm", mv)
            i = em.ve().tensor_mul(t_t, v_t, B.PS[bk][:, 0:TB])
            vv = em.inc(i, "vec")
            em.ve().drain(fusable=False)
            banks.done(bk, "vec", vv)
            i = em.ve().scalar_tensor_tensor(d_t(c), t_t, bm_sb[:, c:c + 1], d_t(c),
                                             ALU.mult, ALU.add)
            vv = em.inc(i, "vec")
            em.ve().drain(fusable=False)
            em.wait("sync", "vec", vv)
        em.wait("scalar", "vec", em.c["vec"])
        for c in range(CH):
            i = em.sc().activation(x2_t, d_t(c), AF.Square)
            sq = em.inc(i, "act")
            em.wait("tensor", "act", sq)
            i = em.te().matmul(B.PS[2][:, 0:TB], onesT, d_t(c),
                               start=(c == 0), stop=(c == CH - 1))
            em.inc(i, "mm")
            i = em.te().matmul(B.PS[3][:, 0:TB], onesT, x2_t,
                               start=(c == 0), stop=(c == CH - 1))
            mmv = em.inc(i, "mm")
            em.wait("scalar", "mm", mmv)
        mmv = em.c["mm"]
        em.wait("vector", "mm", mmv)
        i = em.ve().tensor_scalar(mu_t, B.PS[2][:, 0:TB], 1.0 / D, None, ALU.mult)
        em.inc(i, "vec")
        em.ve().drain(fusable=False)
        i = em.ve().tensor_tensor(iv_t, mu_t, mu_t, ALU.mult)
        em.inc(i, "vec")
        em.ve().drain(fusable=False)
        i = em.ve().scalar_tensor_tensor(iv_t, B.PS[3][:, 0:TB], 1.0 / D, iv_t,
                                         ALU.mult, ALU.subtract)
        vv = em.inc(i, "vec")
        em.wait("scalar", "vec", vv)
        i = em.sc().activation(iv_t, iv_t, AF.Sqrt, bias=sca(B, "eps_attln"))
        sv = em.inc(i, "act")
        em.wait("vector", "act", sv)
        i = em.ve().reciprocal(iv_t, iv_t)
        em.inc(i, "vec")
        em.ve().drain(fusable=False)
        for c in range(CH):
            ls = em.batch(ldsem)
            i = em.sy().dma_start(out=g_t, in_=GTs[ts(c), t0:t0 + TB])
            ldv = em.inc(i, ls, 16)
            i = em.ve().tensor_sub(o_t, d_t(c), mu_t)
            em.inc(i, "vec")
            em.ve().drain(fusable=False)
            i = em.ve().tensor_mul(o_t, o_t, iv_t)
            em.inc(i, "vec")
            em.ve().drain(fusable=False)
            i = em.ve().tensor_scalar(o_t, o_t, lng_sb[:, c:c + 1],
                                      lnb_sb[:, c:c + 1], ALU.mult, ALU.add)
            em.inc(i, "vec")
            em.ve().drain(fusable=False)
            em.wait("vector", ls, ldv)
            i = em.ve().tensor_mul(o_t, o_t, g_t)
            vv = em.inc(i, "vec")
            em.ve().drain(fusable=False)
            em.wait("gpsimd", "vec", vv)
            i = em.gp().dma_start(out=OIN[ts(c), t0:t0 + TB], in_=o_t)
            sv = em.inc(i, "st0", 16)
            em.wait("vector", "st0", sv)
            em.wait("sync", "vec", vv)
    em.wait("gpsimd", "st0", em.c["st0"])
    em.barrier("combine")


def phase_l0ffn(em, B, ldsem="ld7"):
    XN2 = B.scratch["XN2"].ap()
    XR1 = B.scratch["XR1"].ap()
    XR2 = B.scratch["XR2"].ap()
    Win = B.inputs["F0_Win"]
    binv = B.inputs["F0_bin"]
    W1 = B.inputs["F0_W1"]
    b1 = B.inputs["F0_b1"]
    W2 = B.inputs["F0_W2"]
    b2 = B.inputs["F0_b2"]
    Wout = B.inputs["F0_Wout"]
    bout = B.inputs["F0_bout"]

    win_t = lambda c: B.AR1[:, c * 64: (c + 1) * 64]
    wout_t = B.AR1[0:64, 512: 512 + D]
    w1_sb = B.AR2[0:64, 12800:12816]
    b1_sb = B.AR2[0:64, 12816:12832]
    w2_sb = B.AR2[0:64, 12832:12848]
    bin_sb = B.AR2[0:64, 12848:12849]
    b2_sb = B.AR2[0:64, 12849:12850]
    bout_sb = B.AR2[:, 12850:12858]
    x_t = lambda c: B.AR3[:, c * TB: (c + 1) * TB]
    res_t = lambda c: B.AR3[:, 4096 + c * TB: 4096 + (c + 1) * TB]
    s_sb = B.AR3[0:64, 8192: 8192 + TB]
    t_sb = B.AR3[0:64, 8704: 8704 + TB]
    ga_sb = B.AR3[0:64, 9216: 9216 + TB]
    gb_sb = B.AR3[0:64, 9728: 9728 + TB]
    o_t = B.AR3[:, 10240: 10240 + TB]

    ls0 = em.batch(ldsem)
    for c in range(CH):
        i = em.sy().dma_start(out=win_t(c), in_=Win[ts(c), :])
        em.inc(i, ls0, 16)
        i = em.sy().dma_start(out=bout_sb[:, c:c + 1], in_=col(bout, c))
        em.inc(i, ls0, 16)
    i = em.sy().dma_start(out=wout_t, in_=Wout.ap())
    em.inc(i, ls0, 16)
    i = em.sy().dma_start(out=w1_sb, in_=W1.ap())
    em.inc(i, ls0, 16)
    i = em.sy().dma_start(out=b1_sb, in_=b1.ap())
    em.inc(i, ls0, 16)
    i = em.sy().dma_start(out=w2_sb, in_=W2.ap())
    em.inc(i, ls0, 16)
    i = em.sy().dma_start(out=bin_sb, in_=colw(binv, 0, 64))
    em.inc(i, ls0, 16)
    i = em.sy().dma_start(out=b2_sb, in_=colw(b2, 0, 64))
    em.inc(i, ls0, 16)
    ldw = em.c[ls0]
    em.wait("tensor", ls0, ldw)
    em.wait("scalar", ls0, ldw)
    em.wait("vector", ls0, ldw)
    banks = Banks(em, range(8))

    for b in range(NBLK):
        t0 = b * TB
        ls = em.batch(ldsem)
        for c in range(CH):
            i = em.sy().dma_start(out=x_t(c), in_=XN2[ts(c), t0:t0 + TB])
            em.inc(i, ls, 16)
            i = em.sy().dma_start(out=res_t(c), in_=XR1[ts(c), t0:t0 + TB])
            em.inc(i, ls, 16)
        ldv = em.c[ls]
        em.wait("tensor", ls, ldv)
        em.wait("vector", ls, ldv)
        bk = banks.use(0)
        for c in range(CH):
            i = em.te().matmul(B.PS[bk][0:64, 0:TB], win_t(c), x_t(c),
                               start=(c == 0), stop=(c == CH - 1))
            mv = em.inc(i, "mm")
        em.wait("scalar", "mm", mv)
        i = em.sc().activation(s_sb, B.PS[bk][0:64, 0:TB], AF.Identity, bias=bin_sb)
        av = em.inc(i, "act")
        em.sc().drain(fusable=False)
        banks.done(bk, "act", av)
        em.wait("vector", "act", av)
        cur, nxt = ga_sb, gb_sb
        for hh in range(16):
            i = em.sc().activation(t_sb, s_sb, AF.Relu,
                                   bias=b1_sb[:, hh:hh + 1], scale=w1_sb[:, hh:hh + 1])
            av = em.inc(i, "act")
            em.wait("vector", "act", av)
            if hh == 0:
                i = em.ve().tensor_scalar(cur, t_sb, w2_sb[:, hh:hh + 1], None, ALU.mult)
            else:
                i = em.ve().scalar_tensor_tensor(nxt, t_sb, w2_sb[:, hh:hh + 1],
                                                 cur, ALU.mult, ALU.add)
                cur, nxt = nxt, cur
            vv = em.inc(i, "vec")
            em.ve().drain(fusable=False)
            em.wait("scalar", "vec", vv)
        i = em.ve().tensor_scalar(cur, cur, b2_sb, None, ALU.add)
        vv = em.inc(i, "vec")
        em.ve().drain(fusable=False)
        em.wait("tensor", "vec", vv)
        for oc in range(CH):
            bk = banks.use(1 + oc % 4)
            i = em.te().matmul(B.PS[bk][:, 0:TB], wout_t[:, ts(oc)], cur,
                               start=True, stop=True)
            mv = em.inc(i, "mm")
            em.wait("vector", "mm", mv)
            i = em.ve().scalar_tensor_tensor(o_t, B.PS[bk][:, 0:TB],
                                             bout_sb[:, oc:oc + 1], res_t(oc),
                                             ALU.add, ALU.add)
            vv = em.inc(i, "vec")
            em.ve().drain(fusable=False)
            banks.done(bk, "vec", vv)
            em.wait("gpsimd", "vec", vv)
            i = em.gp().dma_start(out=XR2[ts(oc), t0:t0 + TB], in_=o_t)
            sv = em.inc(i, "st0", 16)
            em.wait("vector", "st0", sv)
        em.wait("sync", "mm", em.c["mm"])
        em.wait("sync", "vec", em.c["vec"])
    em.wait("gpsimd", "st0", em.c["st0"])
    em.barrier("l0ffn")


def phase_proj(em, B, src, dst, Wname, ldsem, tag, resid=None, bias=None, li=None):
    Wt = B.inputs[Wname]
    w_t = lambda c: B.AR1[:, c * D: (c + 1) * D]
    x_t = lambda c: B.AR3[:, c * TB: (c + 1) * TB]
    res_t = lambda c: B.AR3[:, 4096 + c * TB: 4096 + (c + 1) * TB]
    o_t = B.AR3[:, 8192: 8192 + TB]
    ls0 = em.batch(ldsem)
    for c in range(CH):
        wsrc = Wt[ts(c), :] if li is None else Wt[li, ts(c), :]
        i = em.sy().dma_start(out=w_t(c), in_=wsrc)
        em.inc(i, ls0, 16)
    em.wait("tensor", ls0, em.c[ls0])
    banks = Banks(em, range(8))
    for b in range(NBLK):
        t0 = b * TB
        ls = em.batch(ldsem)
        for c in range(CH):
            i = em.sy().dma_start(out=x_t(c), in_=src[ts(c), t0:t0 + TB])
            em.inc(i, ls, 16)
            if resid is not None:
                i = em.sy().dma_start(out=res_t(c), in_=resid[ts(c), t0:t0 + TB])
                em.inc(i, ls, 16)
        ldv = em.c[ls]
        em.wait("tensor", ls, ldv)
        em.wait("vector", ls, ldv)
        for oc in range(CH):
            bk = banks.use(oc % 8)
            for c in range(CH):
                i = em.te().matmul(B.PS[bk][:, 0:TB], w_t(c)[:, ts(oc)], x_t(c),
                                   start=(c == 0), stop=(c == CH - 1))
                mmv = em.inc(i, "mm")
            em.wait("vector", "mm", mmv)
            if resid is not None:
                i = em.ve().tensor_add(o_t, B.PS[bk][:, 0:TB], res_t(oc))
            else:
                i = em.ve().tensor_copy(o_t, B.PS[bk][:, 0:TB])
            vv = em.inc(i, "vec")
            em.ve().drain(fusable=False)
            banks.done(bk, "vec", vv)
            em.wait("gpsimd", "vec", vv)
            i = em.gp().dma_start(out=dst[ts(oc), t0:t0 + TB], in_=o_t)
            sv = em.inc(i, "st0", 16)
            em.wait("vector", "st0", sv)
        em.wait("sync", "mm", em.c["mm"])
        em.wait("sync", "vec", em.c["vec"])
    em.wait("gpsimd", "st0", em.c["st0"])
    em.barrier("proj" + tag)


def phase_l1attn_pre(em, B, ldsem="ld8"):
    XN3 = B.scratch["XN3"].ap()
    QF = B.scratch["QF"].ap()
    KF = B.scratch["KF"].ap()
    VTM = B.scratch["VTM"].ap()
    oneh = B.inputs["c_t5oneh"]
    table = B.inputs["A1_table"]

    oneh_sb = B.AR1[0:32, 0:4096]
    tab_sb = B.AR1[0:32, 4096:4112]
    w5_sb = B.AR1[0:16, 4112: 4112 + 4096]
    ls0 = em.batch(ldsem)
    i = em.sy().dma_start(out=oneh_sb, in_=oneh.ap())
    em.inc(i, ls0, 16)
    i = em.sy().dma_start(out=tab_sb, in_=table.ap())
    em.inc(i, ls0, 16)
    ldv = em.c[ls0]
    em.wait("tensor", ls0, ldv)
    banks = Banks(em, range(8))
    for j in range(8):
        bk = banks.use(j % 4)
        i = em.te().matmul(B.PS[bk][0:16, 0:TB], tab_sb, oneh_sb[:, j * TB:(j + 1) * TB],
                           start=True, stop=True)
        mv = em.inc(i, "mm")
        em.wait("scalar", "mm", mv)
        i = em.sc().activation(w5_sb[:, j * TB:(j + 1) * TB], B.PS[bk][0:16, 0:TB],
                               AF.Copy)
        av = em.inc(i, "act")
        banks.done(bk, "act", av)
    avw = em.c["act"]
    em.wait("gpsimd", "act", avw)
    i = em.gp().dma_start(out=B.scratch["W5"].ap(), in_=B.AR1[0:16, 4112: 4112 + 4096])
    wv5 = em.inc(i, "st1", 16)
    em.wait("gpsimd", "st1", wv5)
    for h in range(H):
        wsh = B.scratch[f"WSH{h}"].reshape([128, 4226])
        i = em.gp().dma_start(out=wsh[:, 0:4095],
                              in_=B.scratch["W5"][h:h + 1, 0:4095].broadcast_to([128, 4095]))
        em.inc(i, "st1", 16)

    wq_t = lambda c: B.AR2[:, c * D: (c + 1) * D]
    x_t = lambda c: B.AR3[:, c * TB: (c + 1) * TB]
    o_t = B.AR3[:, 4096: 4096 + TB]
    for b in range(NBLK):
        t0 = b * TB
        lsx = em.batch(ldsem)
        for c in range(CH):
            i = em.sy().dma_start(out=x_t(c), in_=XN3[ts(c), t0:t0 + TB])
            em.inc(i, lsx, 16)
        ldv = em.c[lsx]
        em.wait("tensor", lsx, ldv)
        for wi, (wn, dst) in enumerate((("A1_Wq", QF), ("A1_Wk", KF))):
            Wh = B.inputs[wn]
            em.wait("sync", "mm", em.c["mm"])
            ls = em.batch(ldsem)
            for c in range(CH):
                i = em.sy().dma_start(out=wq_t(c) if wi == 0 else wq_t(c),
                                      in_=Wh[ts(c), :])
                em.inc(i, ls, 16)
            ldv = em.c[ls]
            em.wait("tensor", ls, ldv)
            for oc in range(CH):
                bk = banks.use(oc % 4)
                for c in range(CH):
                    i = em.te().matmul(B.PS[bk][:, 0:TB], wq_t(c)[:, ts(oc)], x_t(c),
                                       start=(c == 0), stop=(c == CH - 1))
                    mmv = em.inc(i, "mm")
                em.wait("scalar", "mm", mmv)
                i = em.sc().activation(o_t, B.PS[bk][:, 0:TB], AF.Copy)
                av = em.inc(i, "act")
                banks.done(bk, "act", av)
                em.wait("gpsimd", "act", av)
                i = em.gp().dma_start(out=dst[ts(oc), t0:t0 + TB], in_=o_t)
                sv = em.inc(i, "st1", 16)
                em.wait("scalar", "st1", sv)
        Wv = B.inputs["A1_Wv"]
        em.wait("sync", "mm", em.c["mm"])
        ls = em.batch(ldsem)
        for c in range(CH):
            i = em.sy().dma_start(out=wq_t(c), in_=Wv[ts(c), :])
            em.inc(i, ls, 16)
        ldv = em.c[ls]
        em.wait("tensor", ls, ldv)
        for tt in range(TB // 128):
            for vb in range(2):
                bk = banks.use(4 + (tt * 2 + vb) % 4)
                for c in range(CH):
                    i = em.te().matmul(B.PS[bk][:, 0:TB], x_t(c)[:, tt * 128:(tt + 1) * 128],
                                       wq_t(c)[:, vb * TB:(vb + 1) * TB],
                                       start=(c == 0), stop=(c == CH - 1))
                    mmv = em.inc(i, "mm")
                em.wait("scalar", "mm", mmv)
                i = em.sc().activation(o_t, B.PS[bk][:, 0:TB], AF.Copy)
                av = em.inc(i, "act")
                banks.done(bk, "act", av)
                em.wait("gpsimd", "act", av)
                i = em.gp().dma_start(
                    out=VTM[t0 + tt * 128: t0 + (tt + 1) * 128, vb * TB:(vb + 1) * TB],
                    in_=o_t)
                sv = em.inc(i, "st1", 16)
                em.wait("scalar", "st1", sv)
        em.wait("sync", "mm", em.c["mm"])
    em.wait("gpsimd", "st1", em.c["st1"])
    em.barrier("l1pre")


def phase_l1attn_core(em, B, ldsem="ld9"):
    QF = B.scratch["QF"].ap()
    KF = B.scratch["KF"].ap()
    VTM = B.scratch["VTM"].ap()
    OATT = B.scratch["OATT"].ap()
    ident = con(B, "ident")
    NT = T // 128

    kf_sb = B.AR1[0:64, 0:T]
    vtm_sb = lambda kt: B.AR1[:, 2048 + kt * 64: 2048 + (kt + 1) * 64]
    q_sb = B.AR2[0:64, 0:128]
    bias_sb = lambda j: B.AR2[:, 128 + j * TB: 128 + (j + 1) * TB]
    sc_sb = lambda j: B.AR2[:, 2176 + j * TB: 2176 + (j + 1) * TB]
    att_sb = lambda kt: B.AR2[:, 4224 + kt * 128: 4224 + (kt + 1) * 128]
    red_sb = B.AR2[:, 6272: 6272 + 8]
    of_sb = B.AR3[0:64, 12000: 12000 + 128]

    banks = Banks(em, range(8))
    prevu = None
    for h in range(H):
        hr = h * 64
        wsh = B.scratch[f"WSH{h}"]
        if prevu is not None:
            for sem, val in prevu:
                em.wait("sync", sem, val)
        lsh = em.batch(ldsem)
        i = em.sy().dma_start(out=kf_sb, in_=KF[hr:hr + 64, :])
        em.inc(i, lsh, 16)
        for kt in range(NT):
            i = em.sy().dma_start(out=vtm_sb(kt), in_=VTM[kt * 128:(kt + 1) * 128, hr:hr + 64])
            em.inc(i, lsh, 16)
        ldh = em.c[lsh]
        for qt in range(NT):
            q0 = qt * 128
            ls = em.batch(ldsem)
            i = em.sy().dma_start(out=q_sb, in_=QF[hr:hr + 64, q0:q0 + 128])
            em.inc(i, ls, 16)
            i = em.sy().dma_start(out=B.AR2[:, 128:128 + 4 * TB],
                                  in_=mkAP(B.scratch[f"WSH{h}"], 2047 - q0,
                                           [[4225, 128], [1, 4 * TB]]))
            em.inc(i, ls, 16)
            ldv = em.c[ls]
            em.wait("tensor", ls, ldv)
            em.wait("tensor", lsh, ldh)
            em.wait("vector", ls, ldv)
            for j in range(4):
                bk = banks.use(j)
                i = em.te().matmul(B.PS[bk][:, 0:TB], q_sb, kf_sb[:, j * TB:(j + 1) * TB],
                                   start=True, stop=True)
                mmv = em.inc(i, "mm")
                em.wait("vector", "mm", mmv)
                i = em.ve().tensor_add(sc_sb(j), B.PS[bk][:, 0:TB], bias_sb(j))
                vv = em.inc(i, "vec")
                em.ve().drain(fusable=False)
                banks.done(bk, "vec", vv)
            i = em.ve().tensor_reduce(red_sb[:, 0:1], B.AR2[:, 2176: 2176 + T],
                                      AX.X, ALU.max, negate=True)
            vv = em.inc(i, "vec")
            em.ve().drain(fusable=False)
            em.wait("scalar", "vec", vv)
            for j in range(4):
                i = em.sc().activation(sc_sb(j), sc_sb(j), AF.Exp,
                                       bias=red_sb[:, 0:1],
                                       accum_out=red_sb[:, 2 + j: 3 + j])
                av = em.inc(i, "act")
                em.sc().drain(fusable=False)
            em.wait("vector", "act", em.c["act"])
            i = em.ve().tensor_reduce(red_sb[:, 6:7], red_sb[:, 2:6], AX.X, ALU.add)
            vv = em.inc(i, "vec")
            em.ve().drain(fusable=False)
            i = em.ve().reciprocal(red_sb[:, 6:7], red_sb[:, 6:7])
            vv = em.inc(i, "vec")
            em.ve().drain(fusable=False)
            em.wait("scalar", "vec", vv)
            for j in range(4):
                i = em.sc().mul(sc_sb(j), sc_sb(j), red_sb[:, 6:7])
                av = em.inc(i, "act")
                em.sc().drain(fusable=False)
            avn = em.c["act"]
            em.wait("tensor", "act", avn)
            for kt in range(NT):
                bk = banks.use(4 + kt % 2)
                i = em.te().transpose(B.PS[bk][:, 0:128],
                                      B.AR2[:, 2176 + kt * 128: 2176 + (kt + 1) * 128],
                                      ident)
                mv = em.inc(i, "mm")
                if kt % 2 == 0:
                    em.wait("scalar", "mm", mv)
                    i = em.sc().activation(att_sb(kt), B.PS[bk][:, 0:128], AF.Copy)
                    av = em.inc(i, "act")
                    banks.done(bk, "act", av)
                else:
                    em.wait("vector", "mm", mv)
                    i = em.ve().tensor_copy(att_sb(kt), B.PS[bk][:, 0:128])
                    av = em.inc(i, "vec")
                    em.ve().drain(fusable=False)
                    banks.done(bk, "vec", av)
            em.wait("tensor", "act", em.c["act"])
            em.wait("tensor", "vec", em.c["vec"])
            em.wait("tensor", lsh, ldh)
            bk_o = banks.use(6 + qt % 2)
            for kt in range(NT):
                i = em.te().matmul(B.PS[bk_o][0:64, 0:128], vtm_sb(kt), att_sb(kt),
                                   start=(kt == 0), stop=(kt == NT - 1))
                mv = em.inc(i, "mm")
            em.wait("vector", "mm", mv)
            i = em.ve().tensor_copy(of_sb, B.PS[bk_o][0:64, 0:128])
            vv = em.inc(i, "vec")
            em.ve().drain(fusable=False)
            banks.done(bk_o, "vec", vv)
            em.wait("gpsimd", "vec", vv)
            i = em.gp().dma_start(out=OATT[hr:hr + 64, q0:q0 + 128], in_=of_sb)
            sv = em.inc(i, "st0", 16)
            em.wait("vector", "st0", sv)
            em.wait("sync", "mm", em.c["mm"])
            em.wait("sync", "act", em.c["act"])
            em.wait("sync", "vec", em.c["vec"])
        prevu = [("mm", em.c["mm"]), ("act", em.c["act"]), ("vec", em.c["vec"])]
    em.wait("gpsimd", "st0", em.c["st0"])
    em.barrier("l1core")


def phase_ffqkv(em, B, li, src, ldsem="ld2"):
    Wqkv = B.inputs["FF_Wqkv"]
    QF2 = B.scratch["QF2"].ap()
    KF2 = B.scratch["KF2"].ap()
    V65 = B.scratch["V65"].ap()
    w_t = lambda c: B.AR1[:, c * D: (c + 1) * D]
    x_t = lambda c: B.AR3[:, c * TB: (c + 1) * TB]
    xs_t = lambda c: B.AR3[:, (4 + c) * TB: (5 + c) * TB]
    o_t = B.AR3[:, 8192: 8192 + TB]
    banks = Banks(em, range(8))
    for wi in range(3):
        em.wait("sync", "mm", em.c["mm"])
        lsw = em.batch(ldsem)
        for c in range(CH):
            i = em.sy().dma_start(out=w_t(c), in_=Wqkv[li, ts(c), wi * D:(wi + 1) * D])
            em.inc(i, lsw, 16)
        ldw = em.c[lsw]
        em.wait("tensor", lsw, ldw)
        for b in range(NBLK):
            t0 = b * TB
            lsx = em.batch(ldsem)
            for c in range(CH):
                if c >= 4:
                    if b == 0:
                        em.wait("vector", "mm", em.c["mm"])
                        i = em.ve().memset(xs_t(c - 4)[:, 0:1], 0.0)
                        em.inc(i, "vec")
                        em.ve().drain(fusable=False)
                        i = em.sy().dma_start(out=xs_t(c - 4)[:, 1:TB],
                                              in_=src[ts(c), 0:TB - 1])
                    else:
                        i = em.sy().dma_start(out=xs_t(c - 4),
                                              in_=src[ts(c), t0 - 1:t0 + TB - 1])
                else:
                    i = em.sy().dma_start(out=x_t(c), in_=src[ts(c), t0:t0 + TB])
                em.inc(i, lsx, 16)
            ldv = em.c[lsx]
            em.wait("tensor", lsx, ldv)
            em.wait("tensor", "vec", em.c["vec"])
            xin = lambda c: x_t(c) if c < 4 else xs_t(c - 4)
            if wi < 2:
                dst = QF2 if wi == 0 else KF2
                for oc in range(CH):
                    bk = banks.use(oc % 4)
                    for c in range(CH):
                        i = em.te().matmul(B.PS[bk][:, 0:TB], w_t(c)[:, ts(oc)], xin(c),
                                           start=(c == 0), stop=(c == CH - 1))
                        mmv = em.inc(i, "mm")
                    em.wait("scalar", "mm", mmv)
                    i = em.sc().activation(o_t, B.PS[bk][:, 0:TB], AF.Copy)
                    av = em.inc(i, "act")
                    banks.done(bk, "act", av)
                    em.wait("gpsimd", "act", av)
                    i = em.gp().dma_start(out=dst[ts(oc), t0:t0 + TB], in_=o_t)
                    sv = em.inc(i, "st1", 16)
                    em.wait("scalar", "st1", sv)
            else:
                for tt in range(TB // 128):
                    for vb in range(2):
                        bk = banks.use(4 + (tt * 2 + vb) % 4)
                        for c in range(CH):
                            i = em.te().matmul(B.PS[bk][:, 0:TB],
                                               xin(c)[:, tt * 128:(tt + 1) * 128],
                                               w_t(c)[:, vb * TB:(vb + 1) * TB],
                                               start=(c == 0), stop=(c == CH - 1))
                            mmv = em.inc(i, "mm")
                        em.wait("scalar", "mm", mmv)
                        i = em.sc().activation(
                            mkAP(B.AR3, 8192, [[16400, 128], [65, 8], [1, 64]]),
                            B.PS[bk][:, 0:TB], AF.Copy)
                        av = em.inc(i, "act")
                        banks.done(bk, "act", av)
                        em.wait("gpsimd", "act", av)
                        i = em.gp().dma_start(
                            out=mkAP(B.scratch["V65"],
                                     (t0 + tt * 128) * (16 * 65) + vb * 8 * 65,
                                     [[16 * 65, 128], [1, 8 * 65]]),
                            in_=B.AR3[:, 8192: 8192 + 8 * 65])
                        sv = em.inc(i, "st1", 16)
                        em.wait("scalar", "st1", sv)
            em.wait("sync", "mm", em.c["mm"])
    em.barrier(f"ffqkv{li}")


def phase_rotnorm(em, B, li, ldsem="ld3"):
    rotp = con(B, "rotperm")
    bdiag = con(B, "bdiag64")
    sc_sb = B.AR2[:, 12800:12802]
    x_sb = B.AR3[:, 0:TB]
    sq_sb = B.AR3[:, 512:512 + TB]
    n_sb = B.AR3[:, 1024:1024 + TB]
    qn_sb = B.AR3[:, 1536:1536 + TB]
    qa_sb = B.AR3[:, 2048:2048 + TB]
    qo_sb = B.AR3[:, 2560:2560 + TB]
    ls0 = em.batch(ldsem)
    for half in range(2):
        for wi, name in ((0, "FF_qs"), (1, "FF_ks")):
            i = em.sy().dma_start(out=sc_sb[half * 64:(half + 1) * 64, wi:wi + 1],
                                  in_=B.inputs[name][li:li + 1, :].transpose([1, 0]))
            em.inc(i, ls0, 16)
    ldw = em.c[ls0]
    em.wait("vector", ls0, ldw)
    banks = Banks(em, range(8))
    for wi, (srcn, dstn, cs_n, ss_n) in enumerate(
            (("QF2", "QR", "csq", "ssq"), ("KF2", "KR", "csk", "ssk"))):
        srch = B.scratch[srcn].ap()
        dsth = B.scratch[dstn].ap()
        for c in range(CH):
            for b in range(NBLK):
                t0 = b * TB
                ls = em.batch(ldsem)
                i = em.sy().dma_start(out=x_sb, in_=srch[ts(c), t0:t0 + TB])
                ldv = em.inc(i, ls, 16)
                em.wait("scalar", ls, ldv)
                i = em.sc().activation(sq_sb, x_sb, AF.Square)
                av = em.inc(i, "act")
                em.wait("tensor", "act", av)
                bk = banks.use(0 if wi == 0 else 2)
                i = em.te().matmul(B.PS[bk][:, 0:TB], bdiag, sq_sb, start=True, stop=True)
                mv = em.inc(i, "mm")
                em.wait("scalar", "mm", mv)
                i = em.sc().activation(n_sb, B.PS[bk][:, 0:TB], AF.Sqrt)
                av = em.inc(i, "act")
                banks.done(bk, "act", av)
                em.wait("vector", "act", av)
                i = em.ve().tensor_scalar_max(n_sb, n_sb, sca(B, "eps12"))
                em.inc(i, "vec")
                em.ve().drain(fusable=False)
                i = em.ve().reciprocal(n_sb, n_sb)
                em.inc(i, "vec")
                em.ve().drain(fusable=False)
                em.wait("vector", ls, ldv)
                i = em.ve().tensor_mul(qn_sb, x_sb, n_sb)
                em.inc(i, "vec")
                em.ve().drain(fusable=False)
                i = em.ve().tensor_scalar(qn_sb, qn_sb, sc_sb[:, wi:wi + 1], None,
                                          ALU.mult)
                vv = em.inc(i, "vec")
                em.ve().drain(fusable=False)
                em.wait("tensor", "vec", vv)
                bk = banks.use(1 if wi == 0 else 3)
                i = em.te().matmul(B.PS[bk][:, 0:TB], rotp, qn_sb, start=True, stop=True)
                mv = em.inc(i, "mm")
                i = em.ve().tensor_mul(qa_sb, qn_sb, con(B, cs_n)[:, t0:t0 + TB])
                em.inc(i, "vec")
                em.ve().drain(fusable=False)
                em.wait("vector", "mm", mv)
                i = em.ve().tensor_mul(qo_sb, B.PS[bk][:, 0:TB],
                                       con(B, ss_n)[:, t0:t0 + TB])
                vv = em.inc(i, "vec")
                em.ve().drain(fusable=False)
                banks.done(bk, "vec", vv)
                i = em.ve().tensor_add(qo_sb, qo_sb, qa_sb)
                vv = em.inc(i, "vec")
                em.ve().drain(fusable=False)
                em.wait("gpsimd", "vec", vv)
                i = em.gp().dma_start(out=dsth[ts(c), t0:t0 + TB], in_=qo_sb)
                sv = em.inc(i, "st1", 16)
                em.wait("vector", "st1", sv)
                em.wait("sync", "vec", vv)
                em.wait("scalar", "vec", vv)
    em.wait("gpsimd", "st1", em.c["st1"])
    em.barrier(f"rotnorm{li}")


def phase_local_attn(em, B, li, ldsem="ld4"):
    QR = B.scratch["QR"].ap()
    KR = B.scratch["KR"].ap()
    OA2 = B.scratch["OA2"].ap()
    NT = T // 128
    trilI = con(B, "trilI")
    triuI = con(B, "triuI")
    onesT = B.CON[:, CON_OFF["ones"]:CON_OFF["ones"] + 128]

    kstr = lambda c: B.AR2[:, c * 384: (c + 1) * 384]
    vstr = lambda j: B.AR2[:, 3072 + j * 1040: 3072 + (j + 1) * 1040]
    qstr = lambda c: B.AR2[:, 6192 + c * 128: 6192 + (c + 1) * 128]
    att_sb = lambda sb: B.AR2[:, 7216 + sb * 128: 7216 + (sb + 1) * 128]
    den_sb = B.AR2[:, 7600: 7600 + 2048]
    oa_sb = B.AR2[:, 9648: 9648 + 2048]

    def k_h(h, j):
        return kstr(h // 2)[(h % 2) * 64:(h % 2) * 64 + 64, j * 128:(j + 1) * 128]

    def q_h(h):
        return qstr(h // 2)[(h % 2) * 64:(h % 2) * 64 + 64, :]

    def v_h(h, j):
        return vstr(j)[:, h * 65: h * 65 + 64]

    banks = Banks(em, range(8))
    prevq = None
    for qt in range(NT):
        q0 = qt * 128
        nsb = min(qt, 2) + 1
        sb0 = 2 - (nsb - 1)
        if prevq is not None:
            for sem, val in prevq:
                em.wait("sync", sem, val)
        ls = em.batch(ldsem)
        for c in range(CH):
            i = em.sy().dma_start(out=kstr(c)[:, sb0 * 128: 384],
                                  in_=KR[ts(c), q0 - (nsb - 1) * 128: q0 + 128])
            em.inc(i, ls, 16)
            i = em.sy().dma_start(out=qstr(c), in_=QR[ts(c), q0:q0 + 128])
            em.inc(i, ls, 16)
        for j in range(nsb):
            s0 = q0 - 128 * (nsb - 1 - j)
            i = em.sy().dma_start(out=vstr(sb0 + j),
                                  in_=B.scratch["V65"][s0:s0 + 128, :])
            em.inc(i, ls, 16)
        ldv = em.c[ls]
        em.wait("tensor", ls, ldv)
        for h in range(H):
            bk_s = banks.use(6 + h % 2)
            for j in range(nsb):
                i = em.te().matmul(B.PS[bk_s][:, j * 128:(j + 1) * 128],
                                   k_h(h, sb0 + j), q_h(h), start=True, stop=True)
                mv = em.inc(i, "mm")
            em.wait("scalar", "mm", mv)
            for j in range(nsb):
                i = em.sc().activation(att_sb(j), B.PS[bk_s][:, j * 128:(j + 1) * 128],
                                       AF.Exp, bias=sca(B, "negB"), scale=8.0)
                av = em.inc(i, "act")
                em.sc().drain(fusable=False)
            banks.done(bk_s, "act", av)
            em.wait("vector", "act", av)
            for j in range(nsb):
                sb_abs = sb0 + j
                if sb_abs == 0:
                    i = em.ve().tensor_mul(att_sb(j), att_sb(j), trilI)
                    em.inc(i, "vec")
                    em.ve().drain(fusable=False)
                elif sb_abs == 2:
                    i = em.ve().tensor_mul(att_sb(j), att_sb(j), triuI)
                    em.inc(i, "vec")
                    em.ve().drain(fusable=False)
            vv = em.c["vec"]
            em.wait("tensor", "vec", vv)
            bk_a = banks.use(h // 4)
            for j in range(nsb):
                i = em.te().matmul(B.PS[bk_a][0:64, (h % 4) * 128:(h % 4 + 1) * 128],
                                   v_h(h, sb0 + j), att_sb(j),
                                   start=(j == 0), stop=(j == nsb - 1))
                mv = em.inc(i, "mm")
            bk_d = banks.use(4 + h % 2)
            for j in range(nsb):
                i = em.te().matmul(B.PS[bk_d][:, 0:128], onesT, att_sb(j),
                                   start=(j == 0), stop=(j == nsb - 1))
                mv = em.inc(i, "mm")
            em.wait("vector", "mm", mv)
            i = em.ve().reciprocal(den_sb[:, h * 128:(h + 1) * 128], B.PS[bk_d][:, 0:128])
            dv = em.inc(i, "vec")
            em.ve().drain(fusable=False)
            banks.done(bk_d, "vec", dv)
        em.wait("vector", "mm", em.c["mm"])
        for h in range(H):
            i = em.ve().tensor_mul(oa_sb[0:64, h * 128:(h + 1) * 128],
                                   B.PS[h // 4][0:64, (h % 4) * 128:(h % 4 + 1) * 128],
                                   den_sb[0:64, h * 128:(h + 1) * 128])
            vv = em.inc(i, "vec")
            em.ve().drain(fusable=False)
            em.wait("gpsimd", "vec", vv)
            i = em.gp().dma_start(out=OA2[h * 64:(h + 1) * 64, q0:q0 + 128],
                                  in_=oa_sb[0:64, h * 128:(h + 1) * 128])
            sv = em.inc(i, "st1", 16)
        for bb in range(4):
            banks.done(bb, "vec", em.c["vec"])
        em.wait("vector", "st1", em.c["st1"])
        prevq = [("mm", em.c["mm"]), ("act", em.c["act"]), ("vec", em.c["vec"])]
    em.barrier(f"latt{li}")


def phase_geglu(em, B, li, src, resid, dst, ldsem="ld6"):
    TBF = 256
    NJ = 22
    W1T = B.inputs["FF_W1T"]
    b1T = B.inputs["FF_b1T"]
    b2T = B.inputs["FF_b2T"]
    HT = T // 2

    xh = lambda c: B.AR3[:, c * HT: (c + 1) * HT]
    oacc = lambda c: B.AR2[:, c * HT: (c + 1) * HT]
    wimg = lambda sl: B.AR1[:, sl * 3072: (sl + 1) * 3072]
    w1a = lambda sl, c: B.AR1[:, sl * 3072 + c * 128: sl * 3072 + (c + 1) * 128]
    w1g = lambda sl, c: B.AR1[:, sl * 3072 + 1024 + c * 128: sl * 3072 + 1024 + (c + 1) * 128]
    w2t = lambda sl, oc: B.AR1[:, sl * 3072 + 2048 + oc * 128: sl * 3072 + 2048 + (oc + 1) * 128]
    g_sb = B.AR3[:, 8192: 8192 + TBF]
    y_sb = B.AR3[:, 8448: 8448 + TBF]
    b1_sb = B.AR3[:, 8704: 8704 + 44]
    b2_sb = B.AR3[:, 8748: 8748 + 8]
    res_t = B.AR3[:, 8756: 8756 + HT]
    o_t = B.AR3[:, 9780: 9780 + HT]

    ls0 = em.batch(ldsem)
    i = em.sy().dma_start(out=b1_sb, in_=b1T[li, :, :])
    em.inc(i, ls0, 16)
    i = em.sy().dma_start(out=b2_sb, in_=b2T[li, :, :])
    em.inc(i, ls0, 16)
    ldb = em.c[ls0]
    em.wait("scalar", ls0, ldb)
    em.wait("vector", ls0, ldb)
    banks = Banks(em, range(8))
    jgs = [list(range(g * 4, min(g * 4 + 4, NJ))) for g in range((NJ + 3) // 4)]
    slot_guard = [0, 0, 0, 0]

    for half in range(2):
        h0 = half * HT
        em.wait("sync", "mm", em.c["mm"])
        lsx = em.batch(ldsem)
        for c in range(CH):
            i = em.sy().dma_start(out=xh(c), in_=src[ts(c), h0:h0 + HT])
            em.inc(i, lsx, 16)
        ldx = em.c[lsx]
        em.wait("tensor", lsx, ldx)
        em.wait("vector", lsx, ldx)
        i = em.ve().memset(B.AR2[:, 0:8192], 0.0)
        mz = em.inc(i, "vec")
        em.ve().drain(fusable=False)
        em.wait("tensor", "vec", mz)
        for gi, jg in enumerate(jgs):
            lsw = em.batch(ldsem)
            for j in jg:
                em.wait("sync", "mm", slot_guard[j % 4])
                i = em.sy().dma_start(out=wimg(j % 4), in_=W1T[li, j, :, :])
                em.inc(i, lsw, 16)
            ldw = em.c[lsw]
            em.wait("tensor", lsw, ldw)
            for b in range(4):
                t0 = b * TBF
                first_j, last_j = jg[0], jg[-1]
                for j in jg:
                    sl = j % 4
                    bk_a = banks.use(4)
                    for c in range(CH):
                        i = em.te().matmul(B.PS[bk_a][:, 0:TBF], w1a(sl, c),
                                           xh(c)[:, t0:t0 + TBF],
                                           start=(c == 0), stop=(c == CH - 1))
                        mmv = em.inc(i, "mm")
                    bk_g = banks.use(5)
                    for c in range(CH):
                        i = em.te().matmul(B.PS[bk_g][:, 0:TBF], w1g(sl, c),
                                           xh(c)[:, t0:t0 + TBF],
                                           start=(c == 0), stop=(c == CH - 1))
                        mmv = em.inc(i, "mm")
                    em.wait("scalar", "mm", mmv)
                    i = em.sc().activation(g_sb, B.PS[bk_g][:, 0:TBF], AF.Gelu,
                                           bias=b1_sb[:, 22 + j: 23 + j])
                    av = em.inc(i, "act")
                    banks.done(bk_g, "act", av)
                    em.wait("vector", "act", av)
                    i = em.ve().scalar_tensor_tensor(y_sb, B.PS[bk_a][:, 0:TBF],
                                                     b1_sb[:, j:j + 1], g_sb,
                                                     ALU.add, ALU.mult)
                    vv = em.inc(i, "vec")
                    em.ve().drain(fusable=False)
                    banks.done(bk_a, "vec", vv)
                    em.wait("tensor", "vec", vv)
                    for oc in range(CH):
                        bk_o = oc // 2
                        if j == first_j and oc % 2 == 0:
                            banks.use(bk_o)
                        i = em.te().matmul(
                            B.PS[bk_o][:, (oc % 2) * TBF: (oc % 2 + 1) * TBF],
                            w2t(sl, oc), y_sb,
                            start=(j == first_j and oc % 2 == 0),
                            stop=(j == last_j and oc % 2 == 1))
                        mmv = em.inc(i, "mm")
                    em.wait("scalar", "vec", vv)
                    if b == 3:
                        slot_guard[sl] = em.c["mm"]
                em.wait("vector", "mm", em.c["mm"])
                for bk_o in range(4):
                    i = em.ve().tensor_add(
                        B.AR2[:, bk_o * 2048 + t0: bk_o * 2048 + t0 + TBF],
                        B.AR2[:, bk_o * 2048 + t0: bk_o * 2048 + t0 + TBF],
                        B.PS[bk_o][:, 0:TBF])
                    em.inc(i, "vec")
                    em.ve().drain(fusable=False)
                    i = em.ve().tensor_add(
                        B.AR2[:, (bk_o * 2 + 1) * 1024 + t0: (bk_o * 2 + 1) * 1024 + t0 + TBF],
                        B.AR2[:, (bk_o * 2 + 1) * 1024 + t0: (bk_o * 2 + 1) * 1024 + t0 + TBF],
                        B.PS[bk_o][:, TBF: 2 * TBF])
                    vv = em.inc(i, "vec")
                    em.ve().drain(fusable=False)
                    banks.done(bk_o, "vec", vv)
        lsr = em.batch(ldsem)
        for c in range(CH):
            i = em.sy().dma_start(out=res_t, in_=resid[ts(c), h0:h0 + HT])
            lv = em.inc(i, lsr, 16)
            em.wait("vector", lsr, lv)
            i = em.ve().scalar_tensor_tensor(o_t, oacc(c), b2_sb[:, c:c + 1],
                                             res_t, ALU.add, ALU.add)
            vv = em.inc(i, "vec")
            em.ve().drain(fusable=False)
            em.wait("gpsimd", "vec", vv)
            i = em.gp().dma_start(out=dst[ts(c), h0:h0 + HT], in_=o_t)
            sv = em.inc(i, "st1", 16)
            em.wait("vector", "st1", sv)
            em.wait("sync", "vec", vv)
        em.wait("sync", "st1", em.c["st1"])
    em.barrier(f"geglu{li}")


def phase_add(em, B, asrc, bsrc, dst, ldsem="ld1"):
    a_t = B.AR3[:, 0:T]
    b_t = B.AR3[:, 2048:2048 + T]
    o_t = B.AR3[:, 4096:4096 + T]
    for c in range(CH):
        ls = em.batch(ldsem)
        i = em.sy().dma_start(out=a_t, in_=asrc[ts(c), :])
        em.inc(i, ls, 16)
        i = em.sy().dma_start(out=b_t, in_=bsrc[ts(c), :])
        em.inc(i, ls, 16)
        ldv = em.c[ls]
        em.wait("vector", ls, ldv)
        i = em.ve().tensor_add(o_t, a_t, b_t)
        vv = em.inc(i, "vec")
        em.ve().drain(fusable=False)
        em.wait("gpsimd", "vec", vv)
        i = em.gp().dma_start(out=dst[ts(c), :], in_=o_t)
        sv = em.inc(i, "st0", 16)
        em.wait("vector", "st0", sv)
        em.wait("sync", "vec", vv)
    em.barrier("addout")




def phase_add(em, B, asrc, bsrc, dst, ldsem="ld1"):
    a_t = B.AR3[:, 0:T]
    b_t = B.AR3[:, 2048:2048 + T]
    o_t = B.AR3[:, 4096:4096 + T]
    for c in range(CH):
        ls = em.batch(ldsem)
        i = em.sy().dma_start(out=a_t, in_=asrc[ts(c), :])
        em.inc(i, ls, 16)
        i = em.sy().dma_start(out=b_t, in_=bsrc[ts(c), :])
        em.inc(i, ls, 16)
        ldv = em.c[ls]
        em.wait("vector", ls, ldv)
        i = em.ve().tensor_add(o_t, a_t, b_t)
        vv = em.inc(i, "vec")
        em.ve().drain(fusable=False)
        em.wait("gpsimd", "vec", vv)
        i = em.gp().dma_start(out=dst[ts(c), :], in_=o_t)
        sv = em.inc(i, "st0", 16)
        em.wait("vector", "st0", sv)
        em.wait("sync", "vec", vv)
    em.barrier("addout")


def emit_all(em, B, phases):
    load_constants(em, B)
    for ph in phases:
        ph(em, B)


def declare_scratch(B):
    for n in ("XN1", "RR", "KK", "VV", "GATE", "ICLR", "SIGT",
              "KN", "BVEC", "RK", "DOUT", "OIN", "XR1", "XN2", "XR2",
              "XN3", "XR3", "XN4", "QF", "KF", "OATT"):
        B.scr(n, (D, T))
    B.scr("VTM", (T, D))
    B.scr("W5", (16, 4096))
    for n in ("QF2", "KF2", "QR", "KR", "OA2", "XNF", "XRA", "XRB"):
        B.scr(n, (D, T))
    B.scr("V65", (T, 16 * 65))
    B.scr("DBGA", (128, 768))
    B.scr("DBGA2", (128, 128))
    B.scr("DBGA3", (128, 128))
    for h in range(H):
        B.scr(f"WSH{h}", (128 * 4226,))


def default_phases(B, upto=None):
    ph = [
        ("ln1", lambda em, B: phase_layernorm(
            em, B, B.inputs["xin"].ap(), B.scratch["XN1"].ap(),
            sca(B, "eps_ln"), "ld1", "1")),
        ("mlp_r", lambda em, B: phase_l0_mlp(
            em, B, B.scratch["XN1"].ap(), B.scratch["RR"].ap(),
            "L0_r", "L0_mu_r", "ld2", "r")),
        ("mlp_k", lambda em, B: phase_l0_mlp(
            em, B, B.scratch["XN1"].ap(), B.scratch["KK"].ap(),
            "L0_k", "L0_mu_k", "ld2", "k")),
        ("mlp_v", lambda em, B: phase_l0_mlp(
            em, B, B.scratch["XN1"].ap(), B.scratch["VV"].ap(),
            "L0_v", "L0_mu_v", "ld2", "v")),
        ("gates", lambda em, B: phase_l0_gates(
            em, B, B.scratch["XN1"].ap(), "ld3")),
        ("prep", lambda em, B: phase_l0_prep(em, B, "ld4")),
        ("deltanet", lambda em, B: phase_deltanet(em, B, "ld5")),
        ("combine", lambda em, B: phase_l0_combine(em, B, "ld6")),
        ("mlp_o", lambda em, B: phase_l0_mlp(
            em, B, B.scratch["OIN"].ap(), B.scratch["XR1"].ap(),
            "L0_o", None, "ld2", "o", E, B.inputs["xin"].ap())),
        ("ln2", lambda em, B: phase_layernorm(
            em, B, B.scratch["XR1"].ap(), B.scratch["XN2"].ap(),
            sca(B, "eps_ln"), "ld1", "2")),
        ("l0ffn", lambda em, B: phase_l0ffn(em, B, "ld7")),
        ("ln3", lambda em, B: phase_layernorm(
            em, B, B.scratch["XR2"].ap(), B.scratch["XN3"].ap(),
            sca(B, "eps_ln"), "ld1", "3")),
        ("l1pre", lambda em, B: phase_l1attn_pre(em, B, "ld8")),
        ("l1core", lambda em, B: phase_l1attn_core(em, B, "ld9")),
        ("l1wo", lambda em, B: phase_proj(
            em, B, B.scratch["OATT"].ap(), B.scratch["XR3"].ap(),
            "A1_Wo", "ld2", "wo", B.scratch["XR2"].ap())),
    ]
    ph += [("ln4", lambda em, B: phase_layernorm(
        em, B, B.scratch["XR3"].ap(), B.scratch["XN4"].ap(),
        sca(B, "eps_ln"), "ld1", "4"))]
    pool = ["XN4", "XRA", "XRB"]
    for li in range(6):
        cur = pool[(2 * li) % 3]
        mid = pool[(2 * li + 1) % 3]
        nxt = pool[(2 * li + 2) % 3]
        last = (li == 5)

        def mk(li, cur, mid, nxt, last):
            def f_qkv(em, B):
                phase_ffqkv(em, B, li, B.scratch[cur].ap(), "ld2")

            def f_rot(em, B):
                phase_rotnorm(em, B, li, "ld3")

            def f_latt(em, B):
                phase_local_attn(em, B, li, "ld4")

            def f_wo(em, B):
                phase_proj(em, B, B.scratch["OA2"].ap(), B.scratch[mid].ap(),
                           "FF_Wo", "ld2", f"ffwo{li}", B.scratch[cur].ap(), li=li)

            def f_ln(em, B):
                phase_layernorm(em, B, B.scratch[mid].ap(), B.scratch["XNF"].ap(),
                                sca(B, "eps_ln"), "ld1", f"f{li}", shift=True,
                                lng=(B.inputs["FF_lng"], li * D),
                                lnb=(B.inputs["FF_lnb"], li * D))

            def f_geglu(em, B):
                phase_geglu(em, B, li, B.scratch["XNF"].ap(),
                            B.scratch[mid].ap(), B.scratch[nxt].ap(), "ld6")
            return f_qkv, f_rot, f_latt, f_wo, f_ln, f_geglu

        fs = mk(li, cur, mid, nxt, last)
        ph += [(f"ffqkv{li}", fs[0]), (f"rot{li}", fs[1]), (f"latt{li}", fs[2]),
               (f"ffwo{li}", fs[3]), (f"ffln{li}", fs[4]), (f"geglu{li}", fs[5])]
    ph += [("addout", lambda em, B: phase_add(
        em, B, B.scratch["XN4"].ap(), B.scratch["XR3"].ap(),
        B.outputs["xout"].ap(), "ld1"))]
    if upto is not None:
        names = [p[0] for p in ph]
        ph = ph[:names.index(upto) + 1]
    return [p[1] for p in ph]


def build_nc(dbg_taps=(), upto=None, final=None):
    nc = bass.Bass("TRN2", target_bir_lowering=False)
    B = Builder(nc, dbg_taps)
    declare_io(B)
    alloc_sbuf(B)
    declare_scratch(B)
    phase_list = default_phases(B, upto)
    if upto is not None:
        fin = final or "XN1"
        phase_list = phase_list + [lambda em, B: phase_copyout(em, B, fin)]

    sems = {}
    semnames = (["ph", "mm", "act", "vec", "gpc"]
                + [f"ld{i}{s}" for i in range(10) for s in ("", "b", "c", "d")]
                + [f"st{i}{s}" for i in range(4) for s in ("", "b", "c", "d")])
    with _multi(nc, semnames, sems):
        with nc.Block() as blk:
            def run(role, eng):
                em = EM(nc, role, eng, sems)
                emit_all(em, B, phase_list)

            @blk.sync
            def _(eng):
                run("sync", eng)

            @blk.tensor
            def _(eng):
                run("tensor", eng)

            @blk.scalar
            def _(eng):
                run("scalar", eng)

            @blk.vector
            def _(eng):
                run("vector", eng)

            @blk.gpsimd
            def _(eng):
                run("gpsimd", eng)
    return nc, B


def phase_copyout(em, B, name="XN1"):
    src = B.scratch[name].ap()
    dst = B.outputs["xout"].ap()
    for c in range(CH):
        t = B.AR3[:, 0:T]
        ls = em.batch("ld2")
        i = em.sy().dma_start(out=t, in_=src[c * 128:(c + 1) * 128, :])
        lv = em.inc(i, ls, 16)
        em.wait("gpsimd", ls, lv)
        i = em.gp().dma_start(out=dst[c * 128:(c + 1) * 128, :], in_=t)
        sv = em.inc(i, "st1", 16)
        em.wait("sync", "st1", sv)
    em.wait("gpsimd", "st1", em.c["st1"])
    em.barrier("out")


from contextlib import ExitStack, contextmanager


@contextmanager
def _multi(nc, names, out):
    with ExitStack() as st:
        for n in names:
            out[n] = st.enter_context(nc.semaphore(n))
        yield


def make_in_map(core_inputs, weights):
    m = dict(weights)
    m.update(core_inputs)
    return m


def host_weights(I):
    w = {}
    for p in ("r", "k", "v", "o"):
        w[f"L0_{p}W1"] = I[f"L0_attn_{p}W1"]
        w[f"L0_{p}b1"] = I[f"L0_attn_{p}b1"]
        w[f"L0_{p}W2"] = I[f"L0_attn_{p}W2"]
        w[f"L0_{p}b2"] = I[f"L0_attn_{p}b2"]
    for p in ("mu_r", "mu_k", "mu_v", "mu_g", "mu_a", "mu_d", "db", "ab",
              "rem_mult", "iclr_mix", "bonus_mult", "ln_g", "ln_b"):
        w[f"L0_{p}"] = I[f"L0_attn_{p}"]
    for p in ("dA", "aA", "gA", "dB", "aB", "gB"):
        w[f"L0_{p}"] = I[f"L0_attn_{p}"]
    w["F0_Win"] = I["L0_ffn_Win"]
    w["F0_bin"] = I["L0_ffn_bin_"]
    for p in ("W1", "b1", "W2", "b2", "Wout", "bout"):
        w[f"F0_{p}"] = I[f"L0_ffn_{p}"]
    for p in ("Wq", "Wk", "Wv", "Wo"):
        w[f"A1_{p}"] = I[f"L1_attn_{p}"]
    w["A1_table"] = I["L1_attn_rel_bias_table"]
    w["FF_Wqkv"] = I["L1_ffn_Wqkv"]
    w["FF_qs"] = I["L1_ffn_q_scale"]
    w["FF_ks"] = I["L1_ffn_k_scale"]
    w["FF_Wo"] = I["L1_ffn_Wo"]
    w["FF_lng"] = I["L1_ffn_ln_g"]
    w["FF_lnb"] = I["L1_ffn_ln_b"]
    W1 = np.asarray(I["L1_ffn_W1"], np.float32)
    W2 = np.asarray(I["L1_ffn_W2"], np.float32)
    b1 = np.asarray(I["L1_ffn_b1"], np.float32)
    b2 = np.asarray(I["L1_ffn_b2"], np.float32)
    W1T = np.zeros((6, 22, 128, 3072), np.float32)
    b1T = np.zeros((6, 128, 44), np.float32)
    for j in range(22):
        wj = min(128, 2730 - j * 128)
        a = W1[:, :, j * 128: j * 128 + wj]
        g = W1[:, :, 2730 + j * 128: 2730 + j * 128 + wj]
        for c in range(8):
            W1T[:, j, :, c * 128: c * 128 + wj] = a[:, c * 128:(c + 1) * 128, :]
            W1T[:, j, :, 1024 + c * 128: 1024 + c * 128 + wj] = g[:, c * 128:(c + 1) * 128, :]
        W1T[:, j, 0:wj, 2048:3072] = W2[:, j * 128: j * 128 + wj, :].reshape(6, wj, 1024)[:, :, :].transpose(0, 1, 2).reshape(6, wj, 1024)[:, :, :] .reshape(6, wj, 1024)[:, :, 0:1024].reshape(6, wj, 1024) if False else W2[:, j * 128: j * 128 + wj, :]
        b1T[:, 0:wj, j] = b1[:, j * 128: j * 128 + wj]
        b1T[:, 0:wj, 22 + j] = b1[:, 2730 + j * 128: 2730 + j * 128 + wj]
    w["FF_W1T"] = W1T
    w["FF_b1T"] = b1T
    w["FF_b2T"] = np.ascontiguousarray(b2.reshape(6, 8, 128).transpose(0, 2, 1))
    C = host_constants()
    for k, v in C.items():
        if "c_" + k in ("c_zeros64",):
            continue
        w["c_" + k] = v
    w.pop("c_zeros64", None)
    return w


def kernel(**inputs):
    X = np.asarray(inputs["L0_attn_queries"], dtype=np.float32)
    Bn = X.shape[0]
    nc, B = build_nc()
    w = host_weights(inputs)
    w = {k: np.ascontiguousarray(np.asarray(v, dtype=np.float32)) for k, v in w.items()}
    in_maps = []
    for b in range(Bn):
        m = dict(w)
        m["xin"] = np.ascontiguousarray(X[b].T)
        in_maps.append(m)
    res = run_bass_kernel_spmd(nc, in_maps, core_ids=list(range(8)))
    out = np.stack([np.ascontiguousarray(r["xout"].T) for r in res.results])
    return out.astype(np.float32)


if __name__ == "__main__":
    pass


# version 03 (final)
# speedup vs baseline: 1.2371x; 1.0611x over previous
import math
import numpy as np
import concourse.bass as bass
import concourse.mybir as mybir
from concourse.bass_utils import run_bass_kernel_spmd
from concourse.ap import AP as mkAP

F32 = mybir.dt.float32
AF = mybir.ActivationFunctionType
ALU = mybir.AluOpType
AX = mybir.AxisListType

T = 2048
D = 1024
H = 16
HD = 64
E = 1228
LRK = 128
TB = 512
NBLK = T // TB
CH = D // 128
DECAY_C = 0.7408182206817179

ROLES = ("sync", "tensor", "scalar", "vector", "gpsimd")


class EM:

    def __init__(self, nc, role, eng, sems):
        self.nc = nc
        self.role = role
        self.eng = eng
        self.S = sems
        self.c = {k: 0 for k in sems}
        self.bi = {}
        self.lastw = {}

    def bump(self, sem, n=1):
        self.c[sem] += n
        return self.c[sem]

    def inc(self, instr, sem, n=1):
        if instr is not None:
            instr.then_inc(self.S[sem], n)
        return self.bump(sem, n)

    def wait(self, role, sem, val):
        if self.role == role:
            last = self.lastw.get(sem, -1)
            if val > last:
                self.eng.wait_ge(self.S[sem], val)
                self.lastw[sem] = val

    def wait_cur(self, role, sem):
        self.wait(role, sem, self.c[sem])

    def ldn(self, base, i):
        return base + ("", "b", "c", "d")[i % 4]

    def batch(self, base):
        self.bi.setdefault(base, -1)
        self.bi[base] += 1
        return self.ldn(base, self.bi[base])

    def flush(self, role, bases):
        for b in bases:
            for suf in ("", "b", "c", "d"):
                nm = b + suf
                if nm in self.S and self.c[nm] > 0:
                    self.wait(role, nm, self.c[nm])

    def sy(self):
        return self.eng if self.role == "sync" else _Null
    def te(self):
        return self.eng if self.role == "tensor" else _Null
    def sc(self):
        return self.eng if self.role == "scalar" else _Null
    def ve(self):
        return self.eng if self.role == "vector" else _Null
    def gp(self):
        return self.eng if self.role == "gpsimd" else _Null

    def barrier(self, tag):
        self.flush("sync", [f"ld{i}" for i in range(10)])
        self.flush("gpsimd", [f"st{i}" for i in range(4)])
        self.eng.sem_inc(self.S["ph"], 1)
        val = self.bump("ph", 5)
        self.eng.wait_ge(self.S["ph"], val)


class _NullEngine:
    def __getattr__(self, name):
        def f(*a, **k):
            return None
        return f


_Null = _NullEngine()


def host_constants():
    C = {}
    C["ident"] = np.eye(128, dtype=np.float32)
    r = np.arange(128)
    C["trilS"] = (r[None, :] < r[:, None]).astype(np.float32)
    C["triuS"] = (r[:, None] < r[None, :]).astype(np.float32)
    C["triuI"] = (r[:, None] <= r[None, :]).astype(np.float32)
    C["ntriuI"] = -C["triuI"]
    bd = np.zeros((128, 128), dtype=np.float32)
    bd[:64, :64] = 1.0
    bd[64:, 64:] = 1.0
    C["bdiag64"] = bd
    P = np.zeros((128, 128), dtype=np.float32)
    for blk in range(2):
        b = 64 * blk
        for i in range(32):
            P[b + 32 + i, b + i] = -1.0
            P[b + i, b + 32 + i] = 1.0
    C["rotperm"] = P
    half = np.arange(0, HD, 2, dtype=np.float32)
    inv_freq = np.float32(1.0) / np.float32(10000.0) ** (half / np.float32(HD))
    t = np.arange(T, dtype=np.float32)
    freqs = t[:, None] * inv_freq[None, :]
    freqs = np.concatenate([freqs, freqs], -1)
    base_scale = (half + np.float32(0.4) * HD) / np.float32(1.4 * HD)
    power = (t - T // 2) / np.float32(128)
    scale = base_scale[None, :] ** power[:, None]
    scale = np.concatenate([scale, scale], -1)
    cos = np.cos(freqs); sin = np.sin(freqs)
    inv = np.float32(1.0) / scale
    def fm(a):
        a = np.ascontiguousarray(a.T.astype(np.float32))
        return np.concatenate([a, a], 0)
    C["csq"] = fm(cos * scale)
    C["ssq"] = fm(sin * scale)
    C["csk"] = fm(cos * inv)
    C["ssk"] = fm(sin * inv)
    NUM_BUCKETS, MAX_DIST = 32, 128
    nb = NUM_BUCKETS // 2
    d = np.arange(-2047, 2048)
    rel_buckets = (d > 0).astype(np.int64) * nb
    ad = np.abs(d)
    max_exact = nb // 2
    is_small = ad < max_exact
    rp_large = max_exact + (
        np.log(np.maximum(ad, 1).astype(np.float32) / max_exact)
        / math.log(MAX_DIST / max_exact) * (nb - max_exact)
    ).astype(np.int32)
    rp_large = np.minimum(rp_large, nb - 1)
    bucket = rel_buckets + np.where(is_small, ad, rp_large)
    oh = np.zeros((4095, 32), dtype=np.float32)
    oh[np.arange(4095), bucket] = 1.0
    ohp = np.zeros((4096, 32), dtype=np.float32)
    ohp[:4095] = oh
    C["t5oneh"] = np.ascontiguousarray(ohp.T)
    masks = []
    for off in (-256, -128, 0):
        s_idx = off + np.arange(128)[:, None]
        t_idx = np.arange(128)[None, :]
        dd = t_idx - s_idx
        masks.append(((dd >= 0) & (dd <= 256)).astype(np.float32))
    C["lmask"] = np.concatenate(masks, axis=1)
    C["zeros64"] = np.zeros((128, 128), dtype=np.float32)
    rp = np.zeros((16, 8 * 128), dtype=np.float32)
    for cc in range(8):
        for mm_ in range(128):
            rp[2 * cc + mm_ // 64, cc * 128 + mm_] = 1.0
    C["repl16"] = rp
    r2 = np.arange(128)
    C["trilI"] = (r2[None, :] <= r2[:, None]).astype(np.float32)
    return C


class Builder:
    def __init__(self, nc, dbg_taps=()):
        self.nc = nc
        self.dbg_taps = set(dbg_taps)
        self.inputs = {}
        self.scratch = {}
        self.outputs = {}
        self.sb = {}
        self.psum = []

    def inp(self, name, shape):
        h = self.nc.dram_tensor(name, list(shape), F32, kind="ExternalInput")
        self.inputs[name] = h
        return h

    def scr(self, name, shape):
        kind = "ExternalOutput" if name in self.dbg_taps else "Internal"
        h = self.nc.dram_tensor(name, list(shape), F32, kind=kind)
        self.scratch[name] = h
        if kind == "ExternalOutput":
            self.outputs[name] = h
        return h

    def out(self, name, shape):
        h = self.nc.dram_tensor(name, list(shape), F32, kind="ExternalOutput")
        self.outputs[name] = h
        return h

    def tap(self, name, src_hbm):
        return name in self.dbg_taps


def declare_io(B, with_weights=True):
    nc = B.nc
    B.inp("xin", (D, T))
    B.out("xout", (D, T))
    if not with_weights:
        return
    for p in ("r", "k", "v", "o"):
        B.inp(f"L0_{p}W1", (D, E))
        B.inp(f"L0_{p}b1", (E,))
        B.inp(f"L0_{p}W2", (E, D))
        B.inp(f"L0_{p}b2", (D,))
    for p in ("mu_r", "mu_k", "mu_v", "mu_g", "mu_a", "mu_d",
              "db", "ab", "rem_mult", "iclr_mix", "bonus_mult",
              "ln_g", "ln_b"):
        B.inp(f"L0_{p}", (D,))
    for p in ("dA", "aA", "gA"):
        B.inp(f"L0_{p}", (D, LRK))
    for p in ("dB", "aB", "gB"):
        B.inp(f"L0_{p}", (LRK, D))
    B.inp("F0_Win", (D, 64))
    B.inp("F0_bin", (64,))
    B.inp("F0_W1", (64, 16))
    B.inp("F0_b1", (64, 16))
    B.inp("F0_W2", (64, 16))
    B.inp("F0_b2", (64,))
    B.inp("F0_Wout", (64, D))
    B.inp("F0_bout", (D,))
    for p in ("Wq", "Wk", "Wv", "Wo"):
        B.inp(f"A1_{p}", (D, D))
    B.inp("A1_table", (32, H))
    B.inp("FF_Wqkv", (6, D, 3 * D))
    B.inp("FF_qs", (6, HD))
    B.inp("FF_ks", (6, HD))
    B.inp("FF_Wo", (6, D, D))
    B.inp("FF_lng", (6, D))
    B.inp("FF_lnb", (6, D))
    B.inp("FF_W1T", (6, 22, 128, 3072))
    B.inp("FF_b1T", (6, 128, 44))
    B.inp("FF_b2T", (6, 128, 8))
    B.inp("c_ident", (128, 128))
    B.inp("c_trilS", (128, 128))
    B.inp("c_triuS", (128, 128))
    B.inp("c_triuI", (128, 128))
    B.inp("c_ntriuI", (128, 128))
    B.inp("c_bdiag64", (128, 128))
    B.inp("c_rotperm", (128, 128))
    B.inp("c_csq", (128, T))
    B.inp("c_ssq", (128, T))
    B.inp("c_csk", (128, T))
    B.inp("c_ssk", (128, T))
    B.inp("c_t5oneh", (32, 4096))
    B.inp("c_lmask", (128, 384))
    B.inp("c_repl16", (16, 8 * 128))
    B.inp("c_trilI", (128, 128))


def alloc_sbuf(B):
    nc = B.nc
    B.AR1 = nc.alloc_sbuf_tensor("AR1", [128, 13000], F32)
    B.AR2 = nc.alloc_sbuf_tensor("AR2", [128, 13000], F32)
    B.AR3 = nc.alloc_sbuf_tensor("AR3", [128, 16400], F32)
    B.CON = nc.alloc_sbuf_tensor("CON", [128, 10800], F32)
    B.PS = [nc.alloc_psum_tensor(f"PS{i}", [128, 512], F32) for i in range(8)]


CON_OFF = {
    "ident": 0, "trilS": 128, "triuS": 256, "triuI": 384, "ntriuI": 512,
    "bdiag64": 640, "rotperm": 768, "lmask": 896,
    "csq": 1280, "ssq": 3328, "csk": 5376, "ssk": 7424,
    "sca": 9472,
    "ones": 9488,
    "repl16": 9616,
    "trilI": 10640,
}

SCA = {"eps_ln": 0, "eps_attln": 1, "negB": 2, "eps12": 3, "one": 4}
SCA_VAL = {"eps_ln": 1e-5, "eps_attln": 1e-6, "negB": -8.0, "eps12": 1e-12,
           "one": 1.0}


def sca(B, name):
    off = CON_OFF["sca"] + SCA[name]
    return B.CON[:, off:off + 1]


def con(B, name, rows=slice(0, 128), width=None):
    off = CON_OFF[name]
    if width is None:
        width = {"csq": T, "ssq": T, "csk": T, "ssk": T, "lmask": 384}.get(name, 128)
    return B.CON[rows, off:off + width]


def load_constants(em, B):
    names = ["ident", "trilS", "triuS", "triuI", "ntriuI", "bdiag64", "rotperm",
             "trilI"]
    for n in names:
        i = em.sy().dma_start(out=con(B, n), in_=B.inputs["c_" + n].ap())
        em.inc(i, "ld0", 16)
    i = em.sy().dma_start(out=con(B, "lmask"), in_=B.inputs["c_lmask"].ap())
    em.inc(i, "ld0", 16)
    i = em.sy().dma_start(out=B.CON[0:16, CON_OFF["repl16"]:CON_OFF["repl16"] + 1024],
                          in_=B.inputs["c_repl16"].ap())
    em.inc(i, "ld0", 16)
    for n in ("csq", "ssq", "csk", "ssk"):
        i = em.sy().dma_start(out=con(B, n), in_=B.inputs["c_" + n].ap())
        em.inc(i, "ld0", 16)
    for nm, val in SCA_VAL.items():
        i = em.ve().memset(sca(B, nm), float(val))
        em.inc(i, "vec")
    i = em.ve().memset(B.CON[:, CON_OFF["ones"]:CON_OFF["ones"] + 128], 1.0)
    em.inc(i, "vec")
    em.ve().drain(fusable=False)
    em.wait("sync", "ld0", em.c["ld0"])
    em.barrier("consts")


def phase_layernorm(em, B, src, dst, eps_ap=None, ldsem="ld1", tag="",
                    shift=False, lng=None, lnb=None):
    nc = B.nc
    onesT = B.CON[:, CON_OFF["ones"]:CON_OFF["ones"] + 128]
    aff_sb = B.AR3[:, 16000:16016]
    if lng is not None:
        gh, go = lng
        bh, bo = lnb
        ls0 = em.batch(ldsem)
        for c in range(CH):
            i = em.sy().dma_start(out=aff_sb[:, c:c + 1],
                                  in_=mkAP(gh, go + c * 128, [[1, 128], [1, 1]]))
            em.inc(i, ls0, 16)
            i = em.sy().dma_start(out=aff_sb[:, 8 + c: 9 + c],
                                  in_=mkAP(bh, bo + c * 128, [[1, 128], [1, 1]]))
            em.inc(i, ls0, 16)
        em.wait("vector", ls0, em.c[ls0])
    x_t = lambda c: B.AR3[:, c * TB: (c + 1) * TB]
    x2_t = B.AR3[:, 4096: 4096 + TB]
    xn_t = lambda c: B.AR3[:, 4608 + c * TB: 4608 + (c + 1) * TB]
    mu_t = B.AR3[:, 8704: 8704 + TB]
    iv_t = B.AR3[:, 9216: 9216 + TB]

    for b in range(NBLK):
        t0 = b * TB
        ls = em.batch(ldsem)
        for c in range(CH):
            if shift and c >= 4:
                if b == 0:
                    i = em.ve().memset(x_t(c)[:, 0:1], 0.0)
                    em.inc(i, "vec")
                    em.ve().drain(fusable=False)
                    i = em.sy().dma_start(out=x_t(c)[:, 1:TB],
                                          in_=src[c * 128:(c + 1) * 128, 0:TB - 1])
                else:
                    i = em.sy().dma_start(out=x_t(c),
                                          in_=src[c * 128:(c + 1) * 128, t0 - 1:t0 + TB - 1])
            else:
                i = em.sy().dma_start(out=x_t(c), in_=src[c * 128:(c + 1) * 128, t0:t0 + TB])
            em.inc(i, ls, 16)
        ldv = em.c[ls]
        em.wait("tensor", ls, ldv)
        em.wait("vector", ls, ldv)
        em.wait("scalar", ls, ldv)
        em.wait("tensor", "vec", em.c["vec"])
        em.wait("scalar", "vec", em.c["vec"])
        for c in range(CH):
            i = em.sc().activation(x2_t, x_t(c), AF.Square)
            sq = em.inc(i, "act")
            em.wait("tensor", "act", sq)
            i = em.te().matmul(B.PS[0][:, 0:TB], onesT, x_t(c),
                               start=(c == 0), stop=(c == CH - 1))
            em.inc(i, "mm")
            i = em.te().matmul(B.PS[1][:, 0:TB], onesT, x2_t,
                               start=(c == 0), stop=(c == CH - 1))
            mmv = em.inc(i, "mm")
            em.wait("scalar", "mm", mmv)
        mmv = em.c["mm"]
        em.wait("vector", "mm", mmv)
        i = em.ve().tensor_scalar(mu_t, B.PS[0][:, 0:TB], 1.0 / D, None, ALU.mult)
        em.inc(i, "vec")
        em.ve().drain(fusable=False)
        i = em.ve().tensor_tensor(iv_t, mu_t, mu_t, ALU.mult)
        em.inc(i, "vec")
        em.ve().drain(fusable=False)
        i = em.ve().scalar_tensor_tensor(iv_t, B.PS[1][:, 0:TB], 1.0 / D, iv_t,
                                         ALU.mult, ALU.subtract)
        vv = em.inc(i, "vec")
        em.wait("scalar", "vec", vv)
        i = em.sc().activation(iv_t, iv_t, AF.Sqrt, bias=eps_ap)
        sv = em.inc(i, "act")
        em.wait("vector", "act", sv)
        i = em.ve().reciprocal(iv_t, iv_t)
        em.inc(i, "vec")
        em.ve().drain(fusable=False)
        for c in range(CH):
            i = em.ve().tensor_sub(xn_t(c), x_t(c), mu_t)
            em.inc(i, "vec")
            em.ve().drain(fusable=False)
            i = em.ve().tensor_mul(xn_t(c), xn_t(c), iv_t)
            vv = em.inc(i, "vec")
            em.ve().drain(fusable=False)
            if lng is not None:
                i = em.ve().tensor_scalar(xn_t(c), xn_t(c), aff_sb[:, c:c + 1],
                                          aff_sb[:, 8 + c: 9 + c], ALU.mult, ALU.add)
                vv = em.inc(i, "vec")
                em.ve().drain(fusable=False)
            em.wait("gpsimd", "vec", vv)
            i = em.gp().dma_start(out=dst[c * 128:(c + 1) * 128, t0:t0 + TB],
                                  in_=xn_t(c))
            sv = em.inc(i, "st0", 16)
            em.wait("vector", "st0", sv)
        em.wait("sync", "vec", em.c["vec"])
        em.wait("sync", "mm", em.c["mm"])
    em.wait("gpsimd", "st0", em.c["st0"])
    em.barrier("ln" + tag)


class Banks:

    def __init__(self, em, idxs):
        self.em = em
        self.pend = {i: None for i in idxs}

    def use(self, i):
        if self.pend[i] is not None:
            sem, val = self.pend[i]
            self.em.wait("tensor", sem, val)
        return i

    def done(self, i, sem, val):
        self.pend[i] = (sem, val)



def col(h, c, rows=128):
    n = h.shape[0]
    return h.reshape([n, 1])[c * rows: c * rows + rows, :]


def colw(h, start, rows):
    n = h.shape[0]
    return h.reshape([n, 1])[start: start + rows, :]

def ts(i, n=128):
    return slice(i * n, (i + 1) * n)


def phase_l0_mlp(em, B, src, dst, wpfx, mu_name, ldsem, tag, eouter=E, resid=None):
    nc = B.nc
    W1 = B.inputs[wpfx + "W1"]
    b1 = B.inputs[wpfx + "b1"]
    W2 = B.inputs[wpfx + "W2"]
    b2 = B.inputs[wpfx + "b2"]
    mu = B.inputs[mu_name] if mu_name else None
    MC = (eouter + 127) // 128
    mws = [min(128, eouter - mc * 128) for mc in range(MC)]

    w1t = lambda c: B.AR1[:, c * eouter: (c + 1) * eouter]
    w2t = lambda mc: B.AR2[:, mc * D: (mc + 1) * D]
    mu_sb = B.AR2[:, 12800:12808]
    b2_sb = B.AR2[:, 12808:12816]
    mb_sb = B.AR2[:, 12816:12816 + MC]
    x_t = lambda c: B.AR3[:, c * TB: (c + 1) * TB]
    h_t = lambda mc: B.AR3[:, 4608 + mc * TB: 4608 + (mc + 1) * TB]
    o_t = B.AR3[:, 4608 + MC * TB: 4608 + MC * TB + TB]
    res_t = lambda c: B.AR3[:, 10240 + c * TB: 10240 + (c + 1) * TB]

    ls0 = em.batch(ldsem)
    for c in range(CH):
        i = em.sy().dma_start(out=w1t(c), in_=W1[ts(c), :])
        em.inc(i, ls0, 16)
    for mc in range(MC):
        i = em.sy().dma_start(out=w2t(mc)[0:mws[mc], :], in_=W2[mc * 128: mc * 128 + mws[mc], :])
        em.inc(i, ls0, 16)
    for c in range(CH):
        if mu is not None:
            i = em.sy().dma_start(out=mu_sb[:, c:c + 1], in_=col(mu, c))
            em.inc(i, ls0, 16)
        i = em.sy().dma_start(out=b2_sb[:, c:c + 1], in_=col(b2, c))
        em.inc(i, ls0, 16)
    for mc in range(MC):
        i = em.sy().dma_start(out=mb_sb[0:mws[mc], mc:mc + 1],
                              in_=colw(b1, mc * 128, mws[mc]))
        em.inc(i, ls0, 16)
    ldw = em.c[ls0]
    em.wait("tensor", ls0, ldw)
    em.wait("vector", ls0, ldw)
    banks = Banks(em, range(8))

    for mc in range(MC if mu is not None else 0):
        bk = banks.use(mc % 4)
        for c in range(CH):
            i = em.te().matmul(B.PS[bk][0:mws[mc], 0:1], w1t(c)[:, mc * 128: mc * 128 + mws[mc]],
                               mu_sb[:, c:c + 1], start=(c == 0), stop=(c == CH - 1))
            mmv = em.inc(i, "mm")
        em.wait("vector", "mm", mmv)
        i = em.ve().tensor_add(mb_sb[0:mws[mc], mc:mc + 1], mb_sb[0:mws[mc], mc:mc + 1],
                               B.PS[bk][0:mws[mc], 0:1])
        vv = em.inc(i, "vec")
        em.ve().drain(fusable=False)
        banks.done(bk, "vec", vv)
    em.wait("scalar", "vec", em.c["vec"])

    for b in range(NBLK):
        t0 = b * TB
        ls = em.batch(ldsem)
        for c in range(CH):
            i = em.sy().dma_start(out=x_t(c), in_=src[ts(c), t0:t0 + TB])
            em.inc(i, ls, 16)
        if resid is not None:
            for c in range(CH):
                i = em.sy().dma_start(out=res_t(c), in_=resid[ts(c), t0:t0 + TB])
                em.inc(i, ls, 16)
        ldv = em.c[ls]
        em.wait("tensor", ls, ldv)
        if resid is not None:
            em.wait("vector", ls, ldv)
        for mc in range(MC):
            bk = banks.use(mc % 4)
            for c in range(CH):
                i = em.te().matmul(B.PS[bk][0:mws[mc], 0:TB], w1t(c)[:, mc * 128: mc * 128 + mws[mc]],
                                   x_t(c), start=(c == 0), stop=(c == CH - 1))
                mmv = em.inc(i, "mm")
            em.wait("scalar", "mm", mmv)
            i = em.sc().activation(h_t(mc)[0:mws[mc], :], B.PS[bk][0:mws[mc], 0:TB],
                                   AF.Gelu, bias=mb_sb[0:mws[mc], mc:mc + 1])
            av = em.inc(i, "act")
            banks.done(bk, "act", av)
        em.wait("tensor", "act", em.c["act"])
        for oc in range(CH):
            bk = banks.use(4 + oc % 4)
            for mc in range(MC):
                i = em.te().matmul(B.PS[bk][:, 0:TB], w2t(mc)[0:mws[mc], ts(oc)],
                                   h_t(mc)[0:mws[mc], :], start=(mc == 0), stop=(mc == MC - 1))
                mmv = em.inc(i, "mm")
            em.wait("vector", "mm", mmv)
            if resid is not None:
                i = em.ve().scalar_tensor_tensor(o_t, B.PS[bk][:, 0:TB],
                                                 b2_sb[:, oc:oc + 1], res_t(oc),
                                                 ALU.add, ALU.add)
            else:
                i = em.ve().tensor_scalar(o_t, B.PS[bk][:, 0:TB], b2_sb[:, oc:oc + 1],
                                          None, ALU.add)
            vv = em.inc(i, "vec")
            em.ve().drain(fusable=False)
            banks.done(bk, "vec", vv)
            em.wait("gpsimd", "vec", vv)
            i = em.gp().dma_start(out=dst[ts(oc), t0:t0 + TB], in_=o_t)
            sv = em.inc(i, "st0", 16)
            em.wait("vector", "st0", sv)
        em.wait("sync", "mm", em.c["mm"])
        em.wait("sync", "vec", em.c["vec"])
    em.barrier("mlp" + tag)


def phase_l0_gates(em, B, src, ldsem="ld3"):
    nc = B.nc
    outs = {"g": B.scratch["GATE"].ap(), "a": B.scratch["ICLR"].ap(),
            "d": B.scratch["SIGT"].ap()}
    A_in = {"g": "L0_gA", "a": "L0_aA", "d": "L0_dA"}
    Bw_in = {"g": "L0_gB", "a": "L0_aB", "d": "L0_dB"}
    mu_in = {"g": "L0_mu_g", "a": "L0_mu_a", "d": "L0_mu_d"}
    b_in = {"g": None, "a": "L0_ab", "d": "L0_db"}

    aT = lambda c: B.AR1[:, c * 128: (c + 1) * 128]
    bT = B.AR1[:, 1024: 1024 + D]
    mu_sb = B.AR2[:, 12800:12808]
    ob_sb = B.AR2[:, 12808:12816]
    mb_sb = B.AR2[:, 12816:12817]
    x_t = lambda c: B.AR3[:, c * TB: (c + 1) * TB]
    inner_t = B.AR3[:, 4608: 4608 + TB]
    o_t = B.AR3[:, 5120: 5120 + TB]
    banks = Banks(em, range(8))

    for pi, p in enumerate(("g", "a", "d")):
        Aw = B.inputs[A_in[p]]
        Bw = B.inputs[Bw_in[p]]
        mu = B.inputs[mu_in[p]]
        ls0 = em.batch(ldsem)
        for c in range(CH):
            i = em.sy().dma_start(out=aT(c), in_=Aw[ts(c), :])
            em.inc(i, ls0, 16)
            i = em.sy().dma_start(out=mu_sb[:, c:c + 1], in_=col(mu, c))
            em.inc(i, ls0, 16)
        i = em.sy().dma_start(out=bT, in_=Bw.ap())
        em.inc(i, ls0, 16)
        if b_in[p]:
            ob = B.inputs[b_in[p]]
            for c in range(CH):
                i = em.sy().dma_start(out=ob_sb[:, c:c + 1], in_=col(ob, c))
                em.inc(i, ls0, 16)
        ldw = em.c[ls0]
        em.wait("tensor", ls0, ldw)
        bk = banks.use(6)
        for c in range(CH):
            i = em.te().matmul(B.PS[bk][:, 0:1], aT(c), mu_sb[:, c:c + 1],
                               start=(c == 0), stop=(c == CH - 1))
            mmv = em.inc(i, "mm")
        em.wait("vector", "mm", mmv)
        i = em.ve().tensor_copy(mb_sb, B.PS[bk][:, 0:1])
        vv = em.inc(i, "vec")
        em.ve().drain(fusable=False)
        banks.done(bk, "vec", vv)
        em.wait("scalar", "vec", vv)
        for b in range(NBLK):
            t0 = b * TB
            ls = em.batch(ldsem)
            for c in range(CH):
                i = em.sy().dma_start(out=x_t(c), in_=src[ts(c), t0:t0 + TB])
                em.inc(i, ls, 16)
            ldv = em.c[ls]
            em.wait("tensor", ls, ldv)
            bk = banks.use(0)
            for c in range(CH):
                i = em.te().matmul(B.PS[bk][:, 0:TB], aT(c), x_t(c),
                                   start=(c == 0), stop=(c == CH - 1))
                mmv = em.inc(i, "mm")
            em.wait("scalar", "mm", mmv)
            i = em.sc().activation(inner_t, B.PS[bk][:, 0:TB], AF.Identity, bias=mb_sb)
            av = em.inc(i, "act")
            banks.done(bk, "act", av)
            em.wait("tensor", "act", av)
            for oc in range(CH):
                bk = banks.use(1 + oc % 4)
                i = em.te().matmul(B.PS[bk][:, 0:TB], bT[:, ts(oc)], inner_t,
                                   start=True, stop=True)
                mmv = em.inc(i, "mm")
                em.wait("scalar", "mm", mmv)
                bias = ob_sb[:, oc:oc + 1] if b_in[p] else 0.0
                if p == "d":
                    i = em.sc().activation(o_t, B.PS[bk][:, 0:TB], AF.Tanh, bias=bias)
                    av = em.inc(i, "act")
                    em.sc().drain(fusable=False)
                    i = em.sc().activation(o_t, o_t, AF.Sigmoid)
                    av = em.inc(i, "act")
                else:
                    i = em.sc().activation(o_t, B.PS[bk][:, 0:TB], AF.Sigmoid, bias=bias)
                    av = em.inc(i, "act")
                banks.done(bk, "act", av)
                em.wait("gpsimd", "act", av)
                i = em.gp().dma_start(out=outs[p][ts(oc), t0:t0 + TB], in_=o_t)
                sv = em.inc(i, "st0", 16)
                em.wait("scalar", "st0", sv)
            em.wait("sync", "mm", em.c["mm"])
            em.wait("sync", "act", em.c["act"])
    em.wait("gpsimd", "st0", em.c["st0"])
    em.barrier("gates")


def phase_l0_prep(em, B, ldsem="ld4"):
    nc = B.nc
    Ksrc = B.scratch["KK"].ap()
    Isrc = B.scratch["ICLR"].ap()
    KN = B.scratch["KN"].ap()
    BV = B.scratch["BVEC"].ap()
    RK = B.scratch["RK"].ap()
    rem = B.inputs["L0_rem_mult"]
    mix = B.inputs["L0_iclr_mix"]

    rem_sb = B.AR2[:, 12800:12808]
    mix_sb = B.AR2[:, 12808:12816]
    om_sb = B.AR2[:, 12816:12824]
    k_t = lambda c: B.AR3[:, c * TB: (c + 1) * TB]
    i_t = lambda c: B.AR3[:, 4096 + c * TB: 4096 + (c + 1) * TB]
    rm_t = B.AR3[:, 8192: 8192 + TB]
    s_t = B.AR3[:, 8704: 8704 + TB]
    o1_t = B.AR3[:, 9216: 9216 + TB]
    o2_t = B.AR3[:, 9728: 9728 + TB]
    o3_t = B.AR3[:, 10240: 10240 + TB]
    bdiag = con(B, "bdiag64")

    ls0 = em.batch(ldsem)
    for c in range(CH):
        i = em.sy().dma_start(out=rem_sb[:, c:c + 1], in_=col(rem, c))
        em.inc(i, ls0, 16)
        i = em.sy().dma_start(out=mix_sb[:, c:c + 1], in_=col(mix, c))
        em.inc(i, ls0, 16)
    ldw = em.c[ls0]
    em.wait("vector", ls0, ldw)
    i = em.ve().tensor_scalar(om_sb, mix_sb, -1.0, 1.0, ALU.mult, ALU.add)
    vv = em.inc(i, "vec")
    em.ve().drain(fusable=False)
    banks = Banks(em, range(8))

    for b in range(NBLK):
        t0 = b * TB
        ls = em.batch(ldsem)
        for c in range(CH):
            i = em.sy().dma_start(out=k_t(c), in_=Ksrc[ts(c), t0:t0 + TB])
            em.inc(i, ls, 16)
            i = em.sy().dma_start(out=i_t(c), in_=Isrc[ts(c), t0:t0 + TB])
            em.inc(i, ls, 16)
        ldv = em.c[ls]
        em.wait("vector", ls, ldv)
        em.wait("scalar", ls, ldv)
        for c in range(CH):
            i = em.ve().tensor_scalar(rm_t, k_t(c), rem_sb[:, c:c + 1], None, ALU.mult)
            vv = em.inc(i, "vec")
            em.ve().drain(fusable=False)
            em.wait("scalar", "vec", vv)
            i = em.sc().activation(s_t, rm_t, AF.Square)
            av = em.inc(i, "act")
            em.wait("tensor", "act", av)
            bk = banks.use(c % 4)
            i = em.te().matmul(B.PS[bk][:, 0:TB], bdiag, s_t, start=True, stop=True)
            mmv = em.inc(i, "mm")
            em.wait("scalar", "mm", mmv)
            i = em.sc().activation(s_t, B.PS[bk][:, 0:TB], AF.Sqrt)
            av = em.inc(i, "act")
            banks.done(bk, "act", av)
            em.wait("vector", "act", av)
            i = em.ve().tensor_scalar_max(s_t, s_t, sca(B, "eps12"))
            em.inc(i, "vec")
            em.ve().drain(fusable=False)
            i = em.ve().reciprocal(s_t, s_t)
            em.inc(i, "vec")
            em.ve().drain(fusable=False)
            i = em.ve().tensor_mul(o1_t, rm_t, s_t)
            vv = em.inc(i, "vec")
            em.ve().drain(fusable=False)
            em.wait("gpsimd", "vec", vv)
            i = em.gp().dma_start(out=KN[ts(c), t0:t0 + TB], in_=o1_t)
            sv = em.inc(i, "st0", 16)
            i = em.ve().tensor_mul(o2_t, i_t(c), o1_t)
            vv = em.inc(i, "vec")
            em.ve().drain(fusable=False)
            em.wait("gpsimd", "vec", vv)
            i = em.gp().dma_start(out=BV[ts(c), t0:t0 + TB], in_=o2_t)
            sv = em.inc(i, "st0", 16)
            i = em.ve().tensor_mul(s_t, k_t(c), i_t(c))
            em.inc(i, "vec")
            em.ve().drain(fusable=False)
            i = em.ve().tensor_scalar(o3_t, k_t(c), om_sb[:, c:c + 1], None, ALU.mult)
            em.inc(i, "vec")
            em.ve().drain(fusable=False)
            i = em.ve().scalar_tensor_tensor(o3_t, s_t, mix_sb[:, c:c + 1], o3_t,
                                             ALU.mult, ALU.add)
            vv = em.inc(i, "vec")
            em.ve().drain(fusable=False)
            em.wait("gpsimd", "vec", vv)
            i = em.gp().dma_start(out=RK[ts(c), t0:t0 + TB], in_=o3_t)
            sv = em.inc(i, "st0", 16)
            em.wait("vector", "st0", sv)
        em.wait("sync", "vec", em.c["vec"])
    em.wait("gpsimd", "st0", em.c["st0"])
    em.barrier("prep")


def phase_deltanet(em, B, ldsem="ld5"):
    CC = 128
    NCH = T // CC
    KNs = B.scratch["KN"].ap()
    BVs = B.scratch["BVEC"].ap()
    RKs = B.scratch["RK"].ap()
    VVs = B.scratch["VV"].ap()
    RRs = B.scratch["RR"].ap()
    SGs = B.scratch["SIGT"].ap()
    DO = B.scratch["DOUT"].ap()
    ident = con(B, "ident")
    id64 = B.CON[0:64, CON_OFF["ident"]: CON_OFF["ident"] + 64]
    trilS = con(B, "trilS")
    triuS = con(B, "triuS")
    triuI = con(B, "triuI")
    ntriuI = con(B, "ntriuI")

    S_f = lambda h: B.AR2[0:64, h * 64: h * 64 + 64]
    ST_f = lambda h: B.AR2[0:64, 1024 + h * 64: 1024 + h * 64 + 64]
    zeros_t = B.AR2[0:64, 2048:2176]

    def W(par, idx, rows=64, width=CC):
        off = par * 8500 + idx * 128
        return B.AR3[0:rows, off:off + width]

    def W128(par, idx, width=CC):
        off = par * 8500 + idx * 128
        return B.AR3[0:128, off:off + width]

    IN_G, IN_KN, IN_BV, IN_RK, IN_V, IN_R = 0, 1, 2, 3, 4, 5
    T_G, T_GP, T_E1, T_E2, T_P = 6, 7, 8, 9, 10
    T_B, T_K, T_V = 11, 12, 13
    T_LM, T_LMT, T_WVB = 14, 15, 16
    T_M2T, T_M3T = 17, 18
    T_PW = 19
    T_PWT = 25
    T_RKT, T_VT, T_KTN, T_U0, T_U1 = 31, 32, 33, 34, 35
    T_UF, T_OSB, T_OUTF = 36, 37, 38

    i = em.ve().memset(B.AR2[0:64, 0:2176], 0.0)
    em.inc(i, "vec")
    em.ve().drain(fusable=False)
    em.barrier("dn0")

    banksP = [Banks(em, (0, 1, 2, 3)), Banks(em, (4, 5, 6, 7))]
    prev = [None, None]
    svec = [0] * H
    sact = [0] * H

    units = [(c, h) for c in range(NCH) for h in range(H)]
    for u, (c, h) in enumerate(units):
        par = u % 2
        banks = banksP[par]
        bk_i = [par * 4, par * 4 + 1, par * 4 + 2, par * 4 + 3]
        t0 = c * CC
        hr = h * 64
        if prev[par] is not None:
            for sem, val in prev[par]:
                em.wait("sync", sem, val)
                em.wait("vector", sem, val)
        ls = em.batch(ldsem)
        for idx, src in ((IN_G, SGs), (IN_KN, KNs), (IN_BV, BVs),
                         (IN_RK, RKs), (IN_V, VVs), (IN_R, RRs)):
            i = em.sy().dma_start(out=W(par, idx), in_=src[hr:hr + 64, t0:t0 + CC])
            em.inc(i, ls, 16)
        ldv = em.c[ls]
        em.wait("vector", ls, ldv)
        i = em.ve().tensor_tensor_scan(W(par, T_G), W(par, IN_G), zeros_t, 0.0,
                                       ALU.add, ALU.add)
        em.inc(i, "vec")
        em.ve().drain(fusable=False)
        i = em.ve().tensor_sub(W(par, T_GP), W(par, T_G), W(par, IN_G))
        vv = em.inc(i, "vec")
        em.wait("scalar", "vec", vv)
        i = em.sc().activation(W(par, T_E1), W(par, T_GP), AF.Exp, scale=-DECAY_C)
        em.inc(i, "act")
        i = em.sc().activation(W(par, T_E2), W(par, T_G), AF.Exp, scale=DECAY_C)
        em.inc(i, "act")
        i = em.sc().activation(W(par, T_P), W(par, T_G), AF.Exp, scale=-DECAY_C)
        av = em.inc(i, "act")
        em.wait("vector", "act", av)
        i = em.ve().tensor_mul(W(par, T_B), W(par, IN_BV), W(par, T_E1))
        em.inc(i, "vec")
        em.ve().drain(fusable=False)
        i = em.ve().tensor_mul(W(par, T_K), W(par, IN_KN), W(par, T_E2))
        em.inc(i, "vec")
        em.ve().drain(fusable=False)
        i = em.ve().tensor_mul(W(par, T_V), W(par, IN_V), W(par, T_E2))
        vv_bkv = em.inc(i, "vec")
        em.wait("tensor", "vec", vv_bkv)
        em.wait("tensor", "vec", svec[h])
        em.wait("tensor", "act", sact[h])

        bk = banks.use(bk_i[0])
        i = em.te().matmul(B.PS[bk][:, 0:CC], W(par, T_B), W(par, T_K),
                           start=True, stop=True)
        mv = em.inc(i, "mm")
        em.wait("vector", "mm", mv)
        i = em.ve().tensor_mul(W128(par, T_LM), B.PS[bk][:, 0:CC], trilS)
        vv = em.inc(i, "vec")
        em.ve().drain(fusable=False)
        banks.done(bk, "vec", vv)
        bk = banks.use(bk_i[1])
        i = em.te().matmul(B.PS[bk][:, 0:CC], W(par, T_K), W(par, T_B),
                           start=True, stop=True)
        mv = em.inc(i, "mm")
        em.wait("vector", "mm", mv)
        i = em.ve().tensor_mul(W128(par, T_LMT), B.PS[bk][:, 0:CC], triuS)
        vv = em.inc(i, "vec")
        em.ve().drain(fusable=False)
        banks.done(bk, "vec", vv)
        bk = banks.use(bk_i[2])
        i = em.te().matmul(B.PS[bk][:, 0:CC], W(par, T_V), W(par, T_B),
                           start=True, stop=True)
        mv = em.inc(i, "mm")
        em.wait("vector", "mm", mv)
        i = em.ve().tensor_mul(W128(par, T_WVB), B.PS[bk][:, 0:CC], triuS)
        vv_wvb = em.inc(i, "vec")
        em.ve().drain(fusable=False)
        banks.done(bk, "vec", vv_wvb)

        bk = banks.use(bk_i[3])
        i = em.te().transpose(B.PS[bk][:, 0:64], W(par, IN_RK), id64)
        mv = em.inc(i, "mm")
        em.wait("scalar", "mm", mv)
        i = em.sc().activation(W128(par, T_RKT, 64), B.PS[bk][:, 0:64], AF.Copy)
        av = em.inc(i, "act")
        banks.done(bk, "act", av)
        bk = banks.use(bk_i[0])
        i = em.te().transpose(B.PS[bk][:, 0:64], W(par, T_V), id64)
        mv = em.inc(i, "mm")
        em.wait("scalar", "mm", mv)
        i = em.sc().activation(W128(par, T_VT, 64), B.PS[bk][:, 0:64], AF.Copy)
        av = em.inc(i, "act")
        banks.done(bk, "act", av)
        bk = banks.use(bk_i[1])
        i = em.te().transpose(B.PS[bk][:, 0:64], W(par, T_K), id64)
        mv = em.inc(i, "mm")
        em.wait("scalar", "mm", mv)
        i = em.sc().mul(W128(par, T_KTN, 64), B.PS[bk][:, 0:64], -1.0)
        av_ktn = em.inc(i, "act")
        banks.done(bk, "act", av_ktn)

        bk_u = banks.use(bk_i[2])
        i = em.te().matmul(B.PS[bk_u][:, 0:64], W(par, T_B), S_f(h),
                           start=True, stop=False)
        em.inc(i, "mm")
        em.wait("tensor", "act", em.c["act"])
        i = em.te().matmul(B.PS[bk_u][:, 0:64], W128(par, T_WVB),
                           W128(par, T_RKT, 64), start=False, stop=True)
        mv = em.inc(i, "mm")
        em.wait("scalar", "mm", mv)
        i = em.sc().activation(W128(par, T_U0, 64), B.PS[bk_u][:, 0:64], AF.Copy)
        av = em.inc(i, "act")
        banks.done(bk_u, "act", av)

        curT, cur = T_LMT, T_LM
        for j in range(6):
            bk = banks.use(bk_i[3 - (j % 2)])
            i = em.te().matmul(B.PS[bk][:, 0:CC], W128(par, curT), W128(par, cur),
                               start=True, stop=True)
            mv = em.inc(i, "mm")
            em.wait("scalar", "mm", mv)
            i = em.sc().activation(W128(par, T_PW + j), B.PS[bk][:, 0:CC], AF.Copy)
            av = em.inc(i, "act")
            banks.done(bk, "act", av)
            bk = banks.use(bk_i[2 + (j % 2)])
            i = em.te().matmul(B.PS[bk][:, 0:CC], W128(par, cur), W128(par, curT),
                               start=True, stop=True)
            mv = em.inc(i, "mm")
            em.wait("scalar", "mm", mv)
            i = em.sc().activation(W128(par, T_PWT + j), B.PS[bk][:, 0:CC], AF.Copy)
            av = em.inc(i, "act")
            banks.done(bk, "act", av)
            em.wait("tensor", "act", av)
            cur, curT = T_PW + j, T_PWT + j

        ucur, unext = T_U0, T_U1
        for j in range(5, -2, -1):
            bk = banks.use(bk_i[(6 - j) % 2])
            lt = T_PWT + j if j >= 0 else T_LMT
            i = em.te().matmul(B.PS[bk][:, 0:64], W128(par, lt), W128(par, ucur, 64),
                               start=True, stop=True)
            mv = em.inc(i, "mm")
            em.wait("vector", "mm", mv)
            op = ALU.add if j >= 0 else ALU.subtract
            i = em.ve().tensor_tensor(W128(par, unext, 64), W128(par, ucur, 64),
                                      B.PS[bk][:, 0:64], op)
            vv = em.inc(i, "vec")
            em.ve().drain(fusable=False)
            banks.done(bk, "vec", vv)
            em.wait("tensor", "vec", vv)
            ucur, unext = unext, ucur
        ufin = ucur

        bk = banks.use(bk_i[0])
        i = em.te().transpose(B.PS[bk][0:64, 0:CC], W128(par, ufin, 64), ident)
        mv = em.inc(i, "mm")
        em.wait("scalar", "mm", mv)
        i = em.sc().activation(W(par, T_UF), B.PS[bk][0:64, 0:CC], AF.Copy)
        av_uf = em.inc(i, "act")
        banks.done(bk, "act", av_uf)

        bk = banks.use(bk_i[1])
        i = em.te().matmul(B.PS[bk][:, 0:CC], W(par, IN_RK), W(par, IN_R),
                           start=True, stop=True)
        mv = em.inc(i, "mm")
        em.wait("vector", "mm", mv)
        i = em.ve().tensor_mul(W128(par, T_M2T), B.PS[bk][:, 0:CC], triuI)
        vv = em.inc(i, "vec")
        em.ve().drain(fusable=False)
        banks.done(bk, "vec", vv)
        em.wait("tensor", "act", av_uf)
        bk = banks.use(bk_i[2])
        i = em.te().matmul(B.PS[bk][:, 0:CC], W(par, T_UF), W(par, IN_R),
                           start=True, stop=True)
        mv = em.inc(i, "mm")
        em.wait("vector", "mm", mv)
        i = em.ve().tensor_mul(W128(par, T_M3T), B.PS[bk][:, 0:CC], triuI)
        vv_m3 = em.inc(i, "vec")
        em.ve().drain(fusable=False)
        banks.done(bk, "vec", vv_m3)
        em.wait("tensor", "vec", vv_m3)

        bk_o = banks.use(bk_i[3])
        i = em.te().matmul(B.PS[bk_o][:, 0:64], W128(par, T_M2T),
                           W128(par, T_VT, 64), start=True, stop=False)
        em.inc(i, "mm")
        i = em.te().matmul(B.PS[bk_o][:, 0:64], W128(par, T_M3T),
                           W128(par, T_KTN, 64), start=False, stop=False)
        em.inc(i, "mm")
        i = em.te().matmul(B.PS[bk_o][:, 0:64], W(par, IN_R), ST_f(h),
                           start=False, stop=True)
        mv = em.inc(i, "mm")
        em.wait("scalar", "mm", mv)
        i = em.sc().activation(W128(par, T_OSB, 64), B.PS[bk_o][:, 0:64], AF.Copy)
        av = em.inc(i, "act")
        banks.done(bk_o, "act", av)
        em.wait("tensor", "act", av)
        bk = banks.use(bk_i[0])
        i = em.te().transpose(B.PS[bk][0:64, 0:CC], W128(par, T_OSB, 64), ident)
        mv = em.inc(i, "mm")
        em.wait("vector", "mm", mv)
        stn = em.ldn("st0", u)
        if prev[par] is not None:
            em.wait("vector", prev[par][3][0], prev[par][3][1])
        i = em.ve().tensor_mul(W(par, T_OUTF), B.PS[bk][0:64, 0:CC], W(par, T_P))
        vv = em.inc(i, "vec")
        em.ve().drain(fusable=False)
        banks.done(bk, "vec", vv)
        em.wait("gpsimd", "vec", vv)
        i = em.gp().dma_start(out=DO[hr:hr + 64, t0:t0 + CC], in_=W(par, T_OUTF))
        stv = em.inc(i, stn, 16)

        bk_s = banks.use(bk_i[1])
        i = em.te().matmul(B.PS[bk_s][0:64, 0:64], W128(par, T_VT, 64),
                           W128(par, T_RKT, 64), start=True, stop=False)
        em.inc(i, "mm")
        i = em.te().matmul(B.PS[bk_s][0:64, 0:64], W128(par, T_KTN, 64),
                           W128(par, ufin, 64), start=False, stop=True)
        mv = em.inc(i, "mm")
        em.wait("vector", "mm", mv)
        i = em.ve().tensor_add(W(par, T_GP, 64, 64), B.PS[bk_s][0:64, 0:64], S_f(h))
        em.inc(i, "vec")
        em.ve().drain(fusable=False)
        i = em.ve().tensor_scalar(S_f(h), W(par, T_GP, 64, 64),
                                  W(par, T_P)[:, 127:128], None, ALU.mult)
        vv = em.inc(i, "vec")
        em.ve().drain(fusable=False)
        banks.done(bk_s, "vec", vv)
        svec[h] = vv
        em.wait("tensor", "vec", vv)
        bk = banks.use(bk_i[2])
        i = em.te().transpose(B.PS[bk][0:64, 0:64], S_f(h), id64)
        mv = em.inc(i, "mm")
        em.wait("scalar", "mm", mv)
        i = em.sc().activation(ST_f(h), B.PS[bk][0:64, 0:64], AF.Copy)
        av = em.inc(i, "act")
        banks.done(bk, "act", av)
        sact[h] = av

        prev[par] = [("mm", em.c["mm"]), ("act", em.c["act"]),
                     ("vec", em.c["vec"]), (stn, stv)]
    em.barrier("deltanet")


def phase_l0_combine(em, B, ldsem="ld6"):
    DOs = B.scratch["DOUT"].ap()
    RRs = B.scratch["RR"].ap()
    RKs = B.scratch["RK"].ap()
    VVs = B.scratch["VV"].ap()
    GTs = B.scratch["GATE"].ap()
    OIN = B.scratch["OIN"].ap()
    bm = B.inputs["L0_bonus_mult"]
    lng = B.inputs["L0_ln_g"]
    lnb = B.inputs["L0_ln_b"]
    bdiag = con(B, "bdiag64")
    onesT = B.CON[:, CON_OFF["ones"]:CON_OFF["ones"] + 128]

    bm_sb = B.AR2[:, 12800:12808]
    lng_sb = B.AR2[:, 12808:12816]
    lnb_sb = B.AR2[:, 12816:12824]
    d_t = lambda c: B.AR3[:, c * TB: (c + 1) * TB]
    r_t = B.AR3[:, 4096: 4096 + TB]
    k_t = B.AR3[:, 4608: 4608 + TB]
    v_t = B.AR3[:, 5120: 5120 + TB]
    t_t = B.AR3[:, 5632: 5632 + TB]
    x2_t = B.AR3[:, 6144: 6144 + TB]
    mu_t = B.AR3[:, 6656: 6656 + TB]
    iv_t = B.AR3[:, 7168: 7168 + TB]
    g_t = B.AR3[:, 7680: 7680 + TB]
    o_t = B.AR3[:, 8192: 8192 + TB]

    ls0 = em.batch(ldsem)
    for c in range(CH):
        i = em.sy().dma_start(out=bm_sb[:, c:c + 1], in_=col(bm, c))
        em.inc(i, ls0, 16)
        i = em.sy().dma_start(out=lng_sb[:, c:c + 1], in_=col(lng, c))
        em.inc(i, ls0, 16)
        i = em.sy().dma_start(out=lnb_sb[:, c:c + 1], in_=col(lnb, c))
        em.inc(i, ls0, 16)
    ldw = em.c[ls0]
    em.wait("vector", ls0, ldw)
    i = em.ve().tensor_scalar(bm_sb, bm_sb, 0.2, None, ALU.mult)
    em.inc(i, "vec")
    em.ve().drain(fusable=False)
    banks = Banks(em, range(8))

    for b in range(NBLK):
        t0 = b * TB
        lsd = em.batch(ldsem)
        for c in range(CH):
            i = em.sy().dma_start(out=d_t(c), in_=DOs[ts(c), t0:t0 + TB])
            em.inc(i, lsd, 16)
        em.wait("vector", lsd, em.c[lsd])
        for c in range(CH):
            ls = em.batch(ldsem)
            i = em.sy().dma_start(out=r_t, in_=RRs[ts(c), t0:t0 + TB])
            em.inc(i, ls, 16)
            i = em.sy().dma_start(out=k_t, in_=RKs[ts(c), t0:t0 + TB])
            em.inc(i, ls, 16)
            i = em.sy().dma_start(out=v_t, in_=VVs[ts(c), t0:t0 + TB])
            em.inc(i, ls, 16)
            ldv = em.c[ls]
            em.wait("vector", ls, ldv)
            i = em.ve().tensor_mul(t_t, r_t, k_t)
            vv = em.inc(i, "vec")
            em.ve().drain(fusable=False)
            em.wait("tensor", "vec", vv)
            bk = banks.use(c % 2)
            i = em.te().matmul(B.PS[bk][:, 0:TB], bdiag, t_t, start=True, stop=True)
            mv = em.inc(i, "mm")
            em.wait("vector", "mm", mv)
            i = em.ve().tensor_mul(t_t, v_t, B.PS[bk][:, 0:TB])
            vv = em.inc(i, "vec")
            em.ve().drain(fusable=False)
            banks.done(bk, "vec", vv)
            i = em.ve().scalar_tensor_tensor(d_t(c), t_t, bm_sb[:, c:c + 1], d_t(c),
                                             ALU.mult, ALU.add)
            vv = em.inc(i, "vec")
            em.ve().drain(fusable=False)
            em.wait("sync", "vec", vv)
        em.wait("scalar", "vec", em.c["vec"])
        for c in range(CH):
            i = em.sc().activation(x2_t, d_t(c), AF.Square)
            sq = em.inc(i, "act")
            em.wait("tensor", "act", sq)
            i = em.te().matmul(B.PS[2][:, 0:TB], onesT, d_t(c),
                               start=(c == 0), stop=(c == CH - 1))
            em.inc(i, "mm")
            i = em.te().matmul(B.PS[3][:, 0:TB], onesT, x2_t,
                               start=(c == 0), stop=(c == CH - 1))
            mmv = em.inc(i, "mm")
            em.wait("scalar", "mm", mmv)
        mmv = em.c["mm"]
        em.wait("vector", "mm", mmv)
        i = em.ve().tensor_scalar(mu_t, B.PS[2][:, 0:TB], 1.0 / D, None, ALU.mult)
        em.inc(i, "vec")
        em.ve().drain(fusable=False)
        i = em.ve().tensor_tensor(iv_t, mu_t, mu_t, ALU.mult)
        em.inc(i, "vec")
        em.ve().drain(fusable=False)
        i = em.ve().scalar_tensor_tensor(iv_t, B.PS[3][:, 0:TB], 1.0 / D, iv_t,
                                         ALU.mult, ALU.subtract)
        vv = em.inc(i, "vec")
        em.wait("scalar", "vec", vv)
        i = em.sc().activation(iv_t, iv_t, AF.Sqrt, bias=sca(B, "eps_attln"))
        sv = em.inc(i, "act")
        em.wait("vector", "act", sv)
        i = em.ve().reciprocal(iv_t, iv_t)
        em.inc(i, "vec")
        em.ve().drain(fusable=False)
        for c in range(CH):
            ls = em.batch(ldsem)
            i = em.sy().dma_start(out=g_t, in_=GTs[ts(c), t0:t0 + TB])
            ldv = em.inc(i, ls, 16)
            i = em.ve().tensor_sub(o_t, d_t(c), mu_t)
            em.inc(i, "vec")
            em.ve().drain(fusable=False)
            i = em.ve().tensor_mul(o_t, o_t, iv_t)
            em.inc(i, "vec")
            em.ve().drain(fusable=False)
            i = em.ve().tensor_scalar(o_t, o_t, lng_sb[:, c:c + 1],
                                      lnb_sb[:, c:c + 1], ALU.mult, ALU.add)
            em.inc(i, "vec")
            em.ve().drain(fusable=False)
            em.wait("vector", ls, ldv)
            i = em.ve().tensor_mul(o_t, o_t, g_t)
            vv = em.inc(i, "vec")
            em.ve().drain(fusable=False)
            em.wait("gpsimd", "vec", vv)
            i = em.gp().dma_start(out=OIN[ts(c), t0:t0 + TB], in_=o_t)
            sv = em.inc(i, "st0", 16)
            em.wait("vector", "st0", sv)
            em.wait("sync", "vec", vv)
    em.wait("gpsimd", "st0", em.c["st0"])
    em.barrier("combine")


def phase_l0ffn(em, B, ldsem="ld7"):
    XN2 = B.scratch["XN2"].ap()
    XR1 = B.scratch["XR1"].ap()
    XR2 = B.scratch["XR2"].ap()
    Win = B.inputs["F0_Win"]
    binv = B.inputs["F0_bin"]
    W1 = B.inputs["F0_W1"]
    b1 = B.inputs["F0_b1"]
    W2 = B.inputs["F0_W2"]
    b2 = B.inputs["F0_b2"]
    Wout = B.inputs["F0_Wout"]
    bout = B.inputs["F0_bout"]

    win_t = lambda c: B.AR1[:, c * 64: (c + 1) * 64]
    wout_t = B.AR1[0:64, 512: 512 + D]
    w1_sb = B.AR2[0:64, 12800:12816]
    b1_sb = B.AR2[0:64, 12816:12832]
    w2_sb = B.AR2[0:64, 12832:12848]
    bin_sb = B.AR2[0:64, 12848:12849]
    b2_sb = B.AR2[0:64, 12849:12850]
    bout_sb = B.AR2[:, 12850:12858]
    x_t = lambda c: B.AR3[:, c * TB: (c + 1) * TB]
    res_t = lambda c: B.AR3[:, 4096 + c * TB: 4096 + (c + 1) * TB]
    s_sb = B.AR3[0:64, 8192: 8192 + TB]
    t_sb = B.AR3[0:64, 8704: 8704 + TB]
    ga_sb = B.AR3[0:64, 9216: 9216 + TB]
    gb_sb = B.AR3[0:64, 9728: 9728 + TB]
    o_t = B.AR3[:, 10240: 10240 + TB]

    ls0 = em.batch(ldsem)
    for c in range(CH):
        i = em.sy().dma_start(out=win_t(c), in_=Win[ts(c), :])
        em.inc(i, ls0, 16)
        i = em.sy().dma_start(out=bout_sb[:, c:c + 1], in_=col(bout, c))
        em.inc(i, ls0, 16)
    i = em.sy().dma_start(out=wout_t, in_=Wout.ap())
    em.inc(i, ls0, 16)
    i = em.sy().dma_start(out=w1_sb, in_=W1.ap())
    em.inc(i, ls0, 16)
    i = em.sy().dma_start(out=b1_sb, in_=b1.ap())
    em.inc(i, ls0, 16)
    i = em.sy().dma_start(out=w2_sb, in_=W2.ap())
    em.inc(i, ls0, 16)
    i = em.sy().dma_start(out=bin_sb, in_=colw(binv, 0, 64))
    em.inc(i, ls0, 16)
    i = em.sy().dma_start(out=b2_sb, in_=colw(b2, 0, 64))
    em.inc(i, ls0, 16)
    ldw = em.c[ls0]
    em.wait("tensor", ls0, ldw)
    em.wait("scalar", ls0, ldw)
    em.wait("vector", ls0, ldw)
    banks = Banks(em, range(8))

    for b in range(NBLK):
        t0 = b * TB
        ls = em.batch(ldsem)
        for c in range(CH):
            i = em.sy().dma_start(out=x_t(c), in_=XN2[ts(c), t0:t0 + TB])
            em.inc(i, ls, 16)
            i = em.sy().dma_start(out=res_t(c), in_=XR1[ts(c), t0:t0 + TB])
            em.inc(i, ls, 16)
        ldv = em.c[ls]
        em.wait("tensor", ls, ldv)
        em.wait("vector", ls, ldv)
        bk = banks.use(0)
        for c in range(CH):
            i = em.te().matmul(B.PS[bk][0:64, 0:TB], win_t(c), x_t(c),
                               start=(c == 0), stop=(c == CH - 1))
            mv = em.inc(i, "mm")
        em.wait("scalar", "mm", mv)
        i = em.sc().activation(s_sb, B.PS[bk][0:64, 0:TB], AF.Identity, bias=bin_sb)
        av = em.inc(i, "act")
        em.sc().drain(fusable=False)
        banks.done(bk, "act", av)
        em.wait("vector", "act", av)
        cur, nxt = ga_sb, gb_sb
        for hh in range(16):
            i = em.sc().activation(t_sb, s_sb, AF.Relu,
                                   bias=b1_sb[:, hh:hh + 1], scale=w1_sb[:, hh:hh + 1])
            av = em.inc(i, "act")
            em.wait("vector", "act", av)
            if hh == 0:
                i = em.ve().tensor_scalar(cur, t_sb, w2_sb[:, hh:hh + 1], None, ALU.mult)
            else:
                i = em.ve().scalar_tensor_tensor(nxt, t_sb, w2_sb[:, hh:hh + 1],
                                                 cur, ALU.mult, ALU.add)
                cur, nxt = nxt, cur
            vv = em.inc(i, "vec")
            em.ve().drain(fusable=False)
            em.wait("scalar", "vec", vv)
        i = em.ve().tensor_scalar(cur, cur, b2_sb, None, ALU.add)
        vv = em.inc(i, "vec")
        em.ve().drain(fusable=False)
        em.wait("tensor", "vec", vv)
        for oc in range(CH):
            bk = banks.use(1 + oc % 4)
            i = em.te().matmul(B.PS[bk][:, 0:TB], wout_t[:, ts(oc)], cur,
                               start=True, stop=True)
            mv = em.inc(i, "mm")
            em.wait("vector", "mm", mv)
            i = em.ve().scalar_tensor_tensor(o_t, B.PS[bk][:, 0:TB],
                                             bout_sb[:, oc:oc + 1], res_t(oc),
                                             ALU.add, ALU.add)
            vv = em.inc(i, "vec")
            em.ve().drain(fusable=False)
            banks.done(bk, "vec", vv)
            em.wait("gpsimd", "vec", vv)
            i = em.gp().dma_start(out=XR2[ts(oc), t0:t0 + TB], in_=o_t)
            sv = em.inc(i, "st0", 16)
            em.wait("vector", "st0", sv)
        em.wait("sync", "mm", em.c["mm"])
        em.wait("sync", "vec", em.c["vec"])
    em.wait("gpsimd", "st0", em.c["st0"])
    em.barrier("l0ffn")


def phase_proj(em, B, src, dst, Wname, ldsem, tag, resid=None, bias=None, li=None):
    Wt = B.inputs[Wname]
    w_t = lambda c: B.AR1[:, c * D: (c + 1) * D]
    x_t = lambda c: B.AR3[:, c * TB: (c + 1) * TB]
    res_t = lambda c: B.AR3[:, 4096 + c * TB: 4096 + (c + 1) * TB]
    o_t = B.AR3[:, 8192: 8192 + TB]
    ls0 = em.batch(ldsem)
    for c in range(CH):
        wsrc = Wt[ts(c), :] if li is None else Wt[li, ts(c), :]
        i = em.sy().dma_start(out=w_t(c), in_=wsrc)
        em.inc(i, ls0, 16)
    em.wait("tensor", ls0, em.c[ls0])
    banks = Banks(em, range(8))
    for b in range(NBLK):
        t0 = b * TB
        ls = em.batch(ldsem)
        for c in range(CH):
            i = em.sy().dma_start(out=x_t(c), in_=src[ts(c), t0:t0 + TB])
            em.inc(i, ls, 16)
            if resid is not None:
                i = em.sy().dma_start(out=res_t(c), in_=resid[ts(c), t0:t0 + TB])
                em.inc(i, ls, 16)
        ldv = em.c[ls]
        em.wait("tensor", ls, ldv)
        em.wait("vector", ls, ldv)
        for oc in range(CH):
            bk = banks.use(oc % 8)
            for c in range(CH):
                i = em.te().matmul(B.PS[bk][:, 0:TB], w_t(c)[:, ts(oc)], x_t(c),
                                   start=(c == 0), stop=(c == CH - 1))
                mmv = em.inc(i, "mm")
            em.wait("vector", "mm", mmv)
            if resid is not None:
                i = em.ve().tensor_add(o_t, B.PS[bk][:, 0:TB], res_t(oc))
            else:
                i = em.ve().tensor_copy(o_t, B.PS[bk][:, 0:TB])
            vv = em.inc(i, "vec")
            em.ve().drain(fusable=False)
            banks.done(bk, "vec", vv)
            em.wait("gpsimd", "vec", vv)
            i = em.gp().dma_start(out=dst[ts(oc), t0:t0 + TB], in_=o_t)
            sv = em.inc(i, "st0", 16)
            em.wait("vector", "st0", sv)
        em.wait("sync", "mm", em.c["mm"])
        em.wait("sync", "vec", em.c["vec"])
    em.wait("gpsimd", "st0", em.c["st0"])
    em.barrier("proj" + tag)


def phase_l1attn_pre(em, B, ldsem="ld8"):
    XN3 = B.scratch["XN3"].ap()
    QF = B.scratch["QF"].ap()
    KF = B.scratch["KF"].ap()
    VTM = B.scratch["VTM"].ap()
    oneh = B.inputs["c_t5oneh"]
    table = B.inputs["A1_table"]

    oneh_sb = B.AR1[0:32, 0:4096]
    tab_sb = B.AR1[0:32, 4096:4112]
    w5_sb = B.AR1[0:16, 4112: 4112 + 4096]
    ls0 = em.batch(ldsem)
    i = em.sy().dma_start(out=oneh_sb, in_=oneh.ap())
    em.inc(i, ls0, 16)
    i = em.sy().dma_start(out=tab_sb, in_=table.ap())
    em.inc(i, ls0, 16)
    ldv = em.c[ls0]
    em.wait("tensor", ls0, ldv)
    banks = Banks(em, range(8))
    for j in range(8):
        bk = banks.use(j % 4)
        i = em.te().matmul(B.PS[bk][0:16, 0:TB], tab_sb, oneh_sb[:, j * TB:(j + 1) * TB],
                           start=True, stop=True)
        mv = em.inc(i, "mm")
        em.wait("scalar", "mm", mv)
        i = em.sc().activation(w5_sb[:, j * TB:(j + 1) * TB], B.PS[bk][0:16, 0:TB],
                               AF.Copy)
        av = em.inc(i, "act")
        banks.done(bk, "act", av)
    avw = em.c["act"]
    em.wait("gpsimd", "act", avw)
    i = em.gp().dma_start(out=B.scratch["W5"].ap(), in_=B.AR1[0:16, 4112: 4112 + 4096])
    wv5 = em.inc(i, "st1", 16)
    em.wait("gpsimd", "st1", wv5)
    for h in range(H):
        wsh = B.scratch[f"WSH{h}"].reshape([128, 4226])
        i = em.gp().dma_start(out=wsh[:, 0:4095],
                              in_=B.scratch["W5"][h:h + 1, 0:4095].broadcast_to([128, 4095]))
        em.inc(i, "st1", 16)

    wq_t = lambda c: B.AR2[:, c * D: (c + 1) * D]
    x_t = lambda c: B.AR3[:, c * TB: (c + 1) * TB]
    o_t = B.AR3[:, 4096: 4096 + TB]
    for b in range(NBLK):
        t0 = b * TB
        lsx = em.batch(ldsem)
        for c in range(CH):
            i = em.sy().dma_start(out=x_t(c), in_=XN3[ts(c), t0:t0 + TB])
            em.inc(i, lsx, 16)
        ldv = em.c[lsx]
        em.wait("tensor", lsx, ldv)
        for wi, (wn, dst) in enumerate((("A1_Wq", QF), ("A1_Wk", KF))):
            Wh = B.inputs[wn]
            em.wait("sync", "mm", em.c["mm"])
            ls = em.batch(ldsem)
            for c in range(CH):
                i = em.sy().dma_start(out=wq_t(c) if wi == 0 else wq_t(c),
                                      in_=Wh[ts(c), :])
                em.inc(i, ls, 16)
            ldv = em.c[ls]
            em.wait("tensor", ls, ldv)
            for oc in range(CH):
                bk = banks.use(oc % 4)
                for c in range(CH):
                    i = em.te().matmul(B.PS[bk][:, 0:TB], wq_t(c)[:, ts(oc)], x_t(c),
                                       start=(c == 0), stop=(c == CH - 1))
                    mmv = em.inc(i, "mm")
                em.wait("scalar", "mm", mmv)
                i = em.sc().activation(o_t, B.PS[bk][:, 0:TB], AF.Copy)
                av = em.inc(i, "act")
                banks.done(bk, "act", av)
                em.wait("gpsimd", "act", av)
                i = em.gp().dma_start(out=dst[ts(oc), t0:t0 + TB], in_=o_t)
                sv = em.inc(i, "st1", 16)
                em.wait("scalar", "st1", sv)
        Wv = B.inputs["A1_Wv"]
        em.wait("sync", "mm", em.c["mm"])
        ls = em.batch(ldsem)
        for c in range(CH):
            i = em.sy().dma_start(out=wq_t(c), in_=Wv[ts(c), :])
            em.inc(i, ls, 16)
        ldv = em.c[ls]
        em.wait("tensor", ls, ldv)
        for tt in range(TB // 128):
            for vb in range(2):
                bk = banks.use(4 + (tt * 2 + vb) % 4)
                for c in range(CH):
                    i = em.te().matmul(B.PS[bk][:, 0:TB], x_t(c)[:, tt * 128:(tt + 1) * 128],
                                       wq_t(c)[:, vb * TB:(vb + 1) * TB],
                                       start=(c == 0), stop=(c == CH - 1))
                    mmv = em.inc(i, "mm")
                em.wait("scalar", "mm", mmv)
                i = em.sc().activation(o_t, B.PS[bk][:, 0:TB], AF.Copy)
                av = em.inc(i, "act")
                banks.done(bk, "act", av)
                em.wait("gpsimd", "act", av)
                i = em.gp().dma_start(
                    out=VTM[t0 + tt * 128: t0 + (tt + 1) * 128, vb * TB:(vb + 1) * TB],
                    in_=o_t)
                sv = em.inc(i, "st1", 16)
                em.wait("scalar", "st1", sv)
        em.wait("sync", "mm", em.c["mm"])
    em.wait("gpsimd", "st1", em.c["st1"])
    em.barrier("l1pre")


def phase_l1attn_core(em, B, ldsem="ld9"):
    QF = B.scratch["QF"].ap()
    KF = B.scratch["KF"].ap()
    VTM = B.scratch["VTM"].ap()
    OATT = B.scratch["OATT"].ap()
    ident = con(B, "ident")
    NT = T // 128

    kf_sb = B.AR1[0:64, 0:T]
    vtm_sb = lambda kt: B.AR1[:, 2048 + kt * 64: 2048 + (kt + 1) * 64]
    q_sb = B.AR2[0:64, 0:128]
    bias_sb = lambda j: B.AR2[:, 128 + j * TB: 128 + (j + 1) * TB]
    sc_sb = lambda j: B.AR2[:, 2176 + j * TB: 2176 + (j + 1) * TB]
    att_sb = lambda kt: B.AR2[:, 4224 + kt * 128: 4224 + (kt + 1) * 128]
    red_sb = B.AR2[:, 6272: 6272 + 8]
    of_sb = B.AR3[0:64, 12000: 12000 + 128]

    banks = Banks(em, range(8))
    prevu = None
    for h in range(H):
        hr = h * 64
        wsh = B.scratch[f"WSH{h}"]
        if prevu is not None:
            for sem, val in prevu:
                em.wait("sync", sem, val)
        lsh = em.batch(ldsem)
        i = em.sy().dma_start(out=kf_sb, in_=KF[hr:hr + 64, :])
        em.inc(i, lsh, 16)
        for kt in range(NT):
            i = em.sy().dma_start(out=vtm_sb(kt), in_=VTM[kt * 128:(kt + 1) * 128, hr:hr + 64])
            em.inc(i, lsh, 16)
        ldh = em.c[lsh]
        for qt in range(NT):
            q0 = qt * 128
            ls = em.batch(ldsem)
            i = em.sy().dma_start(out=q_sb, in_=QF[hr:hr + 64, q0:q0 + 128])
            em.inc(i, ls, 16)
            i = em.sy().dma_start(out=B.AR2[:, 128:128 + 4 * TB],
                                  in_=mkAP(B.scratch[f"WSH{h}"], 2047 - q0,
                                           [[4225, 128], [1, 4 * TB]]))
            em.inc(i, ls, 16)
            ldv = em.c[ls]
            em.wait("tensor", ls, ldv)
            em.wait("tensor", lsh, ldh)
            em.wait("vector", ls, ldv)
            for j in range(4):
                bk = banks.use(j)
                i = em.te().matmul(B.PS[bk][:, 0:TB], q_sb, kf_sb[:, j * TB:(j + 1) * TB],
                                   start=True, stop=True)
                mmv = em.inc(i, "mm")
                em.wait("vector", "mm", mmv)
                i = em.ve().tensor_add(sc_sb(j), B.PS[bk][:, 0:TB], bias_sb(j))
                vv = em.inc(i, "vec")
                em.ve().drain(fusable=False)
                banks.done(bk, "vec", vv)
            i = em.ve().tensor_reduce(red_sb[:, 0:1], B.AR2[:, 2176: 2176 + T],
                                      AX.X, ALU.max, negate=True)
            vv = em.inc(i, "vec")
            em.ve().drain(fusable=False)
            em.wait("scalar", "vec", vv)
            for j in range(4):
                i = em.sc().activation(sc_sb(j), sc_sb(j), AF.Exp,
                                       bias=red_sb[:, 0:1],
                                       accum_out=red_sb[:, 2 + j: 3 + j])
                av = em.inc(i, "act")
                em.sc().drain(fusable=False)
            em.wait("vector", "act", em.c["act"])
            i = em.ve().tensor_reduce(red_sb[:, 6:7], red_sb[:, 2:6], AX.X, ALU.add)
            vv = em.inc(i, "vec")
            em.ve().drain(fusable=False)
            i = em.ve().reciprocal(red_sb[:, 6:7], red_sb[:, 6:7])
            vv = em.inc(i, "vec")
            em.ve().drain(fusable=False)
            em.wait("scalar", "vec", vv)
            for j in range(4):
                i = em.sc().mul(sc_sb(j), sc_sb(j), red_sb[:, 6:7])
                av = em.inc(i, "act")
                em.sc().drain(fusable=False)
            avn = em.c["act"]
            em.wait("tensor", "act", avn)
            for kt in range(NT):
                bk = banks.use(4 + kt % 2)
                i = em.te().transpose(B.PS[bk][:, 0:128],
                                      B.AR2[:, 2176 + kt * 128: 2176 + (kt + 1) * 128],
                                      ident)
                mv = em.inc(i, "mm")
                if kt % 2 == 0:
                    em.wait("scalar", "mm", mv)
                    i = em.sc().activation(att_sb(kt), B.PS[bk][:, 0:128], AF.Copy)
                    av = em.inc(i, "act")
                    banks.done(bk, "act", av)
                else:
                    em.wait("vector", "mm", mv)
                    i = em.ve().tensor_copy(att_sb(kt), B.PS[bk][:, 0:128])
                    av = em.inc(i, "vec")
                    em.ve().drain(fusable=False)
                    banks.done(bk, "vec", av)
            em.wait("tensor", "act", em.c["act"])
            em.wait("tensor", "vec", em.c["vec"])
            em.wait("tensor", lsh, ldh)
            bk_o = banks.use(6 + qt % 2)
            for kt in range(NT):
                i = em.te().matmul(B.PS[bk_o][0:64, 0:128], vtm_sb(kt), att_sb(kt),
                                   start=(kt == 0), stop=(kt == NT - 1))
                mv = em.inc(i, "mm")
            em.wait("vector", "mm", mv)
            i = em.ve().tensor_copy(of_sb, B.PS[bk_o][0:64, 0:128])
            vv = em.inc(i, "vec")
            em.ve().drain(fusable=False)
            banks.done(bk_o, "vec", vv)
            em.wait("gpsimd", "vec", vv)
            i = em.gp().dma_start(out=OATT[hr:hr + 64, q0:q0 + 128], in_=of_sb)
            sv = em.inc(i, "st0", 16)
            em.wait("vector", "st0", sv)
            em.wait("sync", "mm", em.c["mm"])
            em.wait("sync", "act", em.c["act"])
            em.wait("sync", "vec", em.c["vec"])
        prevu = [("mm", em.c["mm"]), ("act", em.c["act"]), ("vec", em.c["vec"])]
    em.wait("gpsimd", "st0", em.c["st0"])
    em.barrier("l1core")


def phase_ffqkv(em, B, li, src, ldsem="ld2"):
    Wqkv = B.inputs["FF_Wqkv"]
    QF2 = B.scratch["QF2"].ap()
    KF2 = B.scratch["KF2"].ap()
    V65 = B.scratch["V65"].ap()
    w_t = lambda c: B.AR1[:, c * D: (c + 1) * D]
    x_t = lambda c: B.AR3[:, c * TB: (c + 1) * TB]
    xs_t = lambda c: B.AR3[:, (4 + c) * TB: (5 + c) * TB]
    o_t = B.AR3[:, 8192: 8192 + TB]
    banks = Banks(em, range(8))
    for wi in range(3):
        em.wait("sync", "mm", em.c["mm"])
        lsw = em.batch(ldsem)
        for c in range(CH):
            i = em.sy().dma_start(out=w_t(c), in_=Wqkv[li, ts(c), wi * D:(wi + 1) * D])
            em.inc(i, lsw, 16)
        ldw = em.c[lsw]
        em.wait("tensor", lsw, ldw)
        for b in range(NBLK):
            t0 = b * TB
            lsx = em.batch(ldsem)
            for c in range(CH):
                if c >= 4:
                    if b == 0:
                        em.wait("vector", "mm", em.c["mm"])
                        i = em.ve().memset(xs_t(c - 4)[:, 0:1], 0.0)
                        em.inc(i, "vec")
                        em.ve().drain(fusable=False)
                        i = em.sy().dma_start(out=xs_t(c - 4)[:, 1:TB],
                                              in_=src[ts(c), 0:TB - 1])
                    else:
                        i = em.sy().dma_start(out=xs_t(c - 4),
                                              in_=src[ts(c), t0 - 1:t0 + TB - 1])
                else:
                    i = em.sy().dma_start(out=x_t(c), in_=src[ts(c), t0:t0 + TB])
                em.inc(i, lsx, 16)
            ldv = em.c[lsx]
            em.wait("tensor", lsx, ldv)
            em.wait("tensor", "vec", em.c["vec"])
            xin = lambda c: x_t(c) if c < 4 else xs_t(c - 4)
            if wi < 2:
                dst = QF2 if wi == 0 else KF2
                for oc in range(CH):
                    bk = banks.use(oc % 4)
                    for c in range(CH):
                        i = em.te().matmul(B.PS[bk][:, 0:TB], w_t(c)[:, ts(oc)], xin(c),
                                           start=(c == 0), stop=(c == CH - 1))
                        mmv = em.inc(i, "mm")
                    em.wait("scalar", "mm", mmv)
                    i = em.sc().activation(o_t, B.PS[bk][:, 0:TB], AF.Copy)
                    av = em.inc(i, "act")
                    banks.done(bk, "act", av)
                    em.wait("gpsimd", "act", av)
                    i = em.gp().dma_start(out=dst[ts(oc), t0:t0 + TB], in_=o_t)
                    sv = em.inc(i, "st1", 16)
                    em.wait("scalar", "st1", sv)
            else:
                for tt in range(TB // 128):
                    for vb in range(2):
                        bk = banks.use(4 + (tt * 2 + vb) % 4)
                        for c in range(CH):
                            i = em.te().matmul(B.PS[bk][:, 0:TB],
                                               xin(c)[:, tt * 128:(tt + 1) * 128],
                                               w_t(c)[:, vb * TB:(vb + 1) * TB],
                                               start=(c == 0), stop=(c == CH - 1))
                            mmv = em.inc(i, "mm")
                        em.wait("scalar", "mm", mmv)
                        i = em.sc().activation(
                            mkAP(B.AR3, 8192, [[16400, 128], [65, 8], [1, 64]]),
                            B.PS[bk][:, 0:TB], AF.Copy)
                        av = em.inc(i, "act")
                        banks.done(bk, "act", av)
                        em.wait("gpsimd", "act", av)
                        i = em.gp().dma_start(
                            out=mkAP(B.scratch["V65"],
                                     (t0 + tt * 128) * (16 * 65) + vb * 8 * 65,
                                     [[16 * 65, 128], [1, 8 * 65]]),
                            in_=B.AR3[:, 8192: 8192 + 8 * 65])
                        sv = em.inc(i, "st1", 16)
                        em.wait("scalar", "st1", sv)
            em.wait("sync", "mm", em.c["mm"])
    em.barrier(f"ffqkv{li}")


def phase_rotnorm(em, B, li, ldsem="ld3"):
    rotp = con(B, "rotperm")
    bdiag = con(B, "bdiag64")
    sc_sb = B.AR2[:, 12800:12802]
    x_sb = B.AR3[:, 0:TB]
    sq_sb = B.AR3[:, 512:512 + TB]
    n_sb = B.AR3[:, 1024:1024 + TB]
    qn_sb = B.AR3[:, 1536:1536 + TB]
    qa_sb = B.AR3[:, 2048:2048 + TB]
    qo_sb = B.AR3[:, 2560:2560 + TB]
    ls0 = em.batch(ldsem)
    for half in range(2):
        for wi, name in ((0, "FF_qs"), (1, "FF_ks")):
            i = em.sy().dma_start(out=sc_sb[half * 64:(half + 1) * 64, wi:wi + 1],
                                  in_=B.inputs[name][li:li + 1, :].transpose([1, 0]))
            em.inc(i, ls0, 16)
    ldw = em.c[ls0]
    em.wait("vector", ls0, ldw)
    banks = Banks(em, range(8))
    for wi, (srcn, dstn, cs_n, ss_n) in enumerate(
            (("QF2", "QR", "csq", "ssq"), ("KF2", "KR", "csk", "ssk"))):
        srch = B.scratch[srcn].ap()
        dsth = B.scratch[dstn].ap()
        for c in range(CH):
            for b in range(NBLK):
                t0 = b * TB
                ls = em.batch(ldsem)
                i = em.sy().dma_start(out=x_sb, in_=srch[ts(c), t0:t0 + TB])
                ldv = em.inc(i, ls, 16)
                em.wait("scalar", ls, ldv)
                i = em.sc().activation(sq_sb, x_sb, AF.Square)
                av = em.inc(i, "act")
                em.wait("tensor", "act", av)
                bk = banks.use(0 if wi == 0 else 2)
                i = em.te().matmul(B.PS[bk][:, 0:TB], bdiag, sq_sb, start=True, stop=True)
                mv = em.inc(i, "mm")
                em.wait("scalar", "mm", mv)
                i = em.sc().activation(n_sb, B.PS[bk][:, 0:TB], AF.Sqrt)
                av = em.inc(i, "act")
                banks.done(bk, "act", av)
                em.wait("vector", "act", av)
                i = em.ve().tensor_scalar_max(n_sb, n_sb, sca(B, "eps12"))
                em.inc(i, "vec")
                em.ve().drain(fusable=False)
                i = em.ve().reciprocal(n_sb, n_sb)
                em.inc(i, "vec")
                em.ve().drain(fusable=False)
                em.wait("vector", ls, ldv)
                i = em.ve().tensor_mul(qn_sb, x_sb, n_sb)
                em.inc(i, "vec")
                em.ve().drain(fusable=False)
                i = em.ve().tensor_scalar(qn_sb, qn_sb, sc_sb[:, wi:wi + 1], None,
                                          ALU.mult)
                vv = em.inc(i, "vec")
                em.ve().drain(fusable=False)
                em.wait("tensor", "vec", vv)
                bk = banks.use(1 if wi == 0 else 3)
                i = em.te().matmul(B.PS[bk][:, 0:TB], rotp, qn_sb, start=True, stop=True)
                mv = em.inc(i, "mm")
                i = em.ve().tensor_mul(qa_sb, qn_sb, con(B, cs_n)[:, t0:t0 + TB])
                em.inc(i, "vec")
                em.ve().drain(fusable=False)
                em.wait("vector", "mm", mv)
                i = em.ve().tensor_mul(qo_sb, B.PS[bk][:, 0:TB],
                                       con(B, ss_n)[:, t0:t0 + TB])
                vv = em.inc(i, "vec")
                em.ve().drain(fusable=False)
                banks.done(bk, "vec", vv)
                i = em.ve().tensor_add(qo_sb, qo_sb, qa_sb)
                vv = em.inc(i, "vec")
                em.ve().drain(fusable=False)
                em.wait("gpsimd", "vec", vv)
                i = em.gp().dma_start(out=dsth[ts(c), t0:t0 + TB], in_=qo_sb)
                sv = em.inc(i, "st1", 16)
                em.wait("vector", "st1", sv)
                em.wait("sync", "vec", vv)
                em.wait("scalar", "vec", vv)
    em.wait("gpsimd", "st1", em.c["st1"])
    em.barrier(f"rotnorm{li}")


def phase_local_attn(em, B, li, ldsem="ld4"):
    QR = B.scratch["QR"].ap()
    KR = B.scratch["KR"].ap()
    OA2 = B.scratch["OA2"].ap()
    NT = T // 128
    trilI = con(B, "trilI")
    triuI = con(B, "triuI")
    onesT = B.CON[:, CON_OFF["ones"]:CON_OFF["ones"] + 128]

    kstr = lambda c: B.AR2[:, c * 384: (c + 1) * 384]
    vstr = lambda j: B.AR2[:, 3072 + j * 1040: 3072 + (j + 1) * 1040]
    qstr = lambda c: B.AR2[:, 6192 + c * 128: 6192 + (c + 1) * 128]
    att_sb = lambda par, sb: B.AR2[:, 7216 + par * 384 + sb * 128:
                                   7216 + par * 384 + (sb + 1) * 128]
    den_sb = B.AR2[:, 8000: 8000 + 2048]
    oa_sb = B.AR2[:, 10048: 10048 + 2048]

    def k_h(h, j):
        return kstr(h // 2)[(h % 2) * 64:(h % 2) * 64 + 64, j * 128:(j + 1) * 128]

    def q_h(h):
        return qstr(h // 2)[(h % 2) * 64:(h % 2) * 64 + 64, :]

    def v_h(h, j):
        return vstr(j)[:, h * 65: h * 65 + 64]

    banks = Banks(em, range(8))
    prevq = None
    for qt in range(NT):
        q0 = qt * 128
        nsb = min(qt, 2) + 1
        sb0 = 2 - (nsb - 1)
        if prevq is not None:
            for sem, val in prevq:
                em.wait("sync", sem, val)
        ls = em.batch(ldsem)
        for c in range(CH):
            i = em.sy().dma_start(out=kstr(c)[:, sb0 * 128: 384],
                                  in_=KR[ts(c), q0 - (nsb - 1) * 128: q0 + 128])
            em.inc(i, ls, 16)
            i = em.sy().dma_start(out=qstr(c), in_=QR[ts(c), q0:q0 + 128])
            em.inc(i, ls, 16)
        for j in range(nsb):
            s0 = q0 - 128 * (nsb - 1 - j)
            i = em.sy().dma_start(out=vstr(sb0 + j),
                                  in_=B.scratch["V65"][s0:s0 + 128, :])
            em.inc(i, ls, 16)
        ldv = em.c[ls]
        em.wait("tensor", ls, ldv)
        mask_vv = [0, 0]

        def emit_avden(h):
            em.wait("tensor", "vec", mask_vv[h % 2])
            bk_a = banks.use(h // 4)
            for j in range(nsb):
                i = em.te().matmul(B.PS[bk_a][0:64, (h % 4) * 128:(h % 4 + 1) * 128],
                                   v_h(h, sb0 + j), att_sb(h % 2, j),
                                   start=(j == 0), stop=(j == nsb - 1))
                em.inc(i, "mm")
            bk_d = banks.use(4 + h % 2)
            for j in range(nsb):
                i = em.te().matmul(B.PS[bk_d][:, 0:128], onesT, att_sb(h % 2, j),
                                   start=(j == 0), stop=(j == nsb - 1))
                mv = em.inc(i, "mm")
            em.wait("vector", "mm", mv)
            i = em.ve().reciprocal(den_sb[:, h * 128:(h + 1) * 128],
                                   B.PS[bk_d][:, 0:128])
            dv = em.inc(i, "vec")
            em.ve().drain(fusable=False)
            banks.done(bk_d, "vec", dv)

        for h in range(H):
            bk_s = banks.use(6 + h % 2)
            for j in range(nsb):
                i = em.te().matmul(B.PS[bk_s][:, j * 128:(j + 1) * 128],
                                   k_h(h, sb0 + j), q_h(h), start=True, stop=True)
                mv = em.inc(i, "mm")
            if h > 0:
                emit_avden(h - 1)
            em.wait("scalar", "mm", mv)
            em.wait("scalar", "mm", mv)
            em.wait("scalar", "mm", em.c["mm"] if h > 1 else 0)
            for j in range(nsb):
                i = em.sc().activation(att_sb(h % 2, j),
                                       B.PS[bk_s][:, j * 128:(j + 1) * 128],
                                       AF.Exp, bias=sca(B, "negB"), scale=8.0)
                av = em.inc(i, "act")
                em.sc().drain(fusable=False)
            banks.done(bk_s, "act", av)
            em.wait("vector", "act", av)
            for j in range(nsb):
                sb_abs = sb0 + j
                if sb_abs == 0:
                    i = em.ve().tensor_mul(att_sb(h % 2, j), att_sb(h % 2, j), trilI)
                    em.inc(i, "vec")
                    em.ve().drain(fusable=False)
                elif sb_abs == 2:
                    i = em.ve().tensor_mul(att_sb(h % 2, j), att_sb(h % 2, j), triuI)
                    em.inc(i, "vec")
                    em.ve().drain(fusable=False)
            mask_vv[h % 2] = em.c["vec"]
        emit_avden(H - 1)
        em.wait("vector", "mm", em.c["mm"])
        for h in range(H):
            i = em.ve().tensor_mul(oa_sb[0:64, h * 128:(h + 1) * 128],
                                   B.PS[h // 4][0:64, (h % 4) * 128:(h % 4 + 1) * 128],
                                   den_sb[0:64, h * 128:(h + 1) * 128])
            vv = em.inc(i, "vec")
            em.ve().drain(fusable=False)
            em.wait("gpsimd", "vec", vv)
            i = em.gp().dma_start(out=OA2[h * 64:(h + 1) * 64, q0:q0 + 128],
                                  in_=oa_sb[0:64, h * 128:(h + 1) * 128])
            sv = em.inc(i, "st1", 16)
        for bb in range(4):
            banks.done(bb, "vec", em.c["vec"])
        em.wait("vector", "st1", em.c["st1"])
        prevq = [("mm", em.c["mm"]), ("act", em.c["act"]), ("vec", em.c["vec"])]
    em.barrier(f"latt{li}")


def phase_geglu(em, B, li, src, resid, dst, ldsem="ld6"):
    TBF = 256
    NJ = 22
    W1T = B.inputs["FF_W1T"]
    b1T = B.inputs["FF_b1T"]
    b2T = B.inputs["FF_b2T"]
    HT = T // 2

    xh = lambda c: B.AR3[:, c * HT: (c + 1) * HT]
    oacc = lambda c: B.AR2[:, c * HT: (c + 1) * HT]
    wimg = lambda sl: B.AR1[:, sl * 3072: (sl + 1) * 3072]
    w1a = lambda sl, c: B.AR1[:, sl * 3072 + c * 128: sl * 3072 + (c + 1) * 128]
    w1g = lambda sl, c: B.AR1[:, sl * 3072 + 1024 + c * 128: sl * 3072 + 1024 + (c + 1) * 128]
    w2t = lambda sl, oc: B.AR1[:, sl * 3072 + 2048 + oc * 128: sl * 3072 + 2048 + (oc + 1) * 128]
    g_sb = B.AR3[:, 8192: 8192 + TBF]
    y_sb = lambda ji: B.AR3[:, 8448 + ji * TBF: 8448 + (ji + 1) * TBF]
    b1_sb = B.AR3[:, 9472: 9472 + 44]
    b2_sb = B.AR3[:, 9516: 9516 + 8]
    res_t = B.AR3[:, 9524: 9524 + HT]
    o_t = B.AR3[:, 10548: 10548 + HT]

    ls0 = em.batch(ldsem)
    i = em.sy().dma_start(out=b1_sb, in_=b1T[li, :, :])
    em.inc(i, ls0, 16)
    i = em.sy().dma_start(out=b2_sb, in_=b2T[li, :, :])
    em.inc(i, ls0, 16)
    ldb = em.c[ls0]
    em.wait("scalar", ls0, ldb)
    em.wait("vector", ls0, ldb)
    banks = Banks(em, range(8))
    jgs = [list(range(g * 4, min(g * 4 + 4, NJ))) for g in range((NJ + 3) // 4)]
    slot_guard = [0, 0, 0, 0]

    for half in range(2):
        h0 = half * HT
        em.wait("sync", "mm", em.c["mm"])
        lsx = em.batch(ldsem)
        for c in range(CH):
            i = em.sy().dma_start(out=xh(c), in_=src[ts(c), h0:h0 + HT])
            em.inc(i, lsx, 16)
        ldx = em.c[lsx]
        em.wait("tensor", lsx, ldx)
        em.wait("vector", lsx, ldx)
        i = em.ve().memset(B.AR2[:, 0:8192], 0.0)
        mz = em.inc(i, "vec")
        em.ve().drain(fusable=False)
        em.wait("tensor", "vec", mz)
        for gi, jg in enumerate(jgs):
            lsw = em.batch(ldsem)
            for j in jg:
                em.wait("sync", "mm", slot_guard[j % 4])
                i = em.sy().dma_start(out=wimg(j % 4), in_=W1T[li, j, :, :])
                em.inc(i, lsw, 16)
            ldw = em.c[lsw]
            em.wait("tensor", lsw, ldw)
            for b in range(4):
                t0 = b * TBF
                first_j, last_j = jg[0], jg[-1]
                for ji, j in enumerate(jg):
                    sl = j % 4
                    bk_a = banks.use(4 + 2 * (ji % 2))
                    for c in range(CH):
                        i = em.te().matmul(B.PS[bk_a][:, 0:TBF], w1a(sl, c),
                                           xh(c)[:, t0:t0 + TBF],
                                           start=(c == 0), stop=(c == CH - 1))
                        mmv = em.inc(i, "mm")
                    bk_g = banks.use(5 + 2 * (ji % 2))
                    for c in range(CH):
                        i = em.te().matmul(B.PS[bk_g][:, 0:TBF], w1g(sl, c),
                                           xh(c)[:, t0:t0 + TBF],
                                           start=(c == 0), stop=(c == CH - 1))
                        mmv = em.inc(i, "mm")
                    em.wait("scalar", "mm", mmv)
                    em.wait("scalar", "vec", em.c["vec"])
                    i = em.sc().activation(g_sb, B.PS[bk_g][:, 0:TBF], AF.Gelu,
                                           bias=b1_sb[:, 22 + j: 23 + j])
                    av = em.inc(i, "act")
                    banks.done(bk_g, "act", av)
                    em.wait("vector", "act", av)
                    i = em.ve().scalar_tensor_tensor(y_sb(ji), B.PS[bk_a][:, 0:TBF],
                                                     b1_sb[:, j:j + 1], g_sb,
                                                     ALU.add, ALU.mult)
                    vv = em.inc(i, "vec")
                    em.ve().drain(fusable=False)
                    banks.done(bk_a, "vec", vv)
                em.wait("tensor", "vec", vv)
                for ji, j in enumerate(jg):
                    sl = j % 4
                    for oc in range(CH):
                        bk_o = oc // 2
                        if j == first_j and oc % 2 == 0:
                            banks.use(bk_o)
                        i = em.te().matmul(
                            B.PS[bk_o][:, (oc % 2) * TBF: (oc % 2 + 1) * TBF],
                            w2t(sl, oc), y_sb(ji),
                            start=(j == first_j and oc % 2 == 0),
                            stop=(j == last_j and oc % 2 == 1))
                        mmv = em.inc(i, "mm")
                    if b == 3:
                        slot_guard[sl] = em.c["mm"]
                em.wait("vector", "mm", em.c["mm"])
                for bk_o in range(4):
                    i = em.ve().tensor_add(
                        B.AR2[:, bk_o * 2048 + t0: bk_o * 2048 + t0 + TBF],
                        B.AR2[:, bk_o * 2048 + t0: bk_o * 2048 + t0 + TBF],
                        B.PS[bk_o][:, 0:TBF])
                    em.inc(i, "vec")
                    em.ve().drain(fusable=False)
                    i = em.ve().tensor_add(
                        B.AR2[:, (bk_o * 2 + 1) * 1024 + t0: (bk_o * 2 + 1) * 1024 + t0 + TBF],
                        B.AR2[:, (bk_o * 2 + 1) * 1024 + t0: (bk_o * 2 + 1) * 1024 + t0 + TBF],
                        B.PS[bk_o][:, TBF: 2 * TBF])
                    vv = em.inc(i, "vec")
                    em.ve().drain(fusable=False)
                    banks.done(bk_o, "vec", vv)
        lsr = em.batch(ldsem)
        for c in range(CH):
            i = em.sy().dma_start(out=res_t, in_=resid[ts(c), h0:h0 + HT])
            lv = em.inc(i, lsr, 16)
            em.wait("vector", lsr, lv)
            i = em.ve().scalar_tensor_tensor(o_t, oacc(c), b2_sb[:, c:c + 1],
                                             res_t, ALU.add, ALU.add)
            vv = em.inc(i, "vec")
            em.ve().drain(fusable=False)
            em.wait("gpsimd", "vec", vv)
            i = em.gp().dma_start(out=dst[ts(c), h0:h0 + HT], in_=o_t)
            sv = em.inc(i, "st1", 16)
            em.wait("vector", "st1", sv)
            em.wait("sync", "vec", vv)
        em.wait("sync", "st1", em.c["st1"])
    em.barrier(f"geglu{li}")


def phase_add(em, B, asrc, bsrc, dst, ldsem="ld1"):
    a_t = B.AR3[:, 0:T]
    b_t = B.AR3[:, 2048:2048 + T]
    o_t = B.AR3[:, 4096:4096 + T]
    for c in range(CH):
        ls = em.batch(ldsem)
        i = em.sy().dma_start(out=a_t, in_=asrc[ts(c), :])
        em.inc(i, ls, 16)
        i = em.sy().dma_start(out=b_t, in_=bsrc[ts(c), :])
        em.inc(i, ls, 16)
        ldv = em.c[ls]
        em.wait("vector", ls, ldv)
        i = em.ve().tensor_add(o_t, a_t, b_t)
        vv = em.inc(i, "vec")
        em.ve().drain(fusable=False)
        em.wait("gpsimd", "vec", vv)
        i = em.gp().dma_start(out=dst[ts(c), :], in_=o_t)
        sv = em.inc(i, "st0", 16)
        em.wait("vector", "st0", sv)
        em.wait("sync", "vec", vv)
    em.barrier("addout")




def phase_add(em, B, asrc, bsrc, dst, ldsem="ld1"):
    a_t = B.AR3[:, 0:T]
    b_t = B.AR3[:, 2048:2048 + T]
    o_t = B.AR3[:, 4096:4096 + T]
    for c in range(CH):
        ls = em.batch(ldsem)
        i = em.sy().dma_start(out=a_t, in_=asrc[ts(c), :])
        em.inc(i, ls, 16)
        i = em.sy().dma_start(out=b_t, in_=bsrc[ts(c), :])
        em.inc(i, ls, 16)
        ldv = em.c[ls]
        em.wait("vector", ls, ldv)
        i = em.ve().tensor_add(o_t, a_t, b_t)
        vv = em.inc(i, "vec")
        em.ve().drain(fusable=False)
        em.wait("gpsimd", "vec", vv)
        i = em.gp().dma_start(out=dst[ts(c), :], in_=o_t)
        sv = em.inc(i, "st0", 16)
        em.wait("vector", "st0", sv)
        em.wait("sync", "vec", vv)
    em.barrier("addout")


def emit_all(em, B, phases):
    load_constants(em, B)
    for ph in phases:
        ph(em, B)


def declare_scratch(B):
    for n in ("XN1", "RR", "KK", "VV", "GATE", "ICLR", "SIGT",
              "KN", "BVEC", "RK", "DOUT", "OIN", "XR1", "XN2", "XR2",
              "XN3", "XR3", "XN4", "QF", "KF", "OATT"):
        B.scr(n, (D, T))
    B.scr("VTM", (T, D))
    B.scr("W5", (16, 4096))
    for n in ("QF2", "KF2", "QR", "KR", "OA2", "XNF", "XRA", "XRB"):
        B.scr(n, (D, T))
    B.scr("V65", (T, 16 * 65))
    B.scr("DBGA", (128, 768))
    B.scr("DBGA2", (128, 128))
    B.scr("DBGA3", (128, 128))
    for h in range(H):
        B.scr(f"WSH{h}", (128 * 4226,))


def default_phases(B, upto=None):
    ph = [
        ("ln1", lambda em, B: phase_layernorm(
            em, B, B.inputs["xin"].ap(), B.scratch["XN1"].ap(),
            sca(B, "eps_ln"), "ld1", "1")),
        ("mlp_r", lambda em, B: phase_l0_mlp(
            em, B, B.scratch["XN1"].ap(), B.scratch["RR"].ap(),
            "L0_r", "L0_mu_r", "ld2", "r")),
        ("mlp_k", lambda em, B: phase_l0_mlp(
            em, B, B.scratch["XN1"].ap(), B.scratch["KK"].ap(),
            "L0_k", "L0_mu_k", "ld2", "k")),
        ("mlp_v", lambda em, B: phase_l0_mlp(
            em, B, B.scratch["XN1"].ap(), B.scratch["VV"].ap(),
            "L0_v", "L0_mu_v", "ld2", "v")),
        ("gates", lambda em, B: phase_l0_gates(
            em, B, B.scratch["XN1"].ap(), "ld3")),
        ("prep", lambda em, B: phase_l0_prep(em, B, "ld4")),
        ("deltanet", lambda em, B: phase_deltanet(em, B, "ld5")),
        ("combine", lambda em, B: phase_l0_combine(em, B, "ld6")),
        ("mlp_o", lambda em, B: phase_l0_mlp(
            em, B, B.scratch["OIN"].ap(), B.scratch["XR1"].ap(),
            "L0_o", None, "ld2", "o", E, B.inputs["xin"].ap())),
        ("ln2", lambda em, B: phase_layernorm(
            em, B, B.scratch["XR1"].ap(), B.scratch["XN2"].ap(),
            sca(B, "eps_ln"), "ld1", "2")),
        ("l0ffn", lambda em, B: phase_l0ffn(em, B, "ld7")),
        ("ln3", lambda em, B: phase_layernorm(
            em, B, B.scratch["XR2"].ap(), B.scratch["XN3"].ap(),
            sca(B, "eps_ln"), "ld1", "3")),
        ("l1pre", lambda em, B: phase_l1attn_pre(em, B, "ld8")),
        ("l1core", lambda em, B: phase_l1attn_core(em, B, "ld9")),
        ("l1wo", lambda em, B: phase_proj(
            em, B, B.scratch["OATT"].ap(), B.scratch["XR3"].ap(),
            "A1_Wo", "ld2", "wo", B.scratch["XR2"].ap())),
    ]
    ph += [("ln4", lambda em, B: phase_layernorm(
        em, B, B.scratch["XR3"].ap(), B.scratch["XN4"].ap(),
        sca(B, "eps_ln"), "ld1", "4"))]
    pool = ["XN4", "XRA", "XRB"]
    for li in range(6):
        cur = pool[(2 * li) % 3]
        mid = pool[(2 * li + 1) % 3]
        nxt = pool[(2 * li + 2) % 3]
        last = (li == 5)

        def mk(li, cur, mid, nxt, last):
            def f_qkv(em, B):
                phase_ffqkv(em, B, li, B.scratch[cur].ap(), "ld2")

            def f_rot(em, B):
                phase_rotnorm(em, B, li, "ld3")

            def f_latt(em, B):
                phase_local_attn(em, B, li, "ld4")

            def f_wo(em, B):
                phase_proj(em, B, B.scratch["OA2"].ap(), B.scratch[mid].ap(),
                           "FF_Wo", "ld2", f"ffwo{li}", B.scratch[cur].ap(), li=li)

            def f_ln(em, B):
                phase_layernorm(em, B, B.scratch[mid].ap(), B.scratch["XNF"].ap(),
                                sca(B, "eps_ln"), "ld1", f"f{li}", shift=True,
                                lng=(B.inputs["FF_lng"], li * D),
                                lnb=(B.inputs["FF_lnb"], li * D))

            def f_geglu(em, B):
                phase_geglu(em, B, li, B.scratch["XNF"].ap(),
                            B.scratch[mid].ap(), B.scratch[nxt].ap(), "ld6")
            return f_qkv, f_rot, f_latt, f_wo, f_ln, f_geglu

        fs = mk(li, cur, mid, nxt, last)
        ph += [(f"ffqkv{li}", fs[0]), (f"rot{li}", fs[1]), (f"latt{li}", fs[2]),
               (f"ffwo{li}", fs[3]), (f"ffln{li}", fs[4]), (f"geglu{li}", fs[5])]
    ph += [("addout", lambda em, B: phase_add(
        em, B, B.scratch["XN4"].ap(), B.scratch["XR3"].ap(),
        B.outputs["xout"].ap(), "ld1"))]
    if upto is not None:
        names = [p[0] for p in ph]
        ph = ph[:names.index(upto) + 1]
    return [p[1] for p in ph]


def build_nc(dbg_taps=(), upto=None, final=None):
    nc = bass.Bass("TRN2", target_bir_lowering=False)
    B = Builder(nc, dbg_taps)
    declare_io(B)
    alloc_sbuf(B)
    declare_scratch(B)
    phase_list = default_phases(B, upto)
    if upto is not None:
        fin = final or "XN1"
        phase_list = phase_list + [lambda em, B: phase_copyout(em, B, fin)]

    sems = {}
    semnames = (["ph", "mm", "act", "vec", "gpc"]
                + [f"ld{i}{s}" for i in range(10) for s in ("", "b", "c", "d")]
                + [f"st{i}{s}" for i in range(4) for s in ("", "b", "c", "d")])
    with _multi(nc, semnames, sems):
        with nc.Block() as blk:
            def run(role, eng):
                em = EM(nc, role, eng, sems)
                emit_all(em, B, phase_list)

            @blk.sync
            def _(eng):
                run("sync", eng)

            @blk.tensor
            def _(eng):
                run("tensor", eng)

            @blk.scalar
            def _(eng):
                run("scalar", eng)

            @blk.vector
            def _(eng):
                run("vector", eng)

            @blk.gpsimd
            def _(eng):
                run("gpsimd", eng)
    return nc, B


def phase_copyout(em, B, name="XN1"):
    src = B.scratch[name].ap()
    dst = B.outputs["xout"].ap()
    for c in range(CH):
        t = B.AR3[:, 0:T]
        ls = em.batch("ld2")
        i = em.sy().dma_start(out=t, in_=src[c * 128:(c + 1) * 128, :])
        lv = em.inc(i, ls, 16)
        em.wait("gpsimd", ls, lv)
        i = em.gp().dma_start(out=dst[c * 128:(c + 1) * 128, :], in_=t)
        sv = em.inc(i, "st1", 16)
        em.wait("sync", "st1", sv)
    em.wait("gpsimd", "st1", em.c["st1"])
    em.barrier("out")


from contextlib import ExitStack, contextmanager


@contextmanager
def _multi(nc, names, out):
    with ExitStack() as st:
        for n in names:
            out[n] = st.enter_context(nc.semaphore(n))
        yield


def make_in_map(core_inputs, weights):
    m = dict(weights)
    m.update(core_inputs)
    return m


def host_weights(I):
    w = {}
    for p in ("r", "k", "v", "o"):
        w[f"L0_{p}W1"] = I[f"L0_attn_{p}W1"]
        w[f"L0_{p}b1"] = I[f"L0_attn_{p}b1"]
        w[f"L0_{p}W2"] = I[f"L0_attn_{p}W2"]
        w[f"L0_{p}b2"] = I[f"L0_attn_{p}b2"]
    for p in ("mu_r", "mu_k", "mu_v", "mu_g", "mu_a", "mu_d", "db", "ab",
              "rem_mult", "iclr_mix", "bonus_mult", "ln_g", "ln_b"):
        w[f"L0_{p}"] = I[f"L0_attn_{p}"]
    for p in ("dA", "aA", "gA", "dB", "aB", "gB"):
        w[f"L0_{p}"] = I[f"L0_attn_{p}"]
    w["F0_Win"] = I["L0_ffn_Win"]
    w["F0_bin"] = I["L0_ffn_bin_"]
    for p in ("W1", "b1", "W2", "b2", "Wout", "bout"):
        w[f"F0_{p}"] = I[f"L0_ffn_{p}"]
    for p in ("Wq", "Wk", "Wv", "Wo"):
        w[f"A1_{p}"] = I[f"L1_attn_{p}"]
    w["A1_table"] = I["L1_attn_rel_bias_table"]
    w["FF_Wqkv"] = I["L1_ffn_Wqkv"]
    w["FF_qs"] = I["L1_ffn_q_scale"]
    w["FF_ks"] = I["L1_ffn_k_scale"]
    w["FF_Wo"] = I["L1_ffn_Wo"]
    w["FF_lng"] = I["L1_ffn_ln_g"]
    w["FF_lnb"] = I["L1_ffn_ln_b"]
    W1 = np.asarray(I["L1_ffn_W1"], np.float32)
    W2 = np.asarray(I["L1_ffn_W2"], np.float32)
    b1 = np.asarray(I["L1_ffn_b1"], np.float32)
    b2 = np.asarray(I["L1_ffn_b2"], np.float32)
    W1T = np.zeros((6, 22, 128, 3072), np.float32)
    b1T = np.zeros((6, 128, 44), np.float32)
    for j in range(22):
        wj = min(128, 2730 - j * 128)
        a = W1[:, :, j * 128: j * 128 + wj]
        g = W1[:, :, 2730 + j * 128: 2730 + j * 128 + wj]
        for c in range(8):
            W1T[:, j, :, c * 128: c * 128 + wj] = a[:, c * 128:(c + 1) * 128, :]
            W1T[:, j, :, 1024 + c * 128: 1024 + c * 128 + wj] = g[:, c * 128:(c + 1) * 128, :]
        W1T[:, j, 0:wj, 2048:3072] = W2[:, j * 128: j * 128 + wj, :].reshape(6, wj, 1024)[:, :, :].transpose(0, 1, 2).reshape(6, wj, 1024)[:, :, :] .reshape(6, wj, 1024)[:, :, 0:1024].reshape(6, wj, 1024) if False else W2[:, j * 128: j * 128 + wj, :]
        b1T[:, 0:wj, j] = b1[:, j * 128: j * 128 + wj]
        b1T[:, 0:wj, 22 + j] = b1[:, 2730 + j * 128: 2730 + j * 128 + wj]
    w["FF_W1T"] = W1T
    w["FF_b1T"] = b1T
    w["FF_b2T"] = np.ascontiguousarray(b2.reshape(6, 8, 128).transpose(0, 2, 1))
    C = host_constants()
    for k, v in C.items():
        if "c_" + k in ("c_zeros64",):
            continue
        w["c_" + k] = v
    w.pop("c_zeros64", None)
    return w


def kernel(**inputs):
    X = np.asarray(inputs["L0_attn_queries"], dtype=np.float32)
    Bn = X.shape[0]
    nc, B = build_nc()
    w = host_weights(inputs)
    w = {k: np.ascontiguousarray(np.asarray(v, dtype=np.float32)) for k, v in w.items()}
    in_maps = []
    for b in range(Bn):
        m = dict(w)
        m["xin"] = np.ascontiguousarray(X[b].T)
        in_maps.append(m)
    res = run_bass_kernel_spmd(nc, in_maps, core_ids=list(range(8)))
    out = np.stack([np.ascontiguousarray(r["xout"].T) for r in res.results])
    return out.astype(np.float32)


if __name__ == "__main__":
    pass
